# version 04
# speedup vs baseline: 1.0843x; 1.0629x over previous
import contextlib

import numpy as np

import concourse.bass as bass
import concourse.mybir as mybir
from concourse.bass_utils import run_bass_kernel_spmd
from concourse.alu_op_type import AluOpType as ALU

F32 = mybir.dt.float32
BF16 = mybir.dt.bfloat16
AF = mybir.ActivationFunctionType
AX = mybir.AxisListType.X

D = 1024
N = 2048
NCH = D // 128
NB = 512
NNB = N // NB
NEG = -30000.0



def _ct(a):
    return np.ascontiguousarray(a, dtype=np.float32)


def _cols(v, p=128):
    v = np.asarray(v)
    k = v.shape[-1]
    return _ct(v.reshape(k // p, p).T)


def prep_shared(inp):
    d = {}
    nn = np.arange(N, dtype=np.float64)

    d["c_identity"] = np.eye(128, dtype=np.float32)
    psw = np.zeros((128, 128), np.float32)
    for i in range(64):
        psw[2 * i, 2 * i + 1] = 1.0
        psw[2 * i + 1, 2 * i] = 1.0
    d["c_pairswap"] = psw
    p64 = np.zeros((128, 128), np.float32)
    for h in range(2):
        b = 64 * h
        for i in range(32):
            p64[b + i, b + 32 + i] = 1.0
            p64[b + 32 + i, b + i] = 1.0
    d["c_swap32"] = p64
    cm = np.zeros((4, 128, 512), np.float32)
    ki = np.arange(128)[:, None]
    qi = np.arange(512)[None, :]
    for r in range(4):
        cm[r] = np.where(r * 128 + ki > qi, NEG, 0.0)
    d["c_causalmask"] = cm.reshape(4 * 128, 512)
    hs2 = np.zeros((128, 65), np.float32)
    hs2[0:64, 0] = 1.0
    hs2[64:128, 64] = 1.0
    d["c_headsel2"] = hs2
    ohbc = np.zeros((16, 16 * 128), np.float32)
    for r in range(16):
        ohbc[r, r * 128:(r + 1) * 128] = 1.0
    d["c_ohbc"] = ohbc

    d["w_l0a_dq"] = inp["L0_attn_Wd_q_w"]
    d["b_l0a_dq"] = _cols(inp["L0_attn_Wd_q_b"])
    d["w_l0a_uq"] = inp["L0_attn_Wu_q_w"]
    d["b_l0a_uq"] = _cols(inp["L0_attn_Wu_q_b"])
    d["w_l0a_qr"] = inp["L0_attn_Wq_r_w"]
    d["b_l0a_qr"] = _cols(inp["L0_attn_Wq_r_b"])
    d["w_l0a_kr"] = inp["L0_attn_Wk_r_w"]
    d["b_l0a_kr"] = _cols(inp["L0_attn_Wk_r_b"])
    d["w_l0a_dkv"] = inp["L0_attn_Wd_kv_w"]
    d["b_l0a_dkv"] = _cols(inp["L0_attn_Wd_kv_b"])
    d["w_l0a_uk"] = inp["L0_attn_Wu_k_w"]
    d["b_l0a_uk"] = _cols(inp["L0_attn_Wu_k_b"])
    d["w_l0a_uv"] = inp["L0_attn_Wu_v_w"]
    d["b_l0a_uv"] = _cols(inp["L0_attn_Wu_v_b"])
    d["br_l0a_uv"] = _ct(np.asarray(inp["L0_attn_Wu_v_b"]).reshape(1, 1024))
    d["w_l0a_o"] = inp["L0_attn_Wo_w"]
    d["b_l0a_o"] = _cols(inp["L0_attn_Wo_b"])
    d["br_l0a_o"] = _ct(np.asarray(inp["L0_attn_Wo_b"]).reshape(1, 1024))
    ii = np.arange(512, dtype=np.float64)
    theta = (10000.0 ** (-2.0 * ii)) / 512.0
    ang = nn[None, :] * theta[:, None]
    c, s = np.cos(ang), np.sin(ang)
    C = np.zeros((1024, N), np.float32)
    S = np.zeros((1024, N), np.float32)
    C[0::2] = c
    C[1::2] = c
    S[0::2] = -s
    S[1::2] = s
    d["c_l0a_cos"] = C
    d["c_l0a_sin"] = S

    d["w_l0f_anorm"] = _cols(inp["L0_ffn_attn_norm_w"])
    d["w_l0f_mnorm"] = _cols(inp["L0_ffn_moe_norm_w"])
    d["w_l0f_q"] = _ct(np.asarray(inp["L0_ffn_wq"]).T)
    d["w_l0f_k"] = _ct(np.asarray(inp["L0_ffn_wk"]).T)
    d["w_l0f_v"] = _ct(np.asarray(inp["L0_ffn_wv"]).T)
    d["w_l0f_o"] = _ct(np.asarray(inp["L0_ffn_wo"]).T)
    d["w_l0f_gate"] = _ct(np.asarray(inp["L0_ffn_gate_w"]).T)
    d["w_l0f_w1"] = _ct(np.asarray(inp["L0_ffn_w1"]).transpose(0, 2, 1))
    d["w_l0f_w2"] = _ct(np.asarray(inp["L0_ffn_w2"]).transpose(0, 2, 1))
    d["b_l0f_b1"] = np.stack([_cols(b) for b in inp["L0_ffn_b1"]])
    d["b2row_l0f"] = np.stack([_cols(b) for b in inp["L0_ffn_b2"]])
    invf = 1.0 / (10000.0 ** (np.arange(0, 64, 2, dtype=np.float64) / 64.0))
    ang = nn[None, :] * invf[:, None]
    c32, s32 = np.cos(ang), np.sin(ang)
    c64 = np.concatenate([c32, c32], 0)
    s64 = np.concatenate([s32, s32], 0)
    sgn = np.concatenate([-np.ones((32, 1)), np.ones((32, 1))], 0)
    d["c_rh_cos"] = _ct(np.tile(c64, (2, 1)))
    d["c_rh_sin"] = _ct(np.tile(s64 * sgn, (2, 1)))

    qp = np.asarray(inp["L1_attn_query_proj"])
    kp = np.asarray(inp["L1_attn_key_proj"])
    vp = np.asarray(inp["L1_attn_value_proj"])
    op = np.asarray(inp["L1_attn_output_proj"])
    wa = np.zeros((1024, 512), np.float32)
    wb = np.zeros((1024, 512), np.float32)
    wa_core = qp.transpose(2, 1, 0)
    wb_core = kp.transpose(2, 0, 1)
    for h in range(16):
        wa[:, 32 * h:32 * h + 16] = wa_core[:, h, :]
        wb[:, 32 * h:32 * h + 16] = wb_core[:, h, :]
    d["w_l1a_a"] = wa
    d["w_l1a_b"] = wb
    d["w_l1a_v"] = _ct(vp.transpose(2, 0, 1).reshape(1024, 1024))
    d["w_l1a_o"] = _ct(op.transpose(1, 2, 0).reshape(1024, 1024))

    d["w_l1f_r"] = inp["L1_ffn_Wr"]
    d["w_l1f_g"] = inp["L1_ffn_Wg"]
    d["w_l1f_u"] = inp["L1_ffn_Wu"]
    d["w_l1f_d"] = inp["L1_ffn_Wd"]
    d["w_l1f_sg"] = inp["L1_ffn_Sg"]
    d["w_l1f_su"] = inp["L1_ffn_Su"]
    d["w_l1f_sd"] = inp["L1_ffn_Sd"]

    d["w_l2a_qkv"] = inp["L2_attn_W_qkv"]
    d["w_l2a_o"] = inp["L2_attn_W_out"]
    fr = 1.0 / (10000.0 ** (np.arange(0, 64, 2, dtype=np.float64) / 64.0))
    ang = nn[None, :] * fr[:, None]
    c2, s2 = np.cos(ang), np.sin(ang)
    C2 = np.zeros((64, N), np.float32)
    S2 = np.zeros((64, N), np.float32)
    C2[0::2] = c2
    C2[1::2] = c2
    S2[0::2] = -s2
    S2[1::2] = s2
    d["c_l2a_cos"] = _ct(np.tile(C2, (2, 1)))
    d["c_l2a_sin"] = _ct(np.tile(S2, (2, 1)))

    d["w_l2f_w1"] = inp["L2_ffn_W1"]
    d["w_l2f_w2"] = inp["L2_ffn_W2"]
    d["b_l2f_b1"] = np.stack([_cols(b) for b in inp["L2_ffn_b1"]])
    d["b2row_l2f"] = np.stack([_cols(b) for b in inp["L2_ffn_b2"]])
    d["w_l2f_r"] = inp["L2_ffn_Wr"]
    d["b_l2f_r"] = np.asarray(inp["L2_ffn_br"]).reshape(1, 8)

    d["w_l3a_in"] = _ct(np.asarray(inp["L3_attn_W_in"]).T)
    d["w_l3a_conv"] = _ct(np.asarray(inp["L3_attn_conv_w"])[:, 0, :].reshape(16, 128, 4))
    d["b_l3a_conv"] = _cols(inp["L3_attn_conv_b"])
    d["w_l3a_alog"] = _ct(np.asarray(inp["L3_attn_A_log"]).reshape(32768).reshape(256, 128).T)
    wx = np.asarray(inp["L3_attn_W_x"]).T
    wxp = np.zeros((2048, 80), np.float32)
    wxp[:, 0] = wx[:, 0]
    wxp[:, 32:48] = wx[:, 1:17]
    wxp[:, 64:80] = wx[:, 17:33]
    d["w_l3a_x"] = wxp
    d["w_l3a_o"] = _ct(np.asarray(inp["L3_attn_W_out"]).T)
    sel = np.zeros((16, 128, 128), np.float32)
    gath = np.zeros((16, 128, 128), np.float32)
    for o in range(16):
        for pp in range(128):
            sel[o, o * 8 + pp // 16, pp] = 1.0
            gath[o, pp, o * 8 + pp // 16] = 1.0
    d["c_l3a_sel"] = sel.reshape(16 * 128, 128)
    d["c_l3a_gath"] = gath.reshape(16 * 128, 128)
    rep = np.zeros((16, 128), np.float32)
    for pp in range(128):
        rep[pp % 16, pp] = 1.0
    d["c_l3a_rep16"] = rep

    d["w_l3f_qkv"] = inp["L3_ffn_wqkv"]
    d["w_l3f_o"] = inp["L3_ffn_wo"]
    w1 = np.asarray(inp["L3_ffn_w1"])
    b1 = np.asarray(inp["L3_ffn_b1"])
    w2 = np.asarray(inp["L3_ffn_w2"])
    HP = 2816
    w1p = np.zeros((2, 1024, 2 * HP), np.float32)
    w1p[:, :, 0:2730] = w1[:, :, 0:2730]
    w1p[:, :, HP:HP + 2730] = w1[:, :, 2730:5460]
    d["w_l3f_w1"] = w1p
    b1a = np.zeros((2, HP), np.float32)
    b1g = np.zeros((2, HP), np.float32)
    b1a[:, 0:2730] = b1[:, 0:2730]
    b1g[:, 0:2730] = b1[:, 2730:5460]
    d["b_l3f_b1a"] = np.stack([_cols(b) for b in b1a])
    d["b_l3f_b1g"] = np.stack([_cols(b) for b in b1g])
    w2p = np.zeros((2, HP, 1024), np.float32)
    w2p[:, 0:2730, :] = w2
    d["w_l3f_w2"] = w2p
    d["b2row_l3f"] = _ct(np.asarray(inp["L3_ffn_b2"]))
    d["w_l3f_qs"] = _ct(np.tile(np.asarray(inp["L3_ffn_q_scale"]), (1, 2)).reshape(2, 128, 1))
    d["w_l3f_ks"] = _ct(np.tile(np.asarray(inp["L3_ffn_k_scale"]), (1, 2)).reshape(2, 128, 1))
    d["w_l3f_lng"] = np.stack([_cols(g) for g in inp["L3_ffn_ln_g"]])
    d["w_l3f_lnb"] = np.stack([_cols(b) for b in inp["L3_ffn_ln_b"]])
    DH = 64
    half = np.arange(0, DH, 2, dtype=np.float64)
    invf = 1.0 / (10000.0 ** (half / DH))
    frq = nn[:, None] * invf[None, :]
    base = (half + 0.4 * DH) / (1.4 * DH)
    power = (nn - N // 2) / (512 // 2)
    sc = base[None, :] ** power[:, None]
    cosf, sinf = np.cos(frq), np.sin(frq)
    cos64 = np.concatenate([cosf, cosf], 1).T
    sin64 = np.concatenate([sinf, sinf], 1).T
    sc64 = np.concatenate([sc, sc], 1).T
    sgn64 = np.concatenate([-np.ones((32, 1)), np.ones((32, 1))], 0)
    d["c_l3f_cq"] = _ct(np.tile(cos64 * sc64, (2, 1)))
    d["c_l3f_sq"] = _ct(np.tile(sin64 * sc64 * sgn64, (2, 1)))
    d["c_l3f_ck"] = _ct(np.tile(cos64 / sc64, (2, 1)))
    d["c_l3f_sk"] = _ct(np.tile(sin64 / sc64 * sgn64, (2, 1)))
    lm = np.zeros((8, 128, 512), np.float32)
    for r in range(8):
        diff = 512 - 128 * r + qi - ki
        lm[r] = np.where((diff < 0) | (diff > 512), NEG, 0.0)
    d["c_l3f_mask"] = lm.reshape(8 * 128, 512)

    for k in list(d.keys()):
        d[k] = np.ascontiguousarray(d[k], dtype=np.float32)
    return d


def prep_core(inp, b):
    return {"x_t": _ct(np.asarray(inp["L0_attn_x"])[b].T)}




class Sem:
    __slots__ = ("h", "n")

    def __init__(self, h):
        self.h = h
        self.n = 0

    def inc(self, instr, k=1):
        instr.then_inc(self.h, k)
        self.n += k
        return self.n


class Prog:
    def __init__(self):
        self.nc = bass.Bass("TRN2", target_bir_lowering=False)
        self.stack = contextlib.ExitStack()
        self.inputs = {}
        self._sems = {}

    def dram_in(self, name, shape):
        if name not in self.inputs:
            t = self.nc.dram_tensor(name, list(shape), F32, kind="ExternalInput")
            self.inputs[name] = t.ap()
        return self.inputs[name]

    def dram_out(self, name, shape, dtype=F32):
        return self.nc.dram_tensor(name, list(shape), dtype, kind="ExternalOutput").ap()

    def sbuf(self, name, shape, dtype):
        return self.stack.enter_context(self.nc.sbuf_tensor(name, list(shape), dtype))

    def psum(self, name, shape, dtype=F32):
        return self.stack.enter_context(self.nc.psum_tensor(name, list(shape), dtype))

    def sem(self, name):
        name = name.rsplit("_", 1)[-1] if "_" in name else name
        if name not in self._sems:
            h = self.stack.enter_context(self.nc.semaphore("sm_" + name))
            self._sems[name] = Sem(h)
        return self._sems[name]

    def chain(self, eng, name, instr):
        s = self.sem(name)
        n = s.inc(instr, 1)
        eng.wait_ge(s.h, n)

    def link(self, eng, sem, instr, k=1):
        n = sem.inc(instr, k)
        eng.wait_ge(sem.h, n)
        return n

    def close(self):
        self.stack.close()


def make_env(p):
    env = {}
    env["X"] = [p.sbuf(f"X{c}", [128, N], F32) for c in range(NCH)]
    env["TN"] = [p.sbuf(f"TN{c}", [128, N], BF16) for c in range(NCH)]
    env["scr_f32"] = p.sbuf("scr_f32", [128, N], F32)
    env["scr_f32b"] = p.sbuf("scr_f32b", [128, N], F32)
    env["rows"] = p.sbuf("rows", [1, 2 * N + 64], F32)
    env["consts"] = p.sbuf("km_consts", [128, 8], F32)
    env["ones_row128"] = p.sbuf("ones_row128", [1, 128], F32)
    env["ones_bf"] = p.sbuf("ones_bf", [128, 1], BF16)
    env["ones_row512"] = p.sbuf("ones_row512", [1, NB], F32)
    env["ones_row_bf"] = p.sbuf("ones_row_bf", [1, 128], BF16)
    env["ones512_bf"] = p.sbuf("ones512_bf", [1, NB], BF16)
    env["wcol"] = p.sbuf("wcol", [128, 64], F32)
    env["PSA"] = [p.psum("psa0", [128, N]), p.psum("psa1", [128, N])]
    return env


def bank(env, i):
    return env["PSA"][i // 4][:, (i % 4) * NB:(i % 4 + 1) * NB]


CONST_COLS = {"one": 0, "eps5": 1, "eps6": 2, "neg1": 3, "epsl2": 4, "zero": 5}
CONST_VALS = {"one": 1.0, "eps5": 1e-5, "eps6": 1e-6, "neg1": -1.0,
              "epsl2": 1e-12, "zero": 0.0}


def ccol(env, name):
    return env["consts"][:, CONST_COLS[name]:CONST_COLS[name] + 1]


def emit_init(p, env):
    nc = p.nc
    x_t = p.dram_in("x_t", (1024, 2048))
    s_ld = p.sem("dma_sync")
    b0 = s_ld.n
    with nc.Block() as blk:
        @blk.sync
        def _(sy):
            for c in range(NCH):
                s_ld.inc(sy.dma_start(env["X"][c][:], x_t[c * 128:(c + 1) * 128, :]), 16)
            sy.wait_ge(s_ld.h, b0 + NCH * 16)

        @blk.vector
        def _(v):
            for nm, val in CONST_VALS.items():
                v.memset(env["consts"][:, CONST_COLS[nm]:CONST_COLS[nm] + 1], val)
            v.memset(env["ones_row128"][:], 1.0)
            v.memset(env["ones_bf"][:], 1.0)
            v.memset(env["ones_row512"][:], 1.0)
            v.memset(env["ones_row_bf"][:], 1.0)
            v.memset(env["ones512_bf"][:], 1.0)




def emit_spill_x(p, env, name="xsave"):
    nc = p.nc
    if name not in env:
        env[name] = nc.dram_tensor(name, [1024, N], F32, kind="Internal").ap()
    xs = env[name]
    s = p.sem("dma_sync")
    b0 = s.n
    with nc.Block() as blk:
        @blk.sync
        def _(sy):
            for c in range(NCH):
                s.inc(sy.dma_start(xs[c * 128:(c + 1) * 128, :], env["X"][c][:]), 16)
            sy.wait_ge(s.h, b0 + NCH * 16)


def emit_restore_add_x(p, env, name="xsave"):
    nc = p.nc
    xs = env[name]
    s = p.sem("dma_gp")
    b0 = s.n
    with nc.Block() as blk:
        @blk.gpsimd
        def _(g):
            for c in range(NCH):
                s.inc(g.dma_start(env["X"][c][:], xs[c * 128:(c + 1) * 128, :],
                                  accum_op=ALU.add), 16)
            g.wait_ge(s.h, b0 + NCH * 16)


def emit_store_x(p, env, out_ap):
    nc = p.nc
    s_st = p.sem("dma_sync")
    b0 = s_st.n
    with nc.Block() as blk:
        @blk.sync
        def _(sy):
            for c in range(NCH):
                s_st.inc(sy.dma_start(out_ap[c * 128:(c + 1) * 128, :], env["X"][c][:]), 16)
            sy.wait_ge(s_st.h, b0 + NCH * 16)




def emit_norm(p, env, kind="ln", w_name=None, src="X", out="TN", eps=1e-5):
    nc = p.nc
    X = env[src]
    TN = env[out]
    sq, sq2 = env["scr_f32"], env["scr_f32b"]
    rows = env["rows"]
    ln = kind == "ln"
    mu = rows[0:1, 0:N]
    mu2 = rows[0:1, N:2 * N]
    var = rows[0:1, N:2 * N]
    sdev = rows[0:1, N:2 * N]
    rstd = rows[0:1, N:2 * N]
    brow = rows[0:1, 0:N]
    eps_name = "eps5" if eps == 1e-5 else "eps6"
    ones_col = ccol(env, "one")
    orow = env["ones_row128"]
    wcol = env["wcol"]

    s_sq = p.sem("n_sq")
    s_mm = p.sem("n_mm")
    s_rv1 = p.sem("n_rv1")
    s_rsc = p.sem("n_rsc")
    s_rv2 = p.sem("n_rv2")
    s_bc = p.sem("n_bc")
    s_ap = p.sem("n_ap")
    s_wld = p.sem("dma_gp")
    b_sq, b_mm, b_rv1, b_rsc, b_rv2, b_bc, b_ap, b_wld = (
        s_sq.n, s_mm.n, s_rv1.n, s_rsc.n, s_rv2.n, s_bc.n, s_ap.n, s_wld.n)

    ps_sq = env["PSA"][0][0:1, :]
    ps_sx = env["PSA"][1][0:1, :]
    bcast_per_nb = 2 if ln else 1

    with nc.Block() as blk:
        if w_name is not None:
            wap = p.dram_in(w_name, (128, 8))

            @blk.gpsimd
            def _(g):
                s_wld.inc(g.dma_start(wcol[:, 0:8], wap[:, :]), 16)

        @blk.scalar
        def _(s):
            for c in range(NCH):
                buf = sq if c % 2 == 0 else sq2
                if c >= 2:
                    s.wait_ge(s_mm.h, b_mm + c - 1)
                s_sq.inc(s.activation(buf[:], X[c][:], AF.Square), 1)
            s.wait_ge(s_rv1.h, b_rv1 + 1)
            s_rsc.inc(s.activation(sdev, var, AF.Sqrt,
                                   bias=ccol(env, eps_name)[0:1, :]), 1)

        @blk.tensor
        def _(t):
            for c in range(NCH):
                buf = sq if c % 2 == 0 else sq2
                t.wait_ge(s_sq.h, b_sq + c + 1)
                mm = None
                for nb in range(NNB):
                    if ln:
                        t.matmul(ps_sx[:, nb * NB:(nb + 1) * NB], ones_col,
                                 X[c][:, nb * NB:(nb + 1) * NB],
                                 start=(c == 0), stop=(c == NCH - 1))
                    mm = t.matmul(ps_sq[:, nb * NB:(nb + 1) * NB], ones_col,
                                  buf[:, nb * NB:(nb + 1) * NB],
                                  start=(c == 0), stop=(c == NCH - 1))
                s_mm.inc(mm, 1)
            t.wait_ge(s_rv2.h, b_rv2 + 1)
            for nb in range(NNB):
                pa = bank(env, (nb % 2) * 4 + 0)
                pb = bank(env, (nb % 2) * 4 + 1)
                if nb >= 2:
                    t.wait_ge(s_ap.h, b_ap + nb - 1)
                s_bc.inc(t.matmul(pa, orow[:],
                                  rstd[:, nb * NB:(nb + 1) * NB],
                                  start=True, stop=True), 1)
                if ln:
                    s_bc.inc(t.matmul(pb, orow[:],
                                      brow[:, nb * NB:(nb + 1) * NB],
                                      start=True, stop=True), 1)

        @blk.vector
        def _(v):
            inv_d = 1.0 / D
            v.wait_ge(s_mm.h, b_mm + NCH)
            if ln:
                p.chain(v, "vch", v.tensor_scalar(mu, ps_sx, inv_d, 0.0,
                                                  ALU.mult, ALU.add))
                p.chain(v, "vch", v.tensor_tensor(mu2, mu, mu, ALU.mult))
                i3 = v.scalar_tensor_tensor(var, ps_sq, inv_d, mu2,
                                            ALU.mult, ALU.subtract)
                p.chain(v, "vch", i3)
                s_rv1.inc(v.nop(), 1)
            else:
                i3 = v.tensor_scalar(var, ps_sq, inv_d, 0.0, ALU.mult, ALU.add)
                p.chain(v, "vch", i3)
                s_rv1.inc(v.nop(), 1)
            v.wait_ge(s_rsc.h, b_rsc + 1)
            p.chain(v, "vch", v.reciprocal(rstd, sdev))
            if ln:
                p.chain(v, "vch", v.scalar_tensor_tensor(brow, mu, -1.0, rstd,
                                                         ALU.mult, ALU.mult))
            s_rv2.inc(v.nop(), 1)
            if w_name is not None:
                v.wait_ge(s_wld.h, b_wld + 16)
            for nb in range(NNB):
                pa = bank(env, (nb % 2) * 4 + 0)
                pb = bank(env, (nb % 2) * 4 + 1)
                v.wait_ge(s_bc.h, b_bc + (nb + 1) * bcast_per_nb)
                sl = slice(nb * NB, (nb + 1) * NB)
                tmps = [(sq if c < 4 else sq2)[:, (c % 4) * NB:(c % 4 + 1) * NB]
                        for c in range(NCH)]
                last = None
                for c in range(NCH):
                    last = v.tensor_tensor(tmps[c], X[c][:, sl], pa, ALU.mult)
                p.chain(v, "vch", last)
                last = None
                for c in range(NCH):
                    if ln:
                        last = v.tensor_tensor(TN[c][:, sl], tmps[c], pb, ALU.add)
                    else:
                        last = v.tensor_scalar(TN[c][:, sl], tmps[c],
                                               wcol[:, c:c + 1], 0.0,
                                               ALU.mult, ALU.add)
                p.chain(v, "vch", last)
                s_ap.inc(v.nop(), 1)




def emit_projA(p, env, tag, w_ap, Kd, Fd, in_tiles, wstage, scalar_cb=None,
               vector_cb=None, bias_ap=None, n_bias=0, w_col0=0,
               bank_base=0, nbanks=4):
    nc = p.nc
    kc = (Kd + 127) // 128
    ftn = (Fd + 127) // 128
    s_w = p.sem(tag + "_w")
    s_mm = p.sem(tag + "_mm")
    s_ev = p.sem(tag + "_ev")
    b_w, b_mm, b_ev = s_w.n, s_mm.n, s_ev.n
    need = 16 + (16 if bias_ap is not None else 0)

    with nc.Block() as blk:
        @blk.gpsimd
        def _(g):
            src = w_ap[0:Kd, w_col0:w_col0 + Fd].rearrange("(kc p) f -> p kc f", p=128)
            dst = wstage[:, 0:kc * Fd].rearrange("p (kc f) -> p kc f", f=Fd)
            s_w.inc(g.dma_start(dst, src), 16)

        if bias_ap is not None:
            @blk.sync
            def _(sy):
                s_w.inc(sy.dma_start(env["wcol"][:, 40:40 + n_bias],
                                     bias_ap[:, 0:n_bias]), 16)

        @blk.tensor
        def _(t):
            t.wait_ge(s_w.h, b_w + need)
            idx = 0
            for ft in range(ftn):
                fw = min(128, Fd - ft * 128)
                for nb in range(NNB):
                    ps = bank(env, bank_base + idx % nbanks)
                    if idx >= nbanks:
                        t.wait_ge(s_ev.h, b_ev + idx - nbanks + 1)
                    mm = None
                    for ic in range(kc):
                        kw = min(128, Kd - ic * 128)
                        mm = t.matmul(
                            ps[0:fw, :],
                            wstage[0:kw, ic * Fd + ft * 128: ic * Fd + ft * 128 + fw],
                            in_tiles[ic][0:kw, nb * NB:(nb + 1) * NB],
                            start=(ic == 0), stop=(ic == kc - 1))
                    s_mm.inc(mm, 1)
                    idx += 1

        if scalar_cb is not None:
            @blk.scalar
            def _(s):
                idx = 0
                for ft in range(ftn):
                    for nb in range(NNB):
                        s.wait_ge(s_mm.h, b_mm + idx + 1)
                        ps = bank(env, bank_base + idx % nbanks)
                        done = scalar_cb(s, ft, nb, ps)
                        if done is not None:
                            s_ev.inc(done, 1)
                        idx += 1

        if vector_cb is not None:
            @blk.vector
            def _(v):
                idx = 0
                for ft in range(ftn):
                    for nb in range(NNB):
                        v.wait_ge(s_mm.h, b_mm + idx + 1)
                        ps = bank(env, bank_base + idx % nbanks)
                        done = vector_cb(v, ft, nb, ps)
                        if done is not None:
                            s_ev.inc(done, 1)
                        idx += 1


def evac_to_tiles(env, out_tiles, func=AF.Copy, scale=1.0, has_bias=False):
    def cb(s, ft, nb, ps):
        kw = {}
        f = func
        if has_bias:
            kw["bias"] = env["wcol"][:, 40 + ft:41 + ft]
            if f == AF.Copy:
                f = AF.Identity
        return s.activation(out_tiles[ft][:, nb * NB:(nb + 1) * NB],
                            ps[:, :], f, scale=scale, **kw)
    return cb


def evac_residual(env, has_bias=False):
    def cb(v, ft, nb, ps):
        sl = slice(nb * NB, (nb + 1) * NB)
        X = env["X"][ft]
        if has_bias:
            return v.scalar_tensor_tensor(X[:, sl], ps[:, :],
                                          env["wcol"][:, 40 + ft:41 + ft],
                                          X[:, sl], ALU.add, ALU.add)
        return v.tensor_tensor(X[:, sl], X[:, sl], ps[:, :], ALU.add)
    return cb




def emit_mha(p, env, tag, w_q, w_k, w_v, qc0, kc0, vc0, cos_name, sin_name,
             perm_name, w_o, wo_bias, qk_scale=0.125, causal=True,
             in_name="TN"):
    nc = p.nc
    TN = env[in_name]
    with contextlib.ExitStack() as outer:
        osb = lambda nm, sh, dt: outer.enter_context(
            nc.sbuf_tensor(tag + nm, list(sh), dt))
        OT = [osb(f"_ot{c}", [128, N], BF16) for c in range(NCH)]
        with contextlib.ExitStack() as scope:
            sb = lambda nm, sh, dt: scope.enter_context(
                nc.sbuf_tensor(tag + nm, list(sh), dt))
            QT = sb("_qt", [128, N], BF16)
            KT = sb("_kt", [128, N], BF16)
            VT = [sb(f"_vt{i}", [128, 130], BF16) for i in range(16)]
            PB = sb("_pb", [128, 16 * NB], BF16)
            WS = PB[:, 0:3 * 1024]
            CS = sb("_cs", [128, 2 * N], BF16)
            PRM = sb("_prm", [128, 128], BF16)
            SW = PB[:, 4 * 1024:6 * 1024]
            SQA = env["scr_f32"][:].bitcast(BF16)[:, 0:N]
            SQK = env["scr_f32"][:].bitcast(BF16)[:, N:2 * N]
            HS2 = None
            MSK = env["scr_f32b"][:, 0:4 * NB]
            rws = env["rows"]
            MQ = MK = NB2 = None
            BCOL = sb("_bc", [128, 2], F32)
            RROW = rws[0:1, 0:NB]
            AVF = env["scr_f32"][:].bitcast(BF16)[0:64, 0:NB]

            s_gp = p.sem("dma_gp")
            s_sy = p.sem("dma_sync")
            b_gp, b_sy = s_gp.n, s_sy.n
            with nc.Block() as blk:
                @blk.gpsimd
                def _(g):
                    s_gp.inc(g.dma_start(CS[:, 0:N], p.dram_in(cos_name, (128, N))[:, :]), 16)
                    s_gp.inc(g.dma_start(CS[:, N:2 * N], p.dram_in(sin_name, (128, N))[:, :]), 16)
                    s_gp.inc(g.dma_start(PRM[:], p.dram_in(perm_name, (128, 128))[:, :]), 16)

                @blk.sync
                def _(sy):
                    if causal:
                        s_sy.inc(sy.dma_start(
                            MSK[:].rearrange("p (r q) -> p r q", r=4),
                            p.dram_in("c_causalmask", (512, 512))
                            .rearrange("(r p) q -> p r q", p=128)), 16)

                @blk.vector
                def _(v):
                    for i in range(16):
                        v.memset(VT[i][:, 64:65], 1.0)
                        v.memset(VT[i][:, 129:130], 1.0)
                    v.wait_ge(s_gp.h, b_gp + 48)
                    if causal:
                        v.wait_ge(s_sy.h, b_sy + 16)

            for hp in range(8):
                _mha_pair_proj(p, env, tag, hp, w_q, w_k, w_v, qc0, kc0, vc0,
                               QT, KT, VT, WS, TN)
                _mha_pair_rope(p, env, tag, QT, KT, CS, PRM, SW)
                _mha_pair_bounds(p, env, tag, QT, KT, SQA, SQK, HS2, MQ, MK,
                                 NB2, BCOL, qk_scale)
                _mha_pair_core(p, env, tag, hp, QT, KT, VT, PB, OT, MSK, RROW,
                               AVF, BCOL, causal, qk_scale)

        with contextlib.ExitStack() as scope2:
            WSO = scope2.enter_context(nc.sbuf_tensor(tag + "_wso", [128, 8192], BF16))
            emit_projA(p, env, tag + "_wo", w_o, 1024, 1024, OT, WSO,
                       vector_cb=evac_residual(env, has_bias=wo_bias is not None),
                       bias_ap=None if wo_bias is None else p.dram_in(wo_bias, (128, 8)),
                       n_bias=8)


def _mha_pair_proj(p, env, tag, hp, w_q, w_k, w_v, qc0, kc0, vc0,
                   QT, KT, VT, WS, TN):
    nc = p.nc
    s_w = p.sem(tag + "p_w")
    s_mm = p.sem(tag + "p_mm")
    s_ev = p.sem(tag + "p_ev")
    b_w, b_mm, b_ev = s_w.n, s_mm.n, s_ev.n
    with nc.Block() as blk:
        @blk.gpsimd
        def _(g):
            for i, (w, c0) in enumerate(((w_q, qc0), (w_k, kc0), (w_v, vc0))):
                src = w[:, c0 + hp * 128: c0 + (hp + 1) * 128].rearrange(
                    "(kc p) f -> p kc f", p=128)
                dst = WS[:, i * 1024:(i + 1) * 1024].rearrange(
                    "p (kc f) -> p kc f", f=128)
                s_w.inc(g.dma_start(dst, src), 16)

        @blk.tensor
        def _(t):
            t.wait_ge(s_w.h, b_w + 48)
            idx = 0
            for i in (0, 1):
                for nb in range(NNB):
                    ps = bank(env, idx % 4)
                    if idx >= 4:
                        t.wait_ge(s_ev.h, b_ev + idx - 3)
                    mm = None
                    for ic in range(8):
                        mm = t.matmul(ps, WS[:, i * 1024 + ic * 128:i * 1024 + (ic + 1) * 128],
                                      TN[ic][:, nb * NB:(nb + 1) * NB],
                                      start=(ic == 0), stop=(ic == 7))
                    s_mm.inc(mm, 1)
                    idx += 1
            for nt in range(16):
                ps = bank(env, 4 + nt % 2)
                if nt >= 2:
                    t.wait_ge(s_ev.h, b_ev + 8 + nt - 1)
                mm = None
                for ic in range(8):
                    mm = t.matmul(ps[:, 0:128],
                                  TN[ic][:, nt * 128:(nt + 1) * 128],
                                  WS[:, 2 * 1024 + ic * 128:2 * 1024 + (ic + 1) * 128],
                                  start=(ic == 0), stop=(ic == 7))
                s_mm.inc(mm, 1)

        @blk.scalar
        def _(s):
            idx = 0
            for i, dest in ((0, QT), (1, KT)):
                for nb in range(NNB):
                    s.wait_ge(s_mm.h, b_mm + idx + 1)
                    ps = bank(env, idx % 4)
                    s_ev.inc(s.activation(dest[:, nb * NB:(nb + 1) * NB], ps,
                                          AF.Copy), 1)
                    idx += 1
            for nt in range(16):
                s.wait_ge(s_mm.h, b_mm + 8 + nt + 1)
                ps = bank(env, 4 + nt % 2)
                src = ps[:, 0:128].rearrange("p (h c) -> p h c", c=64)
                dst = VT[nt][:, 0:130].rearrange("p (h c) -> p h c", c=65)[:, :, 0:64]
                s_ev.inc(s.activation(dst, src, AF.Copy), 1)


def _mha_pair_rope(p, env, tag, QT, KT, CS, PRM, SW):
    nc = p.nc
    s_rp = p.sem(tag + "r_rp")
    s_sw = p.sem(tag + "r_sw")
    s_rv = p.sem(tag + "r_rv")
    b_rp, b_sw, b_rv = s_rp.n, s_sw.n, s_rv.n
    with nc.Block() as blk:
        @blk.tensor
        def _(t):
            for idx in range(8):
                nb = idx % 4
                src = QT if idx < 4 else KT
                ps = bank(env, idx % 4)
                if idx >= 4:
                    t.wait_ge(s_sw.h, b_sw + idx - 3)
                s_rp.inc(t.matmul(ps, PRM[:], src[:, nb * NB:(nb + 1) * NB],
                                  start=True, stop=True), 1)

        @blk.scalar
        def _(s):
            for idx in range(8):
                s.wait_ge(s_rp.h, b_rp + idx + 1)
                if idx >= 4:
                    s.wait_ge(s_rv.h, b_rv + idx - 3)
                ps = bank(env, idx % 4)
                s_sw.inc(s.activation(SW[:, (idx % 4) * NB:(idx % 4 + 1) * NB],
                                      ps, AF.Copy), 1)

        @blk.vector
        def _(v):
            t1 = env["scr_f32"][:, 0:NB // 2].bitcast(BF16)
            t2 = env["scr_f32"][:, NB // 2:NB].bitcast(BF16)
            for idx in range(8):
                nb = idx % 4
                src = QT if idx < 4 else KT
                v.wait_ge(s_sw.h, b_sw + idx + 1)
                sl = slice(nb * NB, (nb + 1) * NB)
                swsl = SW[:, (idx % 4) * NB:(idx % 4 + 1) * NB]
                v.tensor_tensor(t1, src[:, sl], CS[:, sl], ALU.mult)
                i2 = v.tensor_tensor(t2, swsl, CS[:, N + nb * NB:N + (nb + 1) * NB],
                                     ALU.mult)
                p.chain(v, "vch", i2)
                i3 = v.tensor_tensor(src[:, sl], t1, t2, ALU.add)
                p.link(v, s_rv, i3)


def _mha_pair_bounds(p, env, tag, QT, KT, SQA, SQK, HS2, MQ, MK, NB2, BCOL,
                     qk_scale, bcol_off=0):
    nc = p.nc
    rws = env["rows"]
    c0 = 2 * N
    mq = rws[0:1, c0:c0 + 2]
    mk = rws[0:1, c0 + 2:c0 + 4]
    mprod = rws[0:1, c0 + 4:c0 + 6]
    msqrt = rws[0:1, c0 + 6:c0 + 8]
    negb = rws[0:1, c0 + 8:c0 + 10]
    s1 = p.sem(tag + "b_1")
    s2 = p.sem(tag + "b_2")
    s3 = p.sem(tag + "b_3")
    s4 = p.sem(tag + "b_4")
    s5 = p.sem(tag + "b_5")
    s6 = p.sem(tag + "b_6")
    s7 = p.sem(tag + "b_7")
    s8 = p.sem(tag + "b_8")
    b1, b2, b3, b4, b5, b6, b7, b8 = (s1.n, s2.n, s3.n, s4.n, s5.n, s6.n,
                                      s7.n, s8.n)
    ones = env["ones_bf"]
    with nc.Block() as blk:
        @blk.scalar
        def _(s):
            s1.inc(s.activation(SQA[:], QT[:], AF.Square), 1)
            s1.inc(s.activation(SQK[:], KT[:], AF.Square), 1)
            s.wait_ge(s5.h, b5 + 1)
            s6.inc(s.activation(msqrt, mprod, AF.Sqrt), 1)
            s.wait_ge(s8.h, b8 + 1)
            s.activation(BCOL[:, bcol_off:bcol_off + 1],
                         bank(env, 0)[:, 0:1], AF.Copy)
            s.wait_ge(s8.h, b8 + 2)
            s.activation(BCOL[:, bcol_off + 1:bcol_off + 2],
                         bank(env, 1)[:, 0:1], AF.Copy)

        @blk.tensor
        def _(t):
            t.wait_ge(s1.h, b1 + 1)
            mm = None
            for ai, base in ((0, 0), (1, 64)):
                for nb in range(NNB):
                    mm = t.matmul(env["PSA"][ai][0:1, nb * NB:(nb + 1) * NB],
                                  ones[base:base + 64, :],
                                  SQA[base:base + 64, nb * NB:(nb + 1) * NB],
                                  start=True, stop=True)
            s2.inc(mm, 1)
            t.wait_ge(s1.h, b1 + 2)
            t.wait_ge(s3.h, b3 + 1)
            for ai, base in ((0, 0), (1, 64)):
                for nb in range(NNB):
                    mm = t.matmul(env["PSA"][ai][0:1, nb * NB:(nb + 1) * NB],
                                  ones[base:base + 64, :],
                                  SQK[base:base + 64, nb * NB:(nb + 1) * NB],
                                  start=True, stop=True)
            s4.inc(mm, 1)
            t.wait_ge(s7.h, b7 + 1)
            s8.inc(t.matmul(bank(env, 0)[:, 0:1], env["ones_row128"][:],
                            negb[0:1, 0:1], start=True, stop=True), 1)
            s8.inc(t.matmul(bank(env, 1)[:, 0:1], env["ones_row128"][:],
                            negb[0:1, 1:2], start=True, stop=True), 1)

        @blk.vector
        def _(v):
            v.wait_ge(s2.h, b2 + 1)
            v.tensor_reduce(mq[0:1, 0:1], env["PSA"][0][0:1, :], axis=AX, op=ALU.max)
            i2 = v.tensor_reduce(mq[0:1, 1:2], env["PSA"][1][0:1, :], axis=AX, op=ALU.max)
            p.link(v, s3, i2)
            v.wait_ge(s4.h, b4 + 1)
            v.tensor_reduce(mk[0:1, 0:1], env["PSA"][0][0:1, :], axis=AX, op=ALU.max)
            i4 = v.tensor_reduce(mk[0:1, 1:2], env["PSA"][1][0:1, :], axis=AX, op=ALU.max)
            p.chain(v, "vch", i4)
            i5 = v.tensor_tensor(mprod, mq, mk, ALU.mult)
            p.link(v, s5, i5)
            v.wait_ge(s6.h, b6 + 1)
            s7.inc(v.tensor_scalar(negb, msqrt, -qk_scale, 0.0,
                                   ALU.mult, ALU.add), 1)


def _mha_pair_core(p, env, tag, hp, QT, KT, VT, PB, OT, MSK, RROW, AVF,
                   BCOL, causal, qk_scale, ot_tile=None):
    nc = p.nc
    s_smm = p.sem(tag + "c_smm")
    s_msk = p.sem(tag + "c_msk")
    s_exp = p.sem(tag + "c_exp")
    s_av = p.sem(tag + "c_av")
    s_r = p.sem(tag + "c_r")
    s_rb = p.sem(tag + "c_rb")
    s_avc = p.sem(tag + "c_avc")
    s_don = p.sem(tag + "c_don")
    b_smm, b_msk, b_exp, b_av, b_r, b_rb, b_avc, b_don = (
        s_smm.n, s_msk.n, s_exp.n, s_av.n, s_r.n, s_rb.n, s_avc.n, s_don.n)

    units = []
    gidx = 0
    for h in (0, 1):
        for qb in range(NNB):
            ktn = 4 * (qb + 1) if causal else 16
            groups = []
            for g0 in range(0, ktn, 2):
                kts = list(range(g0, min(g0 + 2, ktn)))
                diag = causal and (g0 // 4 == qb)
                groups.append((gidx, kts, diag))
                gidx += 1
            units.append((h, qb, groups, ktn))
    ndiag_before = {}
    nd = 0
    for h, qb, groups, ktn in units:
        for gi, kts, diag in groups:
            ndiag_before[gi] = nd
            if diag:
                nd += 1

    def s_reg(gi, kn):
        half = gi % 2
        return env["PSA"][0][:, half * 2 * NB:(half * 2 + kn) * NB]

    with nc.Block() as blk:
        @blk.tensor
        def _(t):
            ui = 0
            for h, qb, groups, ktn in units:
                hb = 64 * h
                for gi, kts, diag in groups:
                    if gi >= 2:
                        t.wait_ge(s_exp.h, b_exp + gi - 1)
                    mm = None
                    for ki, kt in enumerate(kts):
                        mm = t.matmul(
                            bank(env, (gi % 2) * 2 + ki),
                            KT[hb:hb + 64, kt * 128:(kt + 1) * 128],
                            QT[hb:hb + 64, qb * NB:(qb + 1) * NB],
                            start=True, stop=True)
                    s_smm.inc(mm, 1)
                t.wait_ge(s_exp.h, b_exp + groups[-1][0] + 1)
                if ui >= 2:
                    t.wait_ge(s_don.h, b_don + ui - 1)
                mm = None
                for j, kt in enumerate(range(ktn)):
                    mm = t.matmul(bank(env, 4 + ui % 2)[0:65, :],
                                  VT[kt][:, 65 * h:65 * h + 65],
                                  PB[:, kt * NB:(kt + 1) * NB],
                                  start=(j == 0), stop=(j == ktn - 1))
                s_av.inc(mm, 1)
                t.wait_ge(s_r.h, b_r + ui + 1)
                s_rb.inc(t.matmul(bank(env, 6)[0:64, :],
                                  env["ones_row128"][:, 0:64], RROW[:],
                                  start=True, stop=True), 1)
                ui += 1

        @blk.vector
        def _(v):
            ui = 0
            for h, qb, groups, ktn in units:
                for gi, kts, diag in groups:
                    if diag:
                        v.wait_ge(s_smm.h, b_smm + gi + 1)
                        kn = len(kts)
                        reg = s_reg(gi, kn)
                        r0 = kts[0] - 4 * qb
                        mreg = MSK[:, r0 * NB:(r0 + kn) * NB]
                        s_msk.inc(v.tensor_tensor(reg, reg, mreg, ALU.add), 1)
                v.wait_ge(s_av.h, b_av + ui + 1)
                i1 = v.reciprocal(RROW[:], bank(env, 4 + ui % 2)[64:65, :])
                p.link(v, s_r, i1)
                v.wait_ge(s_rb.h, b_rb + ui + 1)
                v.wait_ge(s_avc.h, b_avc + ui + 1)
                dest = ot_tile if ot_tile is not None else OT[hp]
                i2 = v.tensor_tensor(dest[64 * h:64 * h + 64,
                                          qb * NB:(qb + 1) * NB],
                                     AVF[:], bank(env, 6)[0:64, :], ALU.mult)
                p.link(v, s_don, i2)
                ui += 1

        @blk.scalar
        def _(s):
            ui = 0
            for h, qb, groups, ktn in units:
                for gi, kts, diag in groups:
                    if diag:
                        s.wait_ge(s_msk.h, b_msk + ndiag_before[gi] + 1)
                    else:
                        s.wait_ge(s_smm.h, b_smm + gi + 1)
                    kn = len(kts)
                    reg = s_reg(gi, kn)
                    s_exp.inc(s.activation(PB[:, kts[0] * NB:(kts[0] + kn) * NB],
                                           reg, AF.Exp, scale=qk_scale,
                                           bias=BCOL[:, h:h + 1]), 1)
                s.wait_ge(s_av.h, b_av + ui + 1)
                if ui >= 1:
                    s.wait_ge(s_don.h, b_don + ui)
                s_avc.inc(s.activation(AVF[:], bank(env, 4 + ui % 2)[0:64, :],
                                       AF.Copy), 1)
                ui += 1




def emit_block_l2_attn(p, env):
    emit_norm(p, env, kind="ln")
    emit_mha(p, env, "l2a",
             p.dram_in("w_l2a_qkv", (1024, 3072)),
             p.dram_in("w_l2a_qkv", (1024, 3072)),
             p.dram_in("w_l2a_qkv", (1024, 3072)),
             0, 1024, 2048,
             "c_l2a_cos", "c_l2a_sin", "c_pairswap",
             p.dram_in("w_l2a_o", (1024, 1024)), None,
             qk_scale=0.125, causal=True)


def emit_block_l0_ffn_attn_only(p, env):
    emit_norm(p, env, kind="rms", w_name="w_l0f_anorm", eps=1e-6)
    emit_mha(p, env, "l0fa",
             p.dram_in("w_l0f_q", (1024, 1024)),
             p.dram_in("w_l0f_k", (1024, 1024)),
             p.dram_in("w_l0f_v", (1024, 1024)),
             0, 0, 0,
             "c_rh_cos", "c_rh_sin", "c_swap32",
             p.dram_in("w_l0f_o", (1024, 1024)), None,
             qk_scale=0.125, causal=True)




def emit_moe(p, env, tag, E, w1_name, w2_name, b1_name, b2row_name,
             router, in_name="TN", H=4096):
    nc = p.nc
    TN = env[in_name]
    w1 = p.dram_in(w1_name, (E, 1024, H))
    w2 = p.dram_in(w2_name, (E, H, 1024))
    b1 = p.dram_in(b1_name, (E, 128, H // 128))
    b2r = p.dram_in(b2row_name, (E, 128, 8))
    ohbc = p.dram_in("c_ohbc", (16, 16 * 128))
    HH = H // 4
    HT = HH // 128
    with contextlib.ExitStack() as scope:
        sb = lambda nm, sh, dt: scope.enter_context(
            nc.sbuf_tensor(tag + nm, list(sh), dt))
        WH1 = sb("_wh1", [128, 8 * HH], BF16)
        WH2 = sb("_wh2", [128, HT * 1024], BF16)
        HID = sb("_hid", [128, HT * NB], BF16)
        CR = sb("_cr", [16, N], BF16)
        CBC = sb("_cbc", [128, NB], F32)
        B2C2 = sb("_b2c", [128, 16], F32)
        B2C = [B2C2[:, 0:8], B2C2[:, 8:16]]
        OHB2 = sb("_ohb", [16, 256], BF16)
        OHB = [OHB2[:, 0:128], OHB2[:, 128:256]]
        ROUT = sb("_rout", [128, 40], F32)

        with nc.Block() as blk0:
            @blk0.vector
            def _(v):
                v.memset(CR[:, :], 0.0)
        router(p, env, scope, CR, ROUT)

        steps = [(e, q) for e in range(E) for q in range(4)]
        st = {"w1": None, "w2": None, "b": None}
        s_w = p.sem(tag + "e_w")
        HH = H // 4
        with nc.Block() as blk0:
            @blk0.gpsimd
            def _(g):
                e0, q0 = steps[0]
                src1 = w1[e0, :, q0 * HH:(q0 + 1) * HH].rearrange(
                    "(kc p) f -> p kc f", p=128)
                s_w.inc(g.dma_start(
                    WH1[:].rearrange("p (kc f) -> p kc f", f=HH), src1), 16)
                src2 = w2[e0, q0 * HH:(q0 + 1) * HH, :].rearrange(
                    "(kc p) f -> p kc f", p=128)
                s_w.inc(g.dma_start(
                    WH2[:].rearrange("p (kc f) -> p kc f", f=1024), src2), 16)
                st["w1"] = s_w.n - 16
                st["w2"] = s_w.n
                s_w.inc(g.dma_start(OHB[0][:], ohbc[0:16, e0 * 128:(e0 + 1) * 128]), 16)

            @blk0.sync
            def _(sy):
                e0, q0 = steps[0]
                HT = HH // 128
                s_w.inc(sy.dma_start(env["wcol"][:, 0:HT],
                                     b1[e0, :, q0 * HT:(q0 + 1) * HT]), 16)
                s_w.inc(sy.dma_start(B2C[0], b2r[e0, :, :]), 16)
                st["b"] = s_w.n

        for i, (e, q) in enumerate(steps):
            nxt = steps[i + 1] if i + 1 < len(steps) else None
            _moe_expert_half(p, env, tag, e, q, i, nxt, E, w1, w2, b1, b2r,
                             ohbc, WH1, WH2, HID, CR, CBC, B2C, OHB, TN, H,
                             st, s_w)


def _moe_expert_half(p, env, tag, e, half, i, nxt, E, w1, w2, b1, b2r, ohbc,
                     WH1, WH2, HID, CR, CBC, B2C, OHB, TN, H, st, s_w):
    nc = p.nc
    HH = H // 4
    HT = HH // 128
    s_m1 = p.sem(tag + "e_m1")
    s_g = p.sem(tag + "e_g")
    s_cb = p.sem(tag + "e_cb")
    s_cc = p.sem(tag + "e_cc")
    s_m2 = p.sem(tag + "e_m2")
    s_tm = p.sem(tag + "e_tm")
    s_ad = p.sem(tag + "e_ad")
    b_m1, b_g, b_cb, b_cc, b_m2, b_tm, b_ad = (
        s_m1.n, s_g.n, s_cb.n, s_cc.n, s_m2.n, s_tm.n, s_ad.n)
    w1_tgt, w2_tgt, b_tgt = st["w1"], st["w2"], st["b"]
    par = i % 2
    bcol0 = 8 * par

    with nc.Block() as blk:
        @blk.gpsimd
        def _(g):
            def adds(nb):
                for dt in range(NCH):
                    g.wait_ge(s_tm.h, b_tm + nb * NCH + dt + 1)
                    tmp = (env["scr_f32"] if dt < 4 else env["scr_f32b"])[
                        :, (dt % 4) * NB:(dt % 4 + 1) * NB]
                    sl = slice(nb * NB, (nb + 1) * NB)
                    s_ad.inc(g.tensor_tensor(env["X"][dt][:, sl],
                                             env["X"][dt][:, sl], tmp,
                                             ALU.add), 1)
            adds(0)
            adds(1)
            adds(2)
            if nxt is not None:
                en, qn = nxt
                g.wait_ge(s_m1.h, b_m1 + NNB * HT)
                src1 = w1[en, :, qn * HH:(qn + 1) * HH].rearrange(
                    "(kc p) f -> p kc f", p=128)
                s_w.inc(g.dma_start(
                    WH1[:].rearrange("p (kc f) -> p kc f", f=HH), src1), 16)
                st["w1"] = s_w.n
                if qn == 0:
                    s_w.inc(g.dma_start(OHB[en % 2][:],
                                        ohbc[0:16, en * 128:(en + 1) * 128]), 16)
                st["b"] = s_w.n
            adds(3)
            if nxt is not None:
                en, qn = nxt
                g.wait_ge(s_m2.h, b_m2 + NNB * NCH)
                src2 = w2[en, qn * HH:(qn + 1) * HH, :].rearrange(
                    "(kc p) f -> p kc f", p=128)
                s_w.inc(g.dma_start(
                    WH2[:].rearrange("p (kc f) -> p kc f", f=1024), src2), 16)
                st["w2"] = s_w.n

        @blk.sync
        def _(sy):
            if nxt is not None:
                en, qn = nxt
                s_w.inc(sy.dma_start(env["wcol"][:, 8 * ((i + 1) % 2):
                                                 8 * ((i + 1) % 2) + HT],
                                     b1[en, :, qn * HT:(qn + 1) * HT]), 16)
                if qn == 0:
                    s_w.inc(sy.dma_start(B2C[en % 2], b2r[en, :, :]), 16)
                st["b"] = s_w.n

        @blk.tensor
        def _(t):
            t.wait_ge(s_w.h, max(w1_tgt, b_tgt))
            for nb in range(NNB):
                for ht in range(HT):
                    idx = nb * HT + ht
                    ps = bank(env, idx % 2)
                    if idx >= 2:
                        t.wait_ge(s_g.h, b_g + idx - 1)
                    mm = None
                    for ic in range(8):
                        mm = t.matmul(ps,
                                      WH1[:, ic * HH + ht * 128:ic * HH + (ht + 1) * 128],
                                      TN[ic][:, nb * NB:(nb + 1) * NB],
                                      start=(ic == 0), stop=(ic == 7))
                    s_m1.inc(mm, 1)
                if nb >= 1:
                    t.wait_ge(s_cc.h, b_cc + nb)
                s_cb.inc(t.matmul(bank(env, 4), OHB[e % 2][:],
                                  CR[:, nb * NB:(nb + 1) * NB],
                                  start=True, stop=True), 1)
                t.wait_ge(s_g.h, b_g + (nb + 1) * HT)
                if nb == 0:
                    t.wait_ge(s_w.h, w2_tgt)
                for dt in range(NCH):
                    idx2 = nb * NCH + dt
                    ps = bank(env, 2 + idx2 % 2)
                    if idx2 >= 2:
                        t.wait_ge(s_tm.h, b_tm + idx2 - 1)
                    for kt in range(HT):
                        mm = t.matmul(
                            ps, WH2[:, kt * 1024 + dt * 128:kt * 1024 + (dt + 1) * 128],
                            HID[:, kt * NB:(kt + 1) * NB],
                            start=(kt == 0), stop=(kt == HT - 1))
                    s_m2.inc(mm, 1)

        @blk.scalar
        def _(s):
            for nb in range(NNB):
                for ht in range(HT):
                    idx = nb * HT + ht
                    s.wait_ge(s_m1.h, b_m1 + idx + 1)
                    if nb >= 1 and ht == 0:
                        s.wait_ge(s_m2.h, b_m2 + nb * NCH)
                    ps = bank(env, idx % 2)
                    s_g.inc(s.activation(HID[:, ht * NB:(ht + 1) * NB], ps,
                                         AF.Gelu,
                                         bias=env["wcol"][:, bcol0 + ht:
                                                          bcol0 + ht + 1]), 1)
                s.wait_ge(s_cb.h, b_cb + nb + 1)
                if nb >= 1:
                    s.wait_ge(s_tm.h, b_tm + nb * NCH)
                s_cc.inc(s.activation(CBC[:], bank(env, 4), AF.Copy), 1)

        @blk.vector
        def _(v):
            for nb in range(NNB):
                v.wait_ge(s_cc.h, b_cc + nb + 1)
                for dt in range(NCH):
                    idx2 = nb * NCH + dt
                    v.wait_ge(s_m2.h, b_m2 + idx2 + 1)
                    if nb >= 1:
                        v.wait_ge(s_ad.h, b_ad + nb * NCH)
                    ps = bank(env, 2 + idx2 % 2)
                    tmp = (env["scr_f32"] if dt < 4 else env["scr_f32b"])[
                        :, (dt % 4) * NB:(dt % 4 + 1) * NB]
                    if half == 0:
                        s_tm.inc(v.scalar_tensor_tensor(
                            tmp, ps, B2C[e % 2][:, dt:dt + 1], CBC[:],
                            ALU.add, ALU.mult), 1)
                    else:
                        s_tm.inc(v.tensor_tensor(tmp, ps, CBC[:], ALU.mult), 1)


def router_l2f(p, env, scope, CR, ROUT):
    nc = p.nc
    TN = env["TN"]
    wr = p.dram_in("w_l2f_r", (1024, 8))
    br = p.dram_in("b_l2f_r", (1, 8))
    idf = p.dram_in("c_identity", (128, 128))
    sb = lambda nm, sh, dt: scope.enter_context(
        nc.sbuf_tensor("r2" + nm, list(sh), dt))
    WR = sb("_wr", [128, 64], BF16)
    BRR = sb("_br", [1, 8], BF16)
    IDF = sb("_id", [128, 128], F32)
    OBF = sb("_obf", [1, 128], BF16)
    E = 8
    m1 = ROUT[:, 8:9]
    nm1 = ROUT[:, 9:10]
    den = ROUT[:, 10:11]
    gcol = ROUT[:, 11:12]
    is1 = ROUT[:, 16:24]
    ctok = ROUT[:, 24:32]
    ex = ROUT[:, 32:40]
    s_w = p.sem("r2_w")
    s_mm = p.sem("r2_mm")
    s_v1 = p.sem("r2_v1")
    s_sc = p.sem("r2_sc")
    s_v2 = p.sem("r2_v2")
    s_tr = p.sem("r2_tr")
    s_cp = p.sem("r2_cp")
    b_w, b_mm, b_v1, b_sc, b_v2, b_tr, b_cp = (
        s_w.n, s_mm.n, s_v1.n, s_sc.n, s_v2.n, s_tr.n, s_cp.n)

    with nc.Block() as blk:
        @blk.gpsimd
        def _(g):
            s_w.inc(g.dma_start(
                WR[:].rearrange("p (kc f) -> p kc f", f=8),
                wr.rearrange("(kc p) f -> p kc f", p=128)), 16)
            s_w.inc(g.dma_start(BRR[:], br[:, :]), 16)
            s_w.inc(g.dma_start(OBF[:], env["ones_row128"][:]), 16)

        @blk.sync
        def _(sy):
            s_w.inc(sy.dma_start(IDF[:], idf[:, :]), 16)

        @blk.tensor
        def _(t):
            t.wait_ge(s_w.h, b_w + 64)
            for nt in range(16):
                ps = bank(env, nt % 2)
                if nt >= 2:
                    t.wait_ge(s_v1.h, b_v1 + nt - 1)
                for ic in range(8):
                    t.matmul(ps[:, 0:8], TN[ic][:, nt * 128:(nt + 1) * 128],
                             WR[:, ic * 8:(ic + 1) * 8],
                             start=(ic == 0), stop=False)
                mm = t.matmul(ps[:, 0:8], OBF[:], BRR[:],
                              start=False, stop=True)
                s_mm.inc(mm, 1)
                t.wait_ge(s_v2.h, b_v2 + nt + 1)
                ps2 = bank(env, 4 + nt % 2)
                s_tr.inc(t.transpose(ps2[0:8, 0:128], ctok[:, 0:8], IDF[:]), 1)

        @blk.vector
        def _(v):
            for nt in range(16):
                v.wait_ge(s_mm.h, b_mm + nt + 1)
                ps = bank(env, nt % 2)
                i1 = v.tensor_reduce(m1, ps[:, 0:8], axis=AX, op=ALU.max)
                p.chain(v, "vch", i1)
                i2 = v.tensor_scalar(is1[:, 0:8], ps[:, 0:8], m1, 1.0,
                                     ALU.is_ge, ALU.mult)
                i3 = v.tensor_scalar(nm1, m1, -1.0, 0.0, ALU.mult, ALU.add)
                p.link(v, s_v1, i3)
                v.wait_ge(s_sc.h, b_sc + nt + 1)
                i4 = v.reciprocal(gcol, den)
                p.chain(v, "vch", i4)
                v.wait_ge(s_tr.h, b_tr + nt)
                i5 = v.tensor_scalar(ctok[:, 0:8], is1[:, 0:8], gcol, 1.0,
                                     ALU.mult, ALU.mult)
                p.link(v, s_v2, i5)

        @blk.scalar
        def _(s):
            for nt in range(16):
                s.wait_ge(s_v1.h, b_v1 + nt + 1)
                ps = bank(env, nt % 2)
                s_sc.inc(s.activation(ex[:, 0:8], ps[:, 0:8], AF.Exp,
                                      bias=nm1, accum_out=den), 1)
                s.wait_ge(s_tr.h, b_tr + nt + 1)
                ps2 = bank(env, 4 + nt % 2)
                s_cp.inc(s.activation(CR[0:8, nt * 128:(nt + 1) * 128],
                                      ps2[0:8, 0:128], AF.Copy), 1)



def router_l0f(p, env, scope, CR, ROUT):
    nc = p.nc
    TN = env["TN"]
    wr = p.dram_in("w_l0f_gate", (1024, 8))
    idf = p.dram_in("c_identity", (128, 128))
    sb = lambda nm, sh, dt: scope.enter_context(
        nc.sbuf_tensor("r0" + nm, list(sh), dt))
    WR = sb("_wr", [128, 64], BF16)
    IDF = sb("_id", [128, 128], F32)
    m1 = ROUT[:, 8:9]
    m2 = ROUT[:, 9:10]
    dcol = ROUT[:, 10:11]
    rw1 = ROUT[:, 11:12]
    rw2 = ROUT[:, 12:13]
    is1 = ROUT[:, 16:24]
    is2 = ROUT[:, 24:32]
    msk = ROUT[:, 32:40]
    ctok = ROUT[:, 0:8]
    s_w = p.sem("r0_w")
    s_mm = p.sem("r0_mm")
    s_v1 = p.sem("r0_v1")
    s_sc = p.sem("r0_sc")
    s_v2 = p.sem("r0_v2")
    s_tr = p.sem("r0_tr")
    s_cp = p.sem("r0_cp")
    b_w, b_mm, b_v1, b_sc, b_v2, b_tr, b_cp = (
        s_w.n, s_mm.n, s_v1.n, s_sc.n, s_v2.n, s_tr.n, s_cp.n)

    with nc.Block() as blk:
        @blk.gpsimd
        def _(g):
            s_w.inc(g.dma_start(
                WR[:].rearrange("p (kc f) -> p kc f", f=8),
                wr.rearrange("(kc p) f -> p kc f", p=128)), 16)

        @blk.sync
        def _(sy):
            s_w.inc(sy.dma_start(IDF[:], idf[:, :]), 16)

        @blk.tensor
        def _(t):
            t.wait_ge(s_w.h, b_w + 32)
            for nt in range(16):
                ps = bank(env, nt % 2)
                if nt >= 2:
                    t.wait_ge(s_v1.h, b_v1 + nt - 1)
                mm = None
                for ic in range(8):
                    mm = t.matmul(ps[:, 0:8], TN[ic][:, nt * 128:(nt + 1) * 128],
                                  WR[:, ic * 8:(ic + 1) * 8],
                                  start=(ic == 0), stop=(ic == 7))
                s_mm.inc(mm, 1)
                t.wait_ge(s_v2.h, b_v2 + nt + 1)
                ps2 = bank(env, 4 + nt % 2)
                s_tr.inc(t.transpose(ps2[0:8, 0:128], ctok[:, 0:8], IDF[:]), 1)

        @blk.vector
        def _(v):
            for nt in range(16):
                v.wait_ge(s_mm.h, b_mm + nt + 1)
                ps = bank(env, nt % 2)
                i1 = v.tensor_reduce(m1, ps[:, 0:8], axis=AX, op=ALU.max)
                p.chain(v, "vch", i1)
                i2 = v.tensor_scalar(is1[:, 0:8], ps[:, 0:8], m1, 1.0,
                                     ALU.is_ge, ALU.mult)
                p.chain(v, "vch", i2)
                i3 = v.scalar_tensor_tensor(msk[:, 0:8], is1[:, 0:8], NEG,
                                            ps[:, 0:8], ALU.mult, ALU.add)
                p.chain(v, "vch", i3)
                i4 = v.tensor_reduce(m2, msk[:, 0:8], axis=AX, op=ALU.max)
                p.chain(v, "vch", i4)
                i5 = v.tensor_scalar(is2[:, 0:8], msk[:, 0:8], m2, 1.0,
                                     ALU.is_ge, ALU.mult)
                i6 = v.tensor_tensor(dcol, m1, m2, ALU.subtract)
                p.link(v, s_v1, i6)
                v.wait_ge(s_sc.h, b_sc + nt + 1)
                i7 = v.tensor_scalar(rw2, rw1, -1.0, 1.0, ALU.mult, ALU.add)
                p.chain(v, "vch", i7)
                v.wait_ge(s_tr.h, b_tr + nt)
                i8 = v.tensor_scalar(ctok[:, 0:8], is1[:, 0:8], rw1, 1.0,
                                     ALU.mult, ALU.mult)
                p.chain(v, "vch", i8)
                i9 = v.scalar_tensor_tensor(ctok[:, 0:8], is2[:, 0:8], rw2,
                                            ctok[:, 0:8], ALU.mult, ALU.add)
                p.link(v, s_v2, i9)

        @blk.scalar
        def _(s):
            for nt in range(16):
                s.wait_ge(s_v1.h, b_v1 + nt + 1)
                s_sc.inc(s.activation(rw1, dcol, AF.Sigmoid), 1)
                s.wait_ge(s_tr.h, b_tr + nt + 1)
                ps2 = bank(env, 4 + nt % 2)
                s_cp.inc(s.activation(CR[0:8, nt * 128:(nt + 1) * 128],
                                      ps2[0:8, 0:128], AF.Copy), 1)



def emit_block_l2_ffn(p, env):
    emit_norm(p, env, kind="ln")
    emit_moe(p, env, "l2f", 8, "w_l2f_w1", "w_l2f_w2", "b_l2f_b1",
             "b2row_l2f", router_l2f)


def emit_block_l0_ffn(p, env):
    emit_spill_x(p, env)
    emit_norm(p, env, kind="ln", out="X")
    emit_norm(p, env, kind="rms", w_name="w_l0f_anorm", eps=1e-6)
    emit_mha(p, env, "l0fa",
             p.dram_in("w_l0f_q", (1024, 1024)),
             p.dram_in("w_l0f_k", (1024, 1024)),
             p.dram_in("w_l0f_v", (1024, 1024)),
             0, 0, 0,
             "c_rh_cos", "c_rh_sin", "c_swap32",
             p.dram_in("w_l0f_o", (1024, 1024)), None,
             qk_scale=0.125, causal=True)
    emit_norm(p, env, kind="rms", w_name="w_l0f_mnorm", eps=1e-6)
    emit_moe(p, env, "l0f", 8, "w_l0f_w1", "w_l0f_w2", "b_l0f_b1",
             "b2row_l0f", router_l0f)
    emit_restore_add_x(p, env)




def emit_block_l0_attn(p, env):
    nc = p.nc
    emit_norm(p, env, kind="ln")
    TN = env["TN"]
    qk_scale = 1.0 / float(np.sqrt(1040.0))
    w_dq = p.dram_in("w_l0a_dq", (1024, 512))
    w_dkv = p.dram_in("w_l0a_dkv", (1024, 512))
    w_uq = p.dram_in("w_l0a_uq", (512, 1024))
    w_qr = p.dram_in("w_l0a_qr", (512, 1024))
    w_uk = p.dram_in("w_l0a_uk", (512, 1024))
    w_kr = p.dram_in("w_l0a_kr", (1024, 1024))
    w_uv = p.dram_in("w_l0a_uv", (512, 1024))
    w_o = p.dram_in("w_l0a_o", (1024, 1024))
    cosap = p.dram_in("c_l0a_cos", (1024, N))
    sinap = p.dram_in("c_l0a_sin", (1024, N))

    with contextlib.ExitStack() as outer:
        with contextlib.ExitStack() as scope:
            sb = lambda nm, sh, dt: scope.enter_context(
                nc.sbuf_tensor("l0a" + nm, list(sh), dt))
            OTP = sb("_otp", [128, N], BF16)
            WOS = sb("_wos", [128, 1024], BF16)
            BOR = env["rows"][:].bitcast(BF16)[0:1, 2048:3072]
            CTQ = [sb(f"_ctq{i}", [128, N], BF16) for i in range(4)]
            CTK = [sb(f"_ctk{i}", [128, N], BF16) for i in range(4)]
            QT = sb("_qt", [128, N], BF16)
            KT = sb("_kt", [128, N], BF16)
            VT = [sb(f"_vt{i}", [128, 130], BF16) for i in range(16)]
            PB = sb("_pb", [128, 16 * NB], BF16)
            WS = PB[:, 0:24 * 128]
            CS = PB[:, 3072:3072 + 2 * N]
            SWs = PB[:, 7168:8192]
            PRM = sb("_prm", [128, 128], BF16)
            WSD = PB[:, 0:4096]
            emit_projA(p, env, "l0a_dq", w_dq, 1024, 512, TN, WSD,
                       scalar_cb=evac_to_tiles(env, CTQ, has_bias=True),
                       bias_ap=p.dram_in("b_l0a_dq", (128, 4)), n_bias=4)
            emit_projA(p, env, "l0a_dkv", w_dkv, 1024, 512, TN, WSD,
                       scalar_cb=evac_to_tiles(env, CTK, has_bias=True),
                       bias_ap=p.dram_in("b_l0a_dkv", (128, 4)), n_bias=4)
            SQA = env["scr_f32"][:].bitcast(BF16)[:, 0:N]
            SQK = env["scr_f32"][:].bitcast(BF16)[:, N:2 * N]
            rws = env["rows"]
            BCOL = sb("_bc", [128, 2], F32)
            RROW = rws[0:1, 0:NB]
            AVF = env["scr_f32"][:].bitcast(BF16)[0:64, 0:NB]
            BVR = env["rows"][:].bitcast(BF16)[0:1, 4096:5120]

            s_c = p.sem("l0a_c")
            with nc.Block() as blk:
                @blk.gpsimd
                def _(g):
                    s_c.inc(g.dma_start(PRM[:], p.dram_in("c_pairswap", (128, 128))[:, :]), 16)
                    s_c.inc(g.dma_start(BVR[:], p.dram_in("br_l0a_uv", (1, 1024))[:, :]), 16)
                    s_c.inc(g.dma_start(BOR[:], p.dram_in("br_l0a_o", (1, 1024))[:, :]), 16)

                @blk.sync
                def _(sy):
                    for j, bn in enumerate(("b_l0a_uq", "b_l0a_qr",
                                            "b_l0a_uk", "b_l0a_kr")):
                        s_c.inc(sy.dma_start(env["wcol"][:, 8 + 8 * j:16 + 8 * j],
                                             p.dram_in(bn, (128, 8))[:, :]), 16)

                @blk.vector
                def _(v):
                    for i in range(16):
                        v.memset(VT[i][:, 64:65], 1.0)
                        v.memset(VT[i][:, 129:130], 1.0)
                    v.wait_ge(s_c.h, s_c.n)

            for hp in range(8):
                _l0a_pair_proj(p, env, hp, w_uq, w_qr, w_uk, w_kr, w_uv,
                               cosap, sinap, CTQ, CTK, TN, QT, KT, VT, WS,
                               CS, PRM, SWs, BVR)
                _mha_pair_bounds(p, env, "l0a", QT, KT, SQA, SQK, None,
                                 None, None, None, BCOL, qk_scale)
                _mha_pair_core(p, env, "l0a", hp, QT, KT, VT, PB, None,
                               None, RROW, AVF, BCOL, False, qk_scale,
                               ot_tile=OTP)
                _pair_wo(p, env, "l0a", hp, w_o, OTP, WOS,
                         bias_row=BOR if hp == 0 else None)


def _pair_wo(p, env, tag, hp, w_o, OTP, WOS, bias_row=None):
    nc = p.nc
    s_w = p.sem(tag + "w_w")
    s_mm = p.sem(tag + "w_mm")
    s_ev = p.sem(tag + "w_ev")
    b_w, b_mm, b_ev = s_w.n, s_mm.n, s_ev.n
    with nc.Block() as blk:
        @blk.gpsimd
        def _(g):
            src = w_o[hp * 128:(hp + 1) * 128, :].rearrange("p f -> p f")
            s_w.inc(g.dma_start(WOS[:], src), 16)

        @blk.tensor
        def _(t):
            t.wait_ge(s_w.h, b_w + 16)
            idx = 0
            for ft in range(NCH):
                for nb in range(NNB):
                    ps = bank(env, idx % 4)
                    if idx >= 4:
                        t.wait_ge(s_ev.h, b_ev + idx - 3)
                    mm = t.matmul(ps, WOS[:, ft * 128:(ft + 1) * 128],
                                  OTP[:, nb * NB:(nb + 1) * NB],
                                  start=True, stop=(bias_row is None))
                    if bias_row is not None:
                        mm = t.matmul(ps, bias_row[0:1, ft * 128:(ft + 1) * 128],
                                      env["ones512_bf"][:],
                                      start=False, stop=True)
                    s_mm.inc(mm, 1)
                    idx += 1

        @blk.vector
        def _(v):
            idx = 0
            for ft in range(NCH):
                for nb in range(NNB):
                    v.wait_ge(s_mm.h, b_mm + idx + 1)
                    ps = bank(env, idx % 4)
                    sl = slice(nb * NB, (nb + 1) * NB)
                    s_ev.inc(v.tensor_tensor(env["X"][ft][:, sl],
                                             env["X"][ft][:, sl], ps, ALU.add), 1)
                    idx += 1


def _l0a_pair_proj(p, env, hp, w_uq, w_qr, w_uk, w_kr, w_uv, cosap, sinap,
                   CTQ, CTK, TN, QT, KT, VT, WS, CS, PRM, SWs, BVR):
    nc = p.nc
    s_w = p.sem("l0ap_w")
    s_mm = p.sem("l0ap_mm")
    s_ev = p.sem("l0ap_ev")
    s_pr = p.sem("l0ap_pr")
    s_cb = p.sem("l0ap_cmb")
    b_w, b_mm, b_ev, b_pr, b_cb = s_w.n, s_mm.n, s_ev.n, s_pr.n, s_cb.n
    with nc.Block() as blk:
        @blk.gpsimd
        def _(g):
            for j, (w, kc) in enumerate(((w_uq, 4), (w_qr, 4), (w_uk, 4),
                                         (w_kr, 8), (w_uv, 4))):
                off = (0, 4, 8, 12, 20)[j]
                src = w[:, hp * 128:(hp + 1) * 128].rearrange(
                    "(kc p) f -> p kc f", p=128)
                dst = WS[:, off * 128:(off + kc) * 128].rearrange(
                    "p (kc f) -> p kc f", f=128)
                s_w.inc(g.dma_start(dst, src), 16)
            s_w.inc(g.dma_start(CS[:, 0:N], cosap[hp * 128:(hp + 1) * 128, :]), 16)
            s_w.inc(g.dma_start(CS[:, N:2 * N], sinap[hp * 128:(hp + 1) * 128, :]), 16)

        @blk.tensor
        def _(t):
            t.wait_ge(s_w.h, b_w + 7 * 16)
            for i in (0, 1):
                for nb in range(NNB):
                    u = i * NNB + nb
                    if u >= 1:
                        t.wait_ge(s_cb.h, b_cb + u)
                    sl = slice(nb * NB, (nb + 1) * NB)
                    if i == 0:
                        cw, ckc, cin = 0, 4, CTQ
                        rw_, rkc, rin = 4, 4, CTQ
                    else:
                        cw, ckc, cin = 8, 4, CTK
                        rw_, rkc, rin = 12, 8, TN
                    for ic in range(ckc):
                        t.matmul(bank(env, 0),
                                 WS[:, (cw + ic) * 128:(cw + ic + 1) * 128],
                                 cin[ic][:, sl], start=(ic == 0), stop=(ic == ckc - 1))
                    mm = None
                    for ic in range(rkc):
                        mm = t.matmul(bank(env, 1),
                                      WS[:, (rw_ + ic) * 128:(rw_ + ic + 1) * 128],
                                      rin[ic][:, sl], start=(ic == 0), stop=(ic == rkc - 1))
                    s_mm.inc(mm, 1)
                    t.wait_ge(s_ev.h, b_ev + 2 * u + 1)
                    s_pr.inc(t.matmul(bank(env, 2), PRM[:],
                                      SWs[:, 0:NB], start=True, stop=True), 1)
            for nt in range(16):
                ps = bank(env, 4 + nt % 2)
                if nt >= 2:
                    t.wait_ge(s_ev.h, b_ev + 16 + nt - 1)
                for ic in range(4):
                    t.matmul(ps[:, 0:128],
                             CTK[ic][:, nt * 128:(nt + 1) * 128],
                             WS[:, (20 + ic) * 128:(20 + ic + 1) * 128],
                             start=(ic == 0), stop=False)
                mm = t.matmul(ps[:, 0:128], env["ones_row_bf"][:],
                              BVR[0:1, hp * 128:(hp + 1) * 128],
                              start=False, stop=True)
                s_mm.inc(mm, 1)

        @blk.scalar
        def _(s):
            for u in range(8):
                s.wait_ge(s_mm.h, b_mm + u + 1)
                if u >= 1:
                    s.wait_ge(s_cb.h, b_cb + u)
                bcol = 16 + 16 * (u // 4) + hp
                s_ev.inc(s.activation(SWs[:, 0:NB], bank(env, 1), AF.Identity,
                                      bias=env["wcol"][:, bcol:bcol + 1]), 1)
                s.wait_ge(s_pr.h, b_pr + u + 1)
                s_ev.inc(s.activation(SWs[:, NB:2 * NB], bank(env, 2), AF.Copy), 1)
            for nt in range(16):
                s.wait_ge(s_mm.h, b_mm + 8 + nt + 1)
                ps = bank(env, 4 + nt % 2)
                src = ps[:, 0:128].rearrange("p (h c) -> p h c", c=64)
                dst = VT[nt][:, 0:130].rearrange("p (h c) -> p h c", c=65)[:, :, 0:64]
                s_ev.inc(s.activation(dst, src, AF.Copy), 1)

        @blk.vector
        def _(v):
            t1 = env["scr_f32b"][:, 0:NB // 2].bitcast(BF16)
            t2 = env["scr_f32b"][:, NB // 2:NB].bitcast(BF16)
            for u in range(8):
                i, nb = u // 4, u % 4
                dest = QT if i == 0 else KT
                sl = slice(nb * NB, (nb + 1) * NB)
                v.wait_ge(s_ev.h, b_ev + 2 * u + 2)
                i1 = v.tensor_tensor(t1, SWs[:, 0:NB], CS[:, sl], ALU.mult)
                i2 = v.tensor_tensor(t2, SWs[:, NB:2 * NB],
                                     CS[:, N + nb * NB:N + (nb + 1) * NB], ALU.mult)
                p.chain(v, "vch", i2)
                i3 = v.tensor_tensor(t1, t1, t2, ALU.add)
                p.chain(v, "vch", i3)
                ccol_i = 8 + 16 * (u // 4) + hp
                i4 = v.scalar_tensor_tensor(t2, bank(env, 0),
                                            env["wcol"][:, ccol_i:ccol_i + 1],
                                            t1, ALU.add, ALU.add)
                p.chain(v, "vch", i4)
                i5 = v.tensor_copy(dest[:, sl], t2)
                p.link(v, s_cb, i5)




def emit_block_l1_attn(p, env):
    nc = p.nc
    emit_norm(p, env, kind="ln")
    TN = env["TN"]
    wa = p.dram_in("w_l1a_a", (1024, 512))
    wb = p.dram_in("w_l1a_b", (1024, 512))
    wv = p.dram_in("w_l1a_v", (1024, 1024))
    wo = p.dram_in("w_l1a_o", (1024, 1024))
    with contextlib.ExitStack() as outer:
        osb = lambda nm, sh, dt: outer.enter_context(
            nc.sbuf_tensor("l1a" + nm, list(sh), dt))
        OT = [osb(f"_ot{c}", [128, N], BF16) for c in range(NCH)]
        with contextlib.ExitStack() as scope:
            sb = lambda nm, sh, dt: scope.enter_context(
                nc.sbuf_tensor("l1a" + nm, list(sh), dt))
            AT = [sb(f"_at{i}", [128, N], BF16) for i in range(4)]
            BT = [sb(f"_bt{i}", [128, N], BF16) for i in range(4)]
            VH = sb("_vh", [128, 16 * 64], BF16)
            WVH = sb("_wvh", [128, 8 * 64], BF16)
            EB = env["scr_f32"][:].bitcast(BF16)[:, 0:N]
            WSAB = env["scr_f32b"][:].bitcast(BF16)[:, 0:4096]
            rws = env["rows"]
            dcol = sb("_dcol", [128, 1], F32)
            rcol = sb("_rcol", [128, 1], F32)
            emit_projA(p, env, "l1a_a", wa, 1024, 512, TN, WSAB,
                       scalar_cb=evac_to_tiles(env, AT))
            emit_projA(p, env, "l1a_b", wb, 1024, 512, TN, WSAB,
                       scalar_cb=evac_to_tiles(env, BT))
            for h in range(16):
                _l1a_head(p, env, h, wv, TN, AT, BT, VH, WVH, EB, OT,
                          dcol, rcol)
        with contextlib.ExitStack() as scope2:
            WSO = scope2.enter_context(
                nc.sbuf_tensor("l1a_wso", [128, 8192], BF16))
            emit_projA(p, env, "l1a_wo", wo, 1024, 1024, OT, WSO,
                       vector_cb=evac_residual(env))


def _l1a_head(p, env, h, wv, TN, AT, BT, VH, WVH, EB, OT, dcol, rcol):
    nc = p.nc
    ch, base = h // 4, 32 * (h % 4)
    s_w = p.sem("l1h_w")
    s_vm = p.sem("l1h_vm")
    s_ve = p.sem("l1h_ve")
    s_tm = p.sem("l1h_tm")
    s_ex = p.sem("l1h_ex")
    s_ef = p.sem("l1h_ef")
    s_av = p.sem("l1h_av")
    s_oe = p.sem("l1h_oe")
    b_w, b_vm, b_ve, b_tm, b_ex, b_ef, b_av, b_oe = (
        s_w.n, s_vm.n, s_ve.n, s_tm.n, s_ex.n, s_ef.n, s_av.n, s_oe.n)

    with nc.Block() as blk:
        @blk.gpsimd
        def _(g):
            src = wv[:, h * 64:(h + 1) * 64].rearrange("(kc p) f -> p kc f", p=128)
            s_w.inc(g.dma_start(
                WVH[:].rearrange("p (kc f) -> p kc f", f=64), src), 16)

        @blk.tensor
        def _(t):
            t.wait_ge(s_w.h, b_w + 16)
            for nt in range(16):
                ps = bank(env, nt % 2)
                if nt >= 2:
                    t.wait_ge(s_ve.h, b_ve + nt - 1)
                mm = None
                for ic in range(8):
                    mm = t.matmul(ps[:, 0:64],
                                  TN[ic][:, nt * 128:(nt + 1) * 128],
                                  WVH[:, ic * 64:(ic + 1) * 64],
                                  start=(ic == 0), stop=(ic == 7))
                s_vm.inc(mm, 1)
            for mt in range(16):
                t.wait_ge(s_ve.h, b_ve + 16)
                if mt >= 1:
                    t.wait_ge(s_ex.h, b_ex + mt)
                mm = None
                for nb in range(NNB):
                    mm = t.matmul(bank(env, nb),
                                  BT[ch][base:base + 32,
                                         mt * 128:(mt + 1) * 128],
                                  AT[ch][base:base + 32,
                                         nb * NB:(nb + 1) * NB],
                                  start=True, stop=True,
                                  tile_position=(base, 0))
                s_tm.inc(mm, 1)
                t.wait_ge(s_ef.h, b_ef + mt + 1)
                mm = None
                for nb in range(NNB):
                    mm = t.matmul(bank(env, 4 + nb)[0:64, :],
                                  VH[:, mt * 64:(mt + 1) * 64],
                                  EB[:, nb * NB:(nb + 1) * NB],
                                  start=(mt == 0), stop=(mt == 15))
                s_av.inc(mm, 1)

        @blk.scalar
        def _(s):
            for nt in range(16):
                s.wait_ge(s_vm.h, b_vm + nt + 1)
                ps = bank(env, nt % 2)
                s_ve.inc(s.activation(VH[:, nt * 64:(nt + 1) * 64],
                                      ps[:, 0:64], AF.Copy), 1)
            for mt in range(16):
                s.wait_ge(s_tm.h, b_tm + mt + 1)
                if mt >= 1:
                    s.wait_ge(s_av.h, b_av + mt)
                s_ex.inc(s.activation(EB[:], env["PSA"][0][:, :], AF.Exp,
                                      scale=0.25, accum_out=dcol[:]), 1)
            s.wait_ge(s_av.h, b_av + 16)
            oc = OT[h // 2]
            off = 64 * (h % 2)
            for nb in range(NNB):
                s_oe.inc(s.activation(oc[off:off + 64, nb * NB:(nb + 1) * NB],
                                      bank(env, 4 + nb)[0:64, :], AF.Copy), 1)

        @blk.vector
        def _(v):
            for mt in range(16):
                v.wait_ge(s_ex.h, b_ex + mt + 1)
                i1 = v.reciprocal(rcol[:], dcol[:])
                p.chain(v, "vch", i1)
                i2 = v.tensor_scalar(EB[:], EB[:], rcol[:], 1.0,
                                     ALU.mult, ALU.mult)
                p.link(v, s_ef, i2)




def emit_block_l1_ffn(p, env):
    nc = p.nc
    emit_norm(p, env, kind="ln")
    TN = env["TN"]
    sg = p.dram_in("w_l1f_sg", (1024, 1024))
    su = p.dram_in("w_l1f_su", (1024, 1024))
    sd = p.dram_in("w_l1f_sd", (1024, 1024))
    wr = p.dram_in("w_l1f_r", (1024, 16))
    wg = p.dram_in("w_l1f_g", (16, 1024, 512))
    wu = p.dram_in("w_l1f_u", (16, 1024, 512))
    wd = p.dram_in("w_l1f_d", (16, 512, 1024))
    ohbc = p.dram_in("c_ohbc", (16, 16 * 128))
    with contextlib.ExitStack() as scope:
        sb = lambda nm, sh, dt: scope.enter_context(
            nc.sbuf_tensor("l1f" + nm, list(sh), dt))
        with contextlib.ExitStack() as sc1:
            HUV = [sc1.enter_context(nc.sbuf_tensor(f"l1f_huv{i}", [128, N], BF16))
                   for i in range(NCH)]
            WSH = sc1.enter_context(nc.sbuf_tensor("l1f_wsh", [128, 16384], BF16))
            _l1f_shared(p, env, sg, su, TN, HUV, WSH)
            emit_projA(p, env, "l1f_sd", sd, 1024, 1024, HUV, WSH,
                       vector_cb=evac_residual(env))
        CR = sb("_cr", [16, N], BF16)
        ROUT = sb("_rout", [128, 64], F32)
        TMPG = sb("_tmpg", [128, 1024], BF16)
        with nc.Block() as blk0:
            @blk0.vector
            def _(v):
                v.memset(CR[:, :], 0.0)
        _l1f_router(p, env, scope, wr, TN, CR, ROUT)
        WSE = sb("_wse", [128, 12 * 1024], BF16)
        HGE = sb("_hge", [128, 4 * NB], BF16)
        CBC = sb("_cbc", [128, NB], F32)
        OHB = sb("_ohb", [16, 128], BF16)
        for e in range(16):
            _l1f_expert(p, env, e, wg, wu, wd, ohbc, TN, CR, WSE, HGE,
                        CBC, OHB, TMPG)


def _l1f_shared(p, env, sg, su, TN, HUV, WSH):
    nc = p.nc
    s_w = p.sem("l1s_w")
    s_mm = p.sem("l1s_mm")
    s_si = p.sem("l1s_si")
    s_hv = p.sem("l1s_hv")
    b_w, b_mm, b_si, b_hv = s_w.n, s_mm.n, s_si.n, s_hv.n
    with nc.Block() as blk:
        @blk.gpsimd
        def _(g):
            for j, w in enumerate((sg, su)):
                src = w.rearrange("(kc p) f -> p kc f", p=128)
                dst = WSH[:, j * 8192:(j + 1) * 8192].rearrange(
                    "p (kc f) -> p kc f", f=1024)
                s_w.inc(g.dma_start(dst, src), 16)

        @blk.tensor
        def _(t):
            t.wait_ge(s_w.h, b_w + 32)
            idx = 0
            for ft in range(NCH):
                for nb in range(NNB):
                    if idx >= 2:
                        t.wait_ge(s_hv.h, b_hv + idx - 1)
                    for j in range(2):
                        ps = bank(env, 2 * (idx % 2) + j)
                        mm = None
                        for ic in range(8):
                            mm = t.matmul(
                                ps,
                                WSH[:, j * 8192 + ic * 1024 + ft * 128:
                                    j * 8192 + ic * 1024 + (ft + 1) * 128],
                                TN[ic][:, nb * NB:(nb + 1) * NB],
                                start=(ic == 0), stop=(ic == 7))
                        s_mm.inc(mm, 1)
                    idx += 1

        @blk.scalar
        def _(s):
            idx = 0
            for ft in range(NCH):
                for nb in range(NNB):
                    s.wait_ge(s_mm.h, b_mm + 2 * idx + 1)
                    psg = bank(env, 2 * (idx % 2))
                    tmp = env["scr_f32"][:, 0:NB].bitcast(BF16)[:, (idx % 2) * NB:
                                                                 (idx % 2) * NB + NB]
                    s_si.inc(s.activation(tmp, psg, AF.Silu), 1)
                    idx += 1

        @blk.vector
        def _(v):
            idx = 0
            for ft in range(NCH):
                for nb in range(NNB):
                    v.wait_ge(s_si.h, b_si + idx + 1)
                    v.wait_ge(s_mm.h, b_mm + 2 * idx + 2)
                    psu = bank(env, 2 * (idx % 2) + 1)
                    tmp = env["scr_f32"][:, 0:NB].bitcast(BF16)[:, (idx % 2) * NB:
                                                                 (idx % 2) * NB + NB]
                    s_hv.inc(v.tensor_tensor(HUV[ft][:, nb * NB:(nb + 1) * NB],
                                             tmp, psu, ALU.mult), 1)
                    idx += 1


def _l1f_router(p, env, scope, wr, TN, CR, ROUT):
    nc = p.nc
    idf = p.dram_in("c_identity", (128, 128))
    WR = scope.enter_context(nc.sbuf_tensor("l1f_wr", [128, 128], BF16))
    IDF = scope.enter_context(nc.sbuf_tensor("l1f_id", [128, 128], F32))
    m1 = ROUT[:, 16:17]
    nm1 = ROUT[:, 17:18]
    den = ROUT[:, 18:19]
    gcol = ROUT[:, 19:20]
    m2 = ROUT[:, 20:21]
    prb = ROUT[:, 32:48]
    is12 = ROUT[:, 0:16]
    ctok = ROUT[:, 0:16]
    s_w = p.sem("l1r_w")
    s_mm = p.sem("l1r_mm")
    s_v1 = p.sem("l1r_v1")
    s_sc = p.sem("l1r_sc")
    s_v2 = p.sem("l1r_v2")
    s_tr = p.sem("l1r_tr")
    s_cp = p.sem("l1r_cp")
    b_w, b_mm, b_v1, b_sc, b_v2, b_tr, b_cp = (
        s_w.n, s_mm.n, s_v1.n, s_sc.n, s_v2.n, s_tr.n, s_cp.n)

    with nc.Block() as blk:
        @blk.gpsimd
        def _(g):
            s_w.inc(g.dma_start(
                WR[:].rearrange("p (kc f) -> p kc f", f=16),
                wr.rearrange("(kc p) f -> p kc f", p=128)), 16)

        @blk.sync
        def _(sy):
            s_w.inc(sy.dma_start(IDF[:], idf[:, :]), 16)

        @blk.tensor
        def _(t):
            t.wait_ge(s_w.h, b_w + 32)
            for nt in range(16):
                ps = bank(env, nt % 2)
                if nt >= 2:
                    t.wait_ge(s_v1.h, b_v1 + nt - 1)
                mm = None
                for ic in range(8):
                    mm = t.matmul(ps[:, 0:16], TN[ic][:, nt * 128:(nt + 1) * 128],
                                  WR[:, ic * 16:(ic + 1) * 16],
                                  start=(ic == 0), stop=(ic == 7))
                s_mm.inc(mm, 1)
                t.wait_ge(s_v2.h, b_v2 + nt + 1)
                ps2 = bank(env, 4 + nt % 2)
                s_tr.inc(t.transpose(ps2[0:16, 0:128], ctok[:, 0:16], IDF[:]), 1)

        @blk.vector
        def _(v):
            for nt in range(16):
                v.wait_ge(s_mm.h, b_mm + nt + 1)
                ps = bank(env, nt % 2)
                i1 = v.tensor_reduce(m1, ps[:, 0:16], axis=AX, op=ALU.max)
                p.chain(v, "vch", i1)
                i3 = v.tensor_scalar(nm1, m1, -1.0, 0.0, ALU.mult, ALU.add)
                p.link(v, s_v1, i3)
                v.wait_ge(s_sc.h, b_sc + nt + 1)
                i4 = v.reciprocal(gcol, den)
                p.chain(v, "vch", i4)
                i5 = v.tensor_scalar(prb, prb, gcol, 1.0, ALU.mult, ALU.mult)
                p.chain(v, "vch", i5)
                i6 = v.tensor_reduce(m1, prb, axis=AX, op=ALU.max)
                p.chain(v, "vch", i6)
                i7 = v.tensor_scalar(is12, prb, m1, 1.0, ALU.is_ge, ALU.mult)
                p.chain(v, "vch", i7)
                msk16 = ROUT[:, 48:64]
                i8 = v.scalar_tensor_tensor(msk16, is12, -10.0,
                                            prb, ALU.mult, ALU.add)
                p.chain(v, "vch", i8)
                i9 = v.tensor_reduce(m2, msk16, axis=AX, op=ALU.max)
                p.chain(v, "vch", i9)
                i10 = v.tensor_scalar(msk16, msk16, m2, 1.0,
                                      ALU.is_ge, ALU.mult)
                p.chain(v, "vch", i10)
                i11 = v.tensor_tensor(is12, is12, msk16, ALU.add)
                p.chain(v, "vch", i11)
                v.wait_ge(s_tr.h, b_tr + nt)
                i12 = v.tensor_tensor(ctok[:, 0:16], is12, prb, ALU.mult)
                p.link(v, s_v2, i12)

        @blk.scalar
        def _(s):
            for nt in range(16):
                s.wait_ge(s_v1.h, b_v1 + nt + 1)
                ps = bank(env, nt % 2)
                s_sc.inc(s.activation(prb, ps[:, 0:16], AF.Exp,
                                      bias=nm1, accum_out=den), 1)
                s.wait_ge(s_tr.h, b_tr + nt + 1)
                ps2 = bank(env, 4 + nt % 2)
                s_cp.inc(s.activation(CR[0:16, nt * 128:(nt + 1) * 128],
                                      ps2[0:16, 0:128], AF.Copy), 1)


def _l1f_expert(p, env, e, wg, wu, wd, ohbc, TN, CR, WSE, HGE, CBC, OHB,
                TMPG):
    nc = p.nc
    s_w = p.sem("l1e_w")
    s_mm = p.sem("l1e_mm")
    s_si = p.sem("l1e_si")
    s_hv = p.sem("l1e_hv")
    s_cb = p.sem("l1e_cb")
    s_cc = p.sem("l1e_cc")
    s_m2 = p.sem("l1e_m2")
    s_tm = p.sem("l1e_tm")
    s_ad = p.sem("l1e_ad")
    b_w, b_mm, b_si, b_hv, b_cb, b_cc, b_m2, b_tm, b_ad = (
        s_w.n, s_mm.n, s_si.n, s_hv.n, s_cb.n, s_cc.n, s_m2.n, s_tm.n, s_ad.n)
    with nc.Block() as blk:
        @blk.gpsimd
        def _(g):
            for j, w in enumerate((wg, wu)):
                src = w[e].rearrange("(kc p) f -> p kc f", p=128)
                dst = WSE[:, j * 4096:(j + 1) * 4096].rearrange(
                    "p (kc f) -> p kc f", f=512)
                s_w.inc(g.dma_start(dst, src), 16)
            src = wd[e].rearrange("(kc p) f -> p kc f", p=128)
            dst = WSE[:, 8192:12288].rearrange("p (kc f) -> p kc f", f=1024)
            s_w.inc(g.dma_start(dst, src), 16)
            s_w.inc(g.dma_start(OHB[:], ohbc[0:16, e * 128:(e + 1) * 128]), 16)
            for nb in range(NNB):
                for dt in range(NCH):
                    g.wait_ge(s_tm.h, b_tm + nb * NCH + dt + 1)
                    tmp = (env["scr_f32"] if dt < 4 else env["scr_f32b"])[
                        :, (dt % 4) * NB:(dt % 4 + 1) * NB]
                    sl = slice(nb * NB, (nb + 1) * NB)
                    s_ad.inc(g.tensor_tensor(env["X"][dt][:, sl],
                                             env["X"][dt][:, sl], tmp,
                                             ALU.add), 1)

        @blk.tensor
        def _(t):
            t.wait_ge(s_w.h, b_w + 64)
            for nb in range(NNB):
                for ft in range(4):
                    u = nb * 4 + ft
                    if u >= 2:
                        t.wait_ge(s_hv.h, b_hv + u - 1)
                    for j in range(2):
                        ps = bank(env, 2 * (u % 2) + j)
                        mm = None
                        for ic in range(8):
                            mm = t.matmul(
                                ps,
                                WSE[:, j * 4096 + ic * 512 + ft * 128:
                                    j * 4096 + ic * 512 + (ft + 1) * 128],
                                TN[ic][:, nb * NB:(nb + 1) * NB],
                                start=(ic == 0), stop=(ic == 7))
                        s_mm.inc(mm, 1)
                if nb >= 1:
                    t.wait_ge(s_cc.h, b_cc + nb)
                s_cb.inc(t.matmul(bank(env, 4), OHB[:],
                                  CR[:, nb * NB:(nb + 1) * NB],
                                  start=True, stop=True), 1)
                t.wait_ge(s_hv.h, b_hv + (nb + 1) * 4)
                for dt in range(NCH):
                    idx2 = nb * NCH + dt
                    ps = bank(env, 5 + idx2 % 2)
                    if idx2 >= 2:
                        t.wait_ge(s_tm.h, b_tm + idx2 - 1)
                    mm = None
                    for kt in range(4):
                        mm = t.matmul(
                            ps, WSE[:, 8192 + kt * 1024 + dt * 128:
                                    8192 + kt * 1024 + (dt + 1) * 128],
                            HGE[:, kt * NB:(kt + 1) * NB],
                            start=(kt == 0), stop=(kt == 3))
                    s_m2.inc(mm, 1)

        @blk.scalar
        def _(s):
            for nb in range(NNB):
                for ft in range(4):
                    u = nb * 4 + ft
                    s.wait_ge(s_mm.h, b_mm + 2 * u + 1)
                    if nb >= 1 and ft == 0:
                        s.wait_ge(s_m2.h, b_m2 + nb * NCH)
                    psg = bank(env, 2 * (u % 2))
                    tmp = TMPG[:, (u % 2) * NB:(u % 2) * NB + NB]
                    s_si.inc(s.activation(tmp, psg, AF.Silu), 1)
                s.wait_ge(s_cb.h, b_cb + nb + 1)
                if nb >= 1:
                    s.wait_ge(s_tm.h, b_tm + nb * NCH)
                s_cc.inc(s.activation(CBC[:], bank(env, 4), AF.Copy), 1)

        @blk.vector
        def _(v):
            for nb in range(NNB):
                for ft in range(4):
                    u = nb * 4 + ft
                    v.wait_ge(s_si.h, b_si + u + 1)
                    v.wait_ge(s_mm.h, b_mm + 2 * u + 2)
                    if nb >= 1 and ft == 0:
                        v.wait_ge(s_m2.h, b_m2 + nb * NCH)
                    psu = bank(env, 2 * (u % 2) + 1)
                    tmp = TMPG[:, (u % 2) * NB:(u % 2) * NB + NB]
                    s_hv.inc(v.tensor_tensor(HGE[:, ft * NB:(ft + 1) * NB],
                                             tmp, psu, ALU.mult), 1)
                v.wait_ge(s_cc.h, b_cc + nb + 1)
                for dt in range(NCH):
                    idx2 = nb * NCH + dt
                    v.wait_ge(s_m2.h, b_m2 + idx2 + 1)
                    if nb >= 1:
                        v.wait_ge(s_ad.h, b_ad + nb * NCH)
                    ps = bank(env, 5 + idx2 % 2)
                    tmp2 = (env["scr_f32"] if dt < 4 else env["scr_f32b"])[
                        :, (dt % 4) * NB:(dt % 4 + 1) * NB]
                    s_tm.inc(v.tensor_tensor(tmp2, ps, CBC[:], ALU.mult), 1)




def emit_shift_to_tn(p, env):
    nc = p.nc
    s_v = p.sem("sh_v")
    with nc.Block() as blk:
        @blk.vector
        def _(v):
            for c in range(NCH):
                if c < 4:
                    v.tensor_copy(env["TN"][c][:], env["X"][c][:])
                else:
                    v.memset(env["TN"][c][:, 0:1], 0.0)
                    v.tensor_copy(env["TN"][c][:, 1:N], env["X"][c][:, 0:N - 1])


def emit_norm_tn_affine(p, env, g_ap, b_ap):
    nc = p.nc
    TN = env["TN"]
    sq, sq2 = env["scr_f32"], env["scr_f32b"]
    rows = env["rows"]
    mu = rows[0:1, 0:N]
    mu2 = rows[0:1, N:2 * N]
    var = rows[0:1, N:2 * N]
    sdev = rows[0:1, N:2 * N]
    rstd = rows[0:1, N:2 * N]
    brow = rows[0:1, 0:N]
    ones_col = ccol(env, "one")
    ones_bf = env["ones_bf"]
    orow = env["ones_row128"]
    s_w = p.sem("na_w")
    s_sq = p.sem("na_sq")
    s_mm = p.sem("na_mm")
    s_rv1 = p.sem("na_rv1")
    s_rsc = p.sem("na_rsc")
    s_rv2 = p.sem("na_rv2")
    s_bc = p.sem("na_bc")
    s_ap = p.sem("na_ap")
    b_w, b_sq, b_mm, b_rv1, b_rsc, b_rv2, b_bc, b_apl = (
        s_w.n, s_sq.n, s_mm.n, s_rv1.n, s_rsc.n, s_rv2.n, s_bc.n, s_ap.n)
    ps_sq = env["PSA"][0][0:1, :]
    ps_sx = env["PSA"][1][0:1, :]

    with nc.Block() as blk:
        @blk.sync
        def _(sy):
            s_w.inc(sy.dma_start(env["wcol"][:, 0:8], g_ap), 16)
            s_w.inc(sy.dma_start(env["wcol"][:, 8:16], b_ap), 16)

        @blk.scalar
        def _(s):
            for c in range(NCH):
                buf = sq if c % 2 == 0 else sq2
                if c >= 2:
                    s.wait_ge(s_mm.h, b_mm + c - 1)
                s_sq.inc(s.activation(buf[:], TN[c][:], AF.Square), 1)
            s.wait_ge(s_rv1.h, b_rv1 + 1)
            s_rsc.inc(s.activation(sdev, var, AF.Sqrt,
                                   bias=ccol(env, "eps5")[0:1, :]), 1)

        @blk.tensor
        def _(t):
            for c in range(NCH):
                buf = sq if c % 2 == 0 else sq2
                t.wait_ge(s_sq.h, b_sq + c + 1)
                mm = None
                for nb in range(NNB):
                    t.matmul(ps_sx[:, nb * NB:(nb + 1) * NB], ones_bf[:],
                             TN[c][:, nb * NB:(nb + 1) * NB],
                             start=(c == 0), stop=(c == NCH - 1))
                    mm = t.matmul(ps_sq[:, nb * NB:(nb + 1) * NB], ones_col,
                                  buf[:, nb * NB:(nb + 1) * NB],
                                  start=(c == 0), stop=(c == NCH - 1))
                s_mm.inc(mm, 1)
            t.wait_ge(s_rv2.h, b_rv2 + 1)
            for nb in range(NNB):
                pa = bank(env, (nb % 2) * 4 + 0)
                pb = bank(env, (nb % 2) * 4 + 1)
                if nb >= 2:
                    t.wait_ge(s_ap.h, b_apl + nb - 1)
                s_bc.inc(t.matmul(pa, orow[:], rstd[:, nb * NB:(nb + 1) * NB],
                                  start=True, stop=True), 1)
                s_bc.inc(t.matmul(pb, orow[:], brow[:, nb * NB:(nb + 1) * NB],
                                  start=True, stop=True), 1)

        @blk.vector
        def _(v):
            inv_d = 1.0 / D
            v.wait_ge(s_w.h, b_w + 32)
            v.wait_ge(s_mm.h, b_mm + NCH)
            p.chain(v, "vch", v.tensor_scalar(mu, ps_sx, inv_d, 0.0,
                                              ALU.mult, ALU.add))
            p.chain(v, "vch", v.tensor_tensor(mu2, mu, mu, ALU.mult))
            i3 = v.scalar_tensor_tensor(var, ps_sq, inv_d, mu2,
                                        ALU.mult, ALU.subtract)
            p.chain(v, "vch", i3)
            s_rv1.inc(v.nop(), 1)
            v.wait_ge(s_rsc.h, b_rsc + 1)
            p.chain(v, "vch", v.reciprocal(rstd, sdev))
            p.chain(v, "vch", v.scalar_tensor_tensor(brow, mu, -1.0, rstd,
                                                     ALU.mult, ALU.mult))
            s_rv2.inc(v.nop(), 1)
            for nb in range(NNB):
                pa = bank(env, (nb % 2) * 4 + 0)
                pb = bank(env, (nb % 2) * 4 + 1)
                v.wait_ge(s_bc.h, b_bc + (nb + 1) * 2)
                sl = slice(nb * NB, (nb + 1) * NB)
                tmps = [(sq if c < 4 else sq2)[:, (c % 4) * NB:(c % 4 + 1) * NB]
                        for c in range(NCH)]
                last = None
                for c in range(NCH):
                    last = v.tensor_tensor(tmps[c], TN[c][:, sl], pa, ALU.mult)
                p.chain(v, "vch", last)
                last = None
                for c in range(NCH):
                    last = v.tensor_tensor(TN[c][:, sl], tmps[c], pb, ALU.add)
                p.chain(v, "vch", last)
                last = None
                for c in range(NCH):
                    last = v.tensor_scalar(TN[c][:, sl], TN[c][:, sl],
                                           env["wcol"][:, c:c + 1],
                                           env["wcol"][:, 8 + c:9 + c],
                                           ALU.mult, ALU.add)
                p.chain(v, "vch", last)
                s_ap.inc(v.nop(), 1)


def emit_block_l3_ffn(p, env):
    nc = p.nc
    emit_spill_x(p, env, name="xsave3")
    emit_norm(p, env, kind="ln", out="X")
    for l in range(2):
        _l3f_layer(p, env, l)
    emit_restore_add_x(p, env, name="xsave3")


def _l3f_layer(p, env, l):
    nc = p.nc
    wqkv = p.dram_in("w_l3f_qkv", (2, 1024, 1536))
    wo = p.dram_in("w_l3f_o", (2, 512, 1024))
    w1 = p.dram_in("w_l3f_w1", (2, 1024, 5632))
    w2 = p.dram_in("w_l3f_w2", (2, 2816, 1024))
    b1a = p.dram_in("b_l3f_b1a", (2, 128, 22))
    b1g = p.dram_in("b_l3f_b1g", (2, 128, 22))
    b2r = p.dram_in("b2row_l3f", (2, 1024))
    qsap = p.dram_in("w_l3f_qs", (2, 128, 1))
    ksap = p.dram_in("w_l3f_ks", (2, 128, 1))
    lng = p.dram_in("w_l3f_lng", (2, 128, 8))
    lnb = p.dram_in("w_l3f_lnb", (2, 128, 8))
    TN = env["TN"]

    emit_shift_to_tn(p, env)

    with contextlib.ExitStack() as scope:
        sb = lambda nm, sh, dt: scope.enter_context(
            nc.sbuf_tensor(f"l3f{l}" + nm, list(sh), dt))
        QTp = [sb(f"_qt{i}", [128, N], BF16) for i in range(4)]
        KTp = [sb(f"_kt{i}", [128, N], BF16) for i in range(4)]
        VT = [sb(f"_vt{i}", [128, 520], BF16) for i in range(16)]
        PB = sb("_pb", [128, 8 * NB + NB], BF16)
        OTL = env["TN"][0:4]
        TBL = env["TN"][4:8]
        BCOL = sb("_bc", [128, 32], F32)
        PRM = sb("_prm", [128, 128], BF16)
        rws = env["rows"]
        RROW = rws[0:1, 0:NB]
        AVF = PB[0:64, 8 * NB:9 * NB]

        with contextlib.ExitStack() as sc1:
            STG = sc1.enter_context(nc.sbuf_tensor(f"l3f{l}_stg", [128, 4096], BF16))
            s_c = p.sem("l3f_c")
            b_c = s_c.n
            with nc.Block() as blk:
                @blk.gpsimd
                def _(g):
                    s_c.inc(g.dma_start(PRM[:], p.dram_in("c_swap32", (128, 128))[:, :]), 16)

                @blk.sync
                def _(sy):
                    s_c.inc(sy.dma_start(env["wcol"][:, 30:31], qsap[l, :, :]), 16)
                    s_c.inc(sy.dma_start(env["wcol"][:, 31:32], ksap[l, :, :]), 16)

                @blk.vector
                def _(v):
                    for i in range(16):
                        for hh in range(8):
                            v.memset(VT[i][:, 65 * hh + 64:65 * hh + 65], 1.0)
                    v.wait_ge(s_c.h, b_c + 3 * 16)
            emit_projA(p, env, "l3f_q", wqkv[l], 1024, 512, TN, STG,
                       scalar_cb=evac_to_tiles(env, QTp))
            emit_projA(p, env, "l3f_k", wqkv[l], 1024, 512, TN, STG,
                       scalar_cb=evac_to_tiles(env, KTp), w_col0=512)
            _l3f_vproj(p, env, l, wqkv, TN, VT, STG)
            s_c2 = p.sem("l3f_c2")
            b_c2 = s_c2.n
            with nc.Block() as blk:
                @blk.gpsimd
                def _(g):
                    for j, nm in enumerate(("c_l3f_cq", "c_l3f_sq",
                                            "c_l3f_ck", "c_l3f_sk")):
                        s_c2.inc(g.dma_start(TBL[j][:], p.dram_in(nm, (128, N))[:, :]), 16)
                    g.wait_ge(s_c2.h, b_c2 + 64)

        SQA = env["scr_f32"][:].bitcast(BF16)[:, 0:N]
        SQK = env["scr_f32"][:].bitcast(BF16)[:, N:2 * N]
        for c in range(4):
            _l3f_qk_l2n(p, env, c, QTp[c], env["wcol"][:, 30:31])
            _l3f_qk_l2n(p, env, c, KTp[c], env["wcol"][:, 31:32])
            _l3f_bounds(p, env, c, QTp[c], KTp[c], SQA, SQK, BCOL)
            _l3f_qk_xpos(p, env, c, QTp[c], TBL[0], TBL[1], PRM)
            _l3f_qk_xpos(p, env, c, KTp[c], TBL[2], TBL[3], PRM)

        s_m = p.sem("l3f_m")
        b_m = s_m.n
        mask_ap = p.dram_in("c_l3f_mask", (1024, 512))
        with nc.Block() as blk:
            @blk.sync
            def _(sy):
                s_m.inc(sy.dma_start(
                    env["scr_f32"][:].rearrange("p (r q) -> p r q", r=4),
                    mask_ap[0:512, :].rearrange("(r p) q -> p r q", p=128)), 16)
                s_m.inc(sy.dma_start(
                    env["scr_f32b"][:].rearrange("p (r q) -> p r q", r=4),
                    mask_ap[512:1024, :].rearrange("(r p) q -> p r q", p=128)), 16)
                sy.wait_ge(s_m.h, b_m + 32)

        for h in range(8):
            _l3f_head(p, env, h, QTp, KTp, VT, PB, OTL, BCOL, RROW, AVF)

        with contextlib.ExitStack() as sc2:
            WSO = sc2.enter_context(nc.sbuf_tensor(f"l3f{l}_wso", [128, 4096], BF16))
            emit_projA(p, env, "l3f_wo", wo[l], 512, 1024, OTL, WSO,
                       vector_cb=evac_residual(env))

    emit_shift_to_tn(p, env)
    emit_norm_tn_affine(p, env, lng[l, :, :], lnb[l, :, :])
    _l3f_mlp(p, env, l, w1, w2, b1a, b1g, b2r)


def _l3f_vproj(p, env, l, wqkv, TN, VT, STG):
    nc = p.nc
    s_w = p.sem("l3v_w")
    s_mm = p.sem("l3v_mm")
    s_ev = p.sem("l3v_ev")
    b_w, b_mm, b_ev = s_w.n, s_mm.n, s_ev.n
    with nc.Block() as blk:
        @blk.gpsimd
        def _(g):
            src = wqkv[l][:, 1024:1536].rearrange("(kc p) f -> p kc f", p=128)
            dst = STG[:, 0:4096].rearrange("p (kc f) -> p kc f", f=512)
            s_w.inc(g.dma_start(dst, src), 16)

        @blk.tensor
        def _(t):
            t.wait_ge(s_w.h, b_w + 16)
            for nt in range(16):
                ps = bank(env, nt % 2)
                if nt >= 2:
                    t.wait_ge(s_ev.h, b_ev + nt - 1)
                mm = None
                for ic in range(8):
                    mm = t.matmul(ps, TN[ic][:, nt * 128:(nt + 1) * 128],
                                  STG[:, ic * 512:(ic + 1) * 512],
                                  start=(ic == 0), stop=(ic == 7))
                s_mm.inc(mm, 1)

        @blk.scalar
        def _(s):
            for nt in range(16):
                s.wait_ge(s_mm.h, b_mm + nt + 1)
                ps = bank(env, nt % 2)
                src = ps[:, :].rearrange("p (h c) -> p h c", c=64)
                dst = VT[nt][:, 0:520].rearrange("p (h c) -> p h c", c=65)[:, :, 0:64]
                s_ev.inc(s.activation(dst, src, AF.Copy), 1)


def _l3f_qk_l2n(p, env, c, QC, scol):
    nc = p.nc
    rws = env["rows"]
    s_sq = p.sem("l3n_sq")
    s_nm = p.sem("l3n_nm")
    s_sc = p.sem("l3n_sc")
    s_rv = p.sem("l3n_rv")
    s_bc = p.sem("l3n_bc")
    s_l2 = p.sem("l3n_l2")
    b_sq, b_nm, b_sc, b_rv, b_bc, b_l2 = (s_sq.n, s_nm.n, s_sc.n, s_rv.n,
                                          s_bc.n, s_l2.n)
    SQB = env["scr_f32"][:].bitcast(BF16)[:, 0:N]
    n0 = rws[0:1, 0:N]
    n1 = rws[0:1, N:2 * N]
    ones_bf = env["ones_bf"]
    orow = env["ones_row128"]
    with nc.Block() as blk:
        @blk.scalar
        def _(s):
            s_sq.inc(s.activation(SQB[:], QC[:], AF.Square), 1)
            s.wait_ge(s_nm.h, b_nm + 1)
            s_sc.inc(s.activation(n0, env["PSA"][0][0:1, :], AF.Sqrt), 1)
            s_sc.inc(s.activation(n1, env["PSA"][1][0:1, :], AF.Sqrt), 1)

        @blk.tensor
        def _(t):
            t.wait_ge(s_sq.h, b_sq + 1)
            mm = None
            for ai, base in ((0, 0), (1, 64)):
                for nb in range(NNB):
                    mm = t.matmul(env["PSA"][ai][0:1, nb * NB:(nb + 1) * NB],
                                  ones_bf[base:base + 64, :],
                                  SQB[base:base + 64, nb * NB:(nb + 1) * NB],
                                  start=True, stop=True,
                                  tile_position=(base, 0))
            s_nm.inc(mm, 1)
            t.wait_ge(s_rv.h, b_rv + 1)
            for nb in range(NNB):
                if nb >= 2:
                    t.wait_ge(s_l2.h, b_l2 + 2 * (nb - 1))
                s_bc.inc(t.matmul(bank(env, 2 + nb % 2)[0:64, :],
                                  orow[:, 0:64], n0[:, nb * NB:(nb + 1) * NB],
                                  start=True, stop=True), 1)
                s_bc.inc(t.matmul(bank(env, 4 + nb % 2)[0:64, :],
                                  orow[:, 0:64], n1[:, nb * NB:(nb + 1) * NB],
                                  start=True, stop=True), 1)

        @blk.vector
        def _(v):
            v.wait_ge(s_sc.h, b_sc + 2)
            i1 = v.tensor_scalar(n0, n0, 1e-12, 1.0, ALU.max, ALU.mult)
            i2 = v.tensor_scalar(n1, n1, 1e-12, 1.0, ALU.max, ALU.mult)
            p.chain(v, "vch", i2)
            i3 = v.reciprocal(n0, n0)
            i4 = v.reciprocal(n1, n1)
            p.link(v, s_rv, i4)
            for nb in range(NNB):
                v.wait_ge(s_bc.h, b_bc + 2 * (nb + 1))
                sl = slice(nb * NB, (nb + 1) * NB)
                ia = v.tensor_tensor(QC[0:64, sl], QC[0:64, sl],
                                     bank(env, 2 + nb % 2)[0:64, :], ALU.mult)
                ib = v.tensor_tensor(QC[64:128, sl], QC[64:128, sl],
                                     bank(env, 4 + nb % 2)[0:64, :], ALU.mult)
                p.chain(v, "vch", ib)
                ig = v.tensor_scalar(QC[:, sl], QC[:, sl], scol, 1.0,
                                     ALU.mult, ALU.mult)
                s_l2.inc(ia, 1)
                p.link(v, s_l2, ig)


def _l3f_qk_xpos(p, env, c, QC, CTB, STB, PRM):
    nc = p.nc
    s_pr = p.sem("l3x_pr")
    s_sw = p.sem("l3x_sw")
    s_cb = p.sem("l3x_cb")
    b_pr, b_sw, b_cb = s_pr.n, s_sw.n, s_cb.n
    with nc.Block() as blk:
        @blk.tensor
        def _(t):
            for nb in range(NNB):
                if nb >= 2:
                    t.wait_ge(s_sw.h, b_sw + nb - 1)
                s_pr.inc(t.matmul(bank(env, 6 + nb % 2), PRM[:],
                                  QC[:, nb * NB:(nb + 1) * NB],
                                  start=True, stop=True), 1)

        @blk.scalar
        def _(s):
            for nb in range(NNB):
                s.wait_ge(s_pr.h, b_pr + nb + 1)
                if nb >= 2:
                    s.wait_ge(s_cb.h, b_cb + nb - 1)
                s_sw.inc(s.activation(env["scr_f32b"][:, 0:NB].bitcast(BF16)
                                      [:, (nb % 2) * NB:(nb % 2) * NB + NB],
                                      bank(env, 6 + nb % 2), AF.Copy), 1)

        @blk.vector
        def _(v):
            t1 = env["scr_f32"][:, 0:NB // 2].bitcast(BF16)
            for nb in range(NNB):
                v.wait_ge(s_sw.h, b_sw + nb + 1)
                sl = slice(nb * NB, (nb + 1) * NB)
                sw = env["scr_f32b"][:, 0:NB].bitcast(BF16)[
                    :, (nb % 2) * NB:(nb % 2) * NB + NB]
                ia = v.tensor_tensor(t1, QC[:, sl], CTB[:, sl], ALU.mult)
                ib = v.tensor_tensor(QC[:, sl], sw, STB[:, sl], ALU.mult)
                p.chain(v, "vch", ib)
                ic_ = v.tensor_tensor(QC[:, sl], QC[:, sl], t1, ALU.add)
                p.link(v, s_cb, ic_)


def _l3f_bounds(p, env, c, QC, KC, SQA, SQK, BCOL):
    nc = p.nc
    rws = env["rows"]
    c0 = 2 * N
    mq = rws[0:1, c0:c0 + 8]
    mk = rws[0:1, c0 + 8:c0 + 16]
    mkw = rws[0:1, c0 + 16:c0 + 24]
    prod = rws[0:1, c0 + 24:c0 + 32]
    sq = rws[0:1, c0 + 32:c0 + 40]
    negb = rws[0:1, c0 + 40:c0 + 48]
    ones_bf = env["ones_bf"]
    s1 = p.sem("l3b_1")
    s2 = p.sem("l3b_2")
    s3 = p.sem("l3b_3")
    s4 = p.sem("l3b_4")
    s5 = p.sem("l3b_5")
    s6 = p.sem("l3b_6")
    s7 = p.sem("l3b_7")
    s8 = p.sem("l3b_8")
    b1, b2, b3, b4, b5, b6, b7, b8 = (s1.n, s2.n, s3.n, s4.n, s5.n, s6.n,
                                      s7.n, s8.n)
    with nc.Block() as blk:
        @blk.scalar
        def _(s):
            s1.inc(s.activation(SQA[:], QC[:], AF.Square), 1)
            s1.inc(s.activation(SQK[:], KC[:], AF.Square), 1)
            s.wait_ge(s5.h, b5 + 1)
            s6.inc(s.activation(sq, prod, AF.Sqrt), 1)
            for j in range(8):
                s.wait_ge(s8.h, b8 + j + 1)
                h2, w = j // 4, j % 4
                s.activation(BCOL[:, 4 * (2 * c + h2) + w:
                                  4 * (2 * c + h2) + w + 1],
                             bank(env, j % 2)[:, 0:1], AF.Copy)

        @blk.tensor
        def _(t):
            t.wait_ge(s1.h, b1 + 1)
            mm = None
            for h2, base in ((0, 0), (1, 64)):
                for nb in range(NNB):
                    mm = t.matmul(env["PSA"][h2][0:1, nb * NB:(nb + 1) * NB],
                                  ones_bf[base:base + 64, :],
                                  SQA[base:base + 64, nb * NB:(nb + 1) * NB],
                                  start=True, stop=True,
                                  tile_position=(base, 0))
            s2.inc(mm, 1)
            t.wait_ge(s1.h, b1 + 2)
            t.wait_ge(s3.h, b3 + 1)
            for h2, base in ((0, 0), (1, 64)):
                for nb in range(NNB):
                    mm = t.matmul(env["PSA"][h2][0:1, nb * NB:(nb + 1) * NB],
                                  ones_bf[base:base + 64, :],
                                  SQK[base:base + 64, nb * NB:(nb + 1) * NB],
                                  start=True, stop=True,
                                  tile_position=(base, 0))
            s4.inc(mm, 1)
            t.wait_ge(s7.h, b7 + 1)
            for j in range(8):
                if j >= 2:
                    t.wait_ge(s8.h, b8 + j - 1)
                s8.inc(t.matmul(bank(env, j % 2)[:, 0:1], env["ones_row128"][:],
                                negb[0:1, j:j + 1], start=True, stop=True), 1)

        @blk.vector
        def _(v):
            v.wait_ge(s2.h, b2 + 1)
            last = None
            for h2 in (0, 1):
                for w in range(NNB):
                    last = v.tensor_reduce(mq[0:1, 4 * h2 + w:4 * h2 + w + 1],
                                           env["PSA"][h2][0:1, w * NB:(w + 1) * NB],
                                           axis=AX, op=ALU.max)
            p.link(v, s3, last)
            v.wait_ge(s4.h, b4 + 1)
            for h2 in (0, 1):
                for w in range(NNB):
                    last = v.tensor_reduce(mk[0:1, 4 * h2 + w:4 * h2 + w + 1],
                                           env["PSA"][h2][0:1, w * NB:(w + 1) * NB],
                                           axis=AX, op=ALU.max)
            p.chain(v, "vch", last)
            for h2 in (0, 1):
                v.tensor_copy(mkw[0:1, 4 * h2:4 * h2 + 1],
                              mk[0:1, 4 * h2:4 * h2 + 1])
                last = v.tensor_tensor(mkw[0:1, 4 * h2 + 1:4 * h2 + 4],
                                       mk[0:1, 4 * h2:4 * h2 + 3],
                                       mk[0:1, 4 * h2 + 1:4 * h2 + 4], ALU.max)
            p.chain(v, "vch", last)
            i5 = v.tensor_tensor(prod, mq, mkw, ALU.mult)
            p.link(v, s5, i5)
            v.wait_ge(s6.h, b6 + 1)
            s7.inc(v.tensor_scalar(negb, sq, -8.0, 0.0, ALU.mult, ALU.add), 1)



def _l3f_head(p, env, h, QTp, KTp, VT, PB, OTL, BCOL, RROW, AVF):
    nc = p.nc
    ch, base = h // 2, 64 * (h % 2)
    s_smm = p.sem("l3h_smm")
    s_msk = p.sem("l3h_msk")
    s_exp = p.sem("l3h_exp")
    s_av = p.sem("l3h_av")
    s_r = p.sem("l3h_r")
    s_rb = p.sem("l3h_rb")
    s_avc = p.sem("l3h_avc")
    s_don = p.sem("l3h_don")
    b_smm, b_msk, b_exp, b_av, b_r, b_rb, b_avc, b_don = (
        s_smm.n, s_msk.n, s_exp.n, s_av.n, s_r.n, s_rb.n, s_avc.n, s_don.n)
    units = []
    gidx = 0
    for w in range(NNB):
        groups = []
        for g0 in (0, 2, 4, 6):
            rs = [r for r in (g0, g0 + 1) if 4 * w - 4 + r >= 0]
            if rs:
                groups.append((gidx, rs))
                gidx += 1
        units.append((w, groups))

    def s_reg(gi, kn):
        half = gi % 2
        return env["PSA"][0][:, half * 2 * NB:(half * 2 + kn) * NB]

    with nc.Block() as blk:
        @blk.tensor
        def _(t):
            ui = 0
            for w, groups in units:
                for gi, rs in groups:
                    if gi >= 2:
                        t.wait_ge(s_exp.h, b_exp + gi - 1)
                    mm = None
                    for ki, r in enumerate(rs):
                        kt = 4 * w - 4 + r
                        mm = t.matmul(
                            bank(env, (gi % 2) * 2 + ki),
                            KTp[ch][base:base + 64, kt * 128:(kt + 1) * 128],
                            QTp[ch][base:base + 64, w * NB:(w + 1) * NB],
                            start=True, stop=True)
                    s_smm.inc(mm, 1)
                t.wait_ge(s_exp.h, b_exp + groups[-1][0] + 1)
                if ui >= 2:
                    t.wait_ge(s_don.h, b_don + ui - 1)
                mm = None
                kts = [4 * w - 4 + r for r in range(8) if 4 * w - 4 + r >= 0]
                for j, kt in enumerate(kts):
                    r = kt - (4 * w - 4)
                    mm = t.matmul(bank(env, 4 + ui % 2)[0:65, :],
                                  VT[kt][:, 65 * h:65 * h + 65],
                                  PB[:, r * NB:(r + 1) * NB],
                                  start=(j == 0), stop=(j == len(kts) - 1))
                s_av.inc(mm, 1)
                t.wait_ge(s_r.h, b_r + ui + 1)
                s_rb.inc(t.matmul(bank(env, 6)[0:64, :],
                                  env["ones_row128"][:, 0:64], RROW[:],
                                  start=True, stop=True), 1)
                ui += 1

        @blk.vector
        def _(v):
            ui = 0
            for w, groups in units:
                for gi, rs in groups:
                    v.wait_ge(s_smm.h, b_smm + gi + 1)
                    kn = len(rs)
                    reg = s_reg(gi, kn)
                    msrc = (env["scr_f32"] if rs[0] < 4 else env["scr_f32b"])[
                        :, (rs[0] % 4) * NB:((rs[0] % 4) + kn) * NB]
                    s_msk.inc(v.tensor_tensor(reg, reg, msrc, ALU.add), 1)
                v.wait_ge(s_av.h, b_av + ui + 1)
                i1 = v.reciprocal(RROW[:], bank(env, 4 + ui % 2)[64:65, :])
                p.link(v, s_r, i1)
                v.wait_ge(s_rb.h, b_rb + ui + 1)
                v.wait_ge(s_avc.h, b_avc + ui + 1)
                i2 = v.tensor_tensor(OTL[ch][base:base + 64, w * NB:(w + 1) * NB],
                                     AVF[:], bank(env, 6)[0:64, :], ALU.mult)
                p.link(v, s_don, i2)
                ui += 1

        @blk.scalar
        def _(s):
            ui = 0
            for w, groups in units:
                for gi, rs in groups:
                    s.wait_ge(s_msk.h, b_msk + gi + 1)
                    kn = len(rs)
                    reg = s_reg(gi, kn)
                    s_exp.inc(s.activation(PB[:, rs[0] * NB:(rs[0] + kn) * NB],
                                           reg, AF.Exp, scale=8.0,
                                           bias=BCOL[:, 4 * h + w:4 * h + w + 1]), 1)
                s.wait_ge(s_av.h, b_av + ui + 1)
                if ui >= 1:
                    s.wait_ge(s_don.h, b_don + ui)
                s_avc.inc(s.activation(AVF[:], bank(env, 4 + ui % 2)[0:64, :],
                                       AF.Copy), 1)
                ui += 1


def _l3f_mlp(p, env, l, w1, w2, b1a, b1g, b2r):
    nc = p.nc
    HP = 2816
    NT = HP // 128
    with contextlib.ExitStack() as scope:
        sb = lambda nm, sh, dt: scope.enter_context(
            nc.sbuf_tensor(f"l3m{l}" + nm, list(sh), dt))
        HIDn = sb("_hid", [128, NT * NB], BF16)
        STG = sb("_stg", [128, 2 * 2 * 8 * 256], BF16)
        STG2 = sb("_stg2", [128, 11 * 1024], BF16)
        B2R = env["rows"][0:1, N:N + 1024]
        s_b = p.sem("l3m_b")
        b_b = s_b.n
        with nc.Block() as blk:
            @blk.sync
            def _(sy):
                s_b.inc(sy.dma_start(env["wcol"][:, 0:22], b1a[l, :, :]), 16)
                s_b.inc(sy.dma_start(env["wcol"][:, 22:44], b1g[l, :, :]), 16)
                s_b.inc(sy.dma_start(B2R[:], b2r[l:l + 1, :]), 16)
                sy.wait_ge(s_b.h, b_b + 48)
        for nb in range(NNB):
            _l3f_mlp_w1(p, env, l, nb, w1, HIDn, STG)
            for sect in range(2):
                _l3f_mlp_w2(p, env, l, nb, sect, w2, HIDn, STG2,
                            B2R if sect == 0 else None)


def _l3f_mlp_w1(p, env, l, nb, w1, HIDn, STG):
    nc = p.nc
    NT = 22
    s_w = p.sem("l3w1_w")
    s_mm = p.sem("l3w1_mm")
    s_si = p.sem("l3w1_si")
    s_hv = p.sem("l3w1_hv")
    b_w, b_mm, b_si, b_hv = s_w.n, s_mm.n, s_si.n, s_hv.n
    TN = env["TN"]
    GS = 2 * 8 * 256
    with nc.Block() as blk:
        @blk.gpsimd
        def _(g):
            for grp in range(11):
                if grp >= 2:
                    g.wait_ge(s_mm.h, b_mm + 4 * (grp - 1))
                par = (grp % 2) * GS
                for j in range(2):
                    for tp in range(2):
                        ti = grp * 2 + tp
                        src = w1[l][:, j * 2816 + ti * 128:
                                    j * 2816 + (ti + 1) * 128].rearrange(
                            "(kc p) f -> p kc f", p=128)
                        dst = STG[:, par + (j * 2 + tp) * 1024:
                                  par + (j * 2 + tp + 1) * 1024
                                  ].rearrange("p (kc f) -> p kc f", f=128)
                        s_w.inc(g.dma_start(dst, src), 16)

        @blk.tensor
        def _(t):
            for grp in range(11):
                t.wait_ge(s_w.h, b_w + (grp + 1) * 64)
                par = (grp % 2) * GS
                for tp in range(2):
                    ti = grp * 2 + tp
                    u = ti
                    if u >= 2:
                        t.wait_ge(s_hv.h, b_hv + u - 1)
                    for j in range(2):
                        ps = bank(env, 2 * (u % 2) + j)
                        mm = None
                        for ic in range(8):
                            mm = t.matmul(
                                ps, STG[:, par + (j * 2 + tp) * 1024 + ic * 128:
                                        par + (j * 2 + tp) * 1024 + (ic + 1) * 128],
                                TN[ic][:, nb * NB:(nb + 1) * NB],
                                start=(ic == 0), stop=(ic == 7))
                        s_mm.inc(mm, 1)

        @blk.scalar
        def _(s):
            for u in range(NT):
                s.wait_ge(s_mm.h, b_mm + 2 * u + 2)
                if u >= 2:
                    s.wait_ge(s_hv.h, b_hv + u - 1)
                psg = bank(env, 2 * (u % 2) + 1)
                tmp = env["scr_f32"][:, 0:NB].bitcast(BF16)[
                    :, (u % 2) * NB:(u % 2) * NB + NB]
                s_si.inc(s.activation(tmp, psg, AF.Gelu,
                                      bias=env["wcol"][:, 22 + u:23 + u]), 1)

        @blk.vector
        def _(v):
            for u in range(NT):
                v.wait_ge(s_si.h, b_si + u + 1)
                psa = bank(env, 2 * (u % 2))
                tmp = env["scr_f32"][:, 0:NB].bitcast(BF16)[
                    :, (u % 2) * NB:(u % 2) * NB + NB]
                s_hv.inc(v.scalar_tensor_tensor(HIDn[:, u * NB:(u + 1) * NB],
                                                psa, env["wcol"][:, u:u + 1],
                                                tmp, ALU.add, ALU.mult), 1)


def _l3f_mlp_w2(p, env, l, nb, sect, w2, HIDn, STG2, B2R):
    nc = p.nc
    s_w = p.sem("l3w2_w")
    s_mm = p.sem("l3w2_mm")
    s_tm = p.sem("l3w2_tm")
    b_w, b_mm, b_tm = s_w.n, s_mm.n, s_tm.n
    kts = list(range(sect * 11, sect * 11 + 11))
    with nc.Block() as blk:
        @blk.gpsimd
        def _(g):
            src = w2[l][sect * 11 * 128:(sect + 1) * 11 * 128, :].rearrange(
                "(kc p) f -> p kc f", p=128)
            dst = STG2[:].rearrange("p (kc f) -> p kc f", f=1024)
            s_w.inc(g.dma_start(dst, src), 16)

        @blk.tensor
        def _(t):
            t.wait_ge(s_w.h, b_w + 16)
            for dt in range(NCH):
                ps = bank(env, 4 + dt % 2)
                if dt >= 2:
                    t.wait_ge(s_tm.h, b_tm + dt - 1)
                mm = None
                for j, kt in enumerate(kts):
                    mm = t.matmul(ps, STG2[:, j * 1024 + dt * 128:
                                           j * 1024 + (dt + 1) * 128],
                                  HIDn[:, kt * NB:(kt + 1) * NB],
                                  start=(j == 0),
                                  stop=(j == 10 and B2R is None))
                if B2R is not None:
                    mm = t.matmul(ps, B2R[:, dt * 128:(dt + 1) * 128],
                                  env["ones_row512"][:],
                                  start=False, stop=True)
                s_mm.inc(mm, 1)

        @blk.vector
        def _(v):
            for dt in range(NCH):
                v.wait_ge(s_mm.h, b_mm + dt + 1)
                ps = bank(env, 4 + dt % 2)
                sl = slice(nb * NB, (nb + 1) * NB)
                s_tm.inc(v.tensor_tensor(env["X"][dt][:, sl],
                                         env["X"][dt][:, sl], ps, ALU.add), 1)




def emit_block_l3_attn(p, env):
    nc = p.nc
    emit_norm(p, env, kind="ln")
    TN = env["TN"]
    w_in = p.dram_in("w_l3a_in", (1024, 4096))
    w_x = p.dram_in("w_l3a_x", (2048, 80))
    w_out = p.dram_in("w_l3a_o", (2048, 1024))
    conv_w = p.dram_in("w_l3a_conv", (16, 128, 4))
    conv_b = p.dram_in("b_l3a_conv", (128, 16))
    alog = p.dram_in("w_l3a_alog", (128, 256))
    sel_ap = p.dram_in("c_l3a_sel", (16 * 128, 128))
    gath_ap = p.dram_in("c_l3a_gath", (16 * 128, 128))
    rep_ap = p.dram_in("c_l3a_rep16", (16, 128))
    xc_sv = nc.dram_tensor("xc_save", [16, 128, N], BF16, kind="Internal").ap()
    z_sv = nc.dram_tensor("z_save", [16, 128, N], BF16, kind="Internal").ap()
    y_sv = nc.dram_tensor("y_save", [16, 128, N], BF16, kind="Internal").ap()

    with contextlib.ExitStack() as scope:
        sb = lambda nm, sh, dt: scope.enter_context(
            nc.sbuf_tensor("l3a" + nm, list(sh), dt))
        with contextlib.ExitStack() as sc1:
            s1 = lambda nm, sh, dt: sc1.enter_context(
                nc.sbuf_tensor("l3ab" + nm, list(sh), dt))
            XP = [s1(f"_xp{i}", [128, N + 3], BF16) for i in range(4)]
            CV = s1("_cv", [128, N], F32)
            XCC = s1("_xcc", [128, N], BF16)
            WSP = s1("_wsp", [128, 4096], BF16)
            WXS = s1("_wxs", [128, 16 * 80], BF16)
            _l3a_xz(p, env, w_in, w_x, conv_w, conv_b, TN, XP, CV, XCC,
                    WSP, WXS, xc_sv, z_sv)
        with contextlib.ExitStack() as sc2:
            s2 = lambda nm, sh, dt: sc2.enter_context(
                nc.sbuf_tensor("l3ae" + nm, list(sh), dt))
            DBREP = s2("_dbrep", [128, N], BF16)
            CREP = s2("_crep", [128, N], BF16)
            DREP = s2("_drep", [128, N], F32)
            A_SB = s2("_a", [128, 256], F32)
            SELS = s2("_sels", [128, 16 * 128], BF16)
            GATHS = s2("_gaths", [128, 16 * 128], BF16)
            REP16 = s2("_rep", [16, 128], BF16)
            _l3a_scanprep(p, env, alog, rep_ap, sel_ap, gath_ap, DBREP, CREP,
                          DREP, A_SB, SELS, GATHS, REP16)
            XCI = [s2(f"_xci{i}", [128, N], BF16) for i in range(2)]
            UEV = s2("_uev", [128, N], BF16)
            ZIN = s2("_zin", [128, N], BF16)
            SZ = s2("_sz", [128, N], BF16)
            YC = s2("_yc", [128, N], BF16)
            DA = env["scr_f32"][:]
            G = env["scr_f32b"][:]
            H = s2("_h", [128, N], F32)
            HCB = s2("_hcb", [128, N], BF16)
            for dt in range(16):
                _l3a_scan_dt(p, env, dt, xc_sv, z_sv, y_sv, XCI, ZIN, SZ, YC,
                             DA, G, H, HCB, UEV, DBREP, CREP, DREP, A_SB,
                             SELS, GATHS)
        with contextlib.ExitStack() as sc3:
            s3 = lambda nm, sh, dt: sc3.enter_context(
                nc.sbuf_tensor("l3af" + nm, list(sh), dt))
            YIN = [s3(f"_yin{i}", [128, NB], BF16) for i in range(16)]
            WSO = s3("_wso", [128, 16384], BF16)
            _l3a_wout(p, env, w_out, y_sv, YIN, WSO)


def _l3a_xz(p, env, w_in, w_x, conv_w, conv_b, TN, XP, CV, XCC, WSP, WXS,
            xc_sv, z_sv):
    nc = p.nc
    s_wx = p.sem("l3x_wx")
    b_wx = s_wx.n
    with nc.Block() as blk:
        @blk.gpsimd
        def _(g):
            src = w_x.rearrange("(kc p) f -> p kc f", p=128)
            s_wx.inc(g.dma_start(
                WXS[:].rearrange("p (kc f) -> p kc f", f=80), src), 16)
            g.wait_ge(s_wx.h, b_wx + 16)

    for piece in range(8):
        is_xp = piece < 4
        s_w = p.sem("l3x_w")
        s_mm = p.sem("l3x_mm")
        s_ev = p.sem("l3x_ev")
        s_cv = p.sem("l3x_cv")
        s_si = p.sem("l3x_si")
        s_sp = p.sem("dma_gp")
        s_xmm = p.sem("l3x_xmm")
        b_w, b_mm, b_ev, b_cv, b_si, b_sp, b_xmm = (
            s_w.n, s_mm.n, s_ev.n, s_cv.n, s_si.n, s_sp.n, s_xmm.n)
        with nc.Block() as blk:
            @blk.gpsimd
            def _(g):
                src = w_in[:, piece * 512:(piece + 1) * 512].rearrange(
                    "(kc p) f -> p kc f", p=128)
                s_w.inc(g.dma_start(
                    WSP[:].rearrange("p (kc f) -> p kc f", f=512), src), 16)
                for ci in range(4):
                    ch = piece * 4 + ci
                    if is_xp:
                        g.wait_ge(s_si.h, b_si + ci + 1)
                        s_sp.inc(g.dma_start(xc_sv[ch, :, :], XCC[:]), 16)
                        g.wait_ge(s_sp.h, s_sp.n)
                    else:
                        g.wait_ge(s_ev.h, b_ev + (ci + 1) * 4)
                        s_sp.inc(g.dma_start(z_sv[ch - 16, :, :],
                                             XP[ci][:, 3:N + 3]), 16)
                if not is_xp:
                    g.wait_ge(s_sp.h, s_sp.n)

            @blk.sync
            def _(sy):
                if is_xp:
                    for ci in range(4):
                        s_w.inc(sy.dma_start(env["wcol"][:, 40 + ci:41 + ci],
                                             conv_b[:, piece * 4 + ci:
                                                    piece * 4 + ci + 1],
                                             allow_slow_non_contiguous=True), 16)
                        s_w.inc(sy.dma_start(
                            env["wcol"][:, 44 + 4 * ci:48 + 4 * ci],
                            conv_w[piece * 4 + ci, :, :]), 16)

            @blk.tensor
            def _(t):
                t.wait_ge(s_w.h, b_w + (16 * 9 if is_xp else 16))
                idx = 0
                for ft in range(4):
                    for nb in range(NNB):
                        ps = bank(env, idx % 4)
                        if idx >= 4:
                            t.wait_ge(s_ev.h, b_ev + idx - 3)
                        mm = None
                        for ic in range(8):
                            mm = t.matmul(ps, WSP[:, ic * 512 + ft * 128:
                                                  ic * 512 + (ft + 1) * 128],
                                          TN[ic][:, nb * NB:(nb + 1) * NB],
                                          start=(ic == 0), stop=(ic == 7))
                        s_mm.inc(mm, 1)
                        idx += 1
                if is_xp:
                    for ci in range(4):
                        ch = piece * 4 + ci
                        t.wait_ge(s_si.h, b_si + ci + 1)
                        mm = None
                        for nb in range(NNB):
                            mm = t.matmul(env["PSA"][1][0:80, nb * NB:(nb + 1) * NB],
                                          WXS[:, ch * 80:(ch + 1) * 80],
                                          XCC[:, nb * NB:(nb + 1) * NB],
                                          start=(ch == 0), stop=(ch == 15))
                        s_xmm.inc(mm, 1)

            @blk.scalar
            def _(s):
                idx = 0
                for ft in range(4):
                    for nb in range(NNB):
                        s.wait_ge(s_mm.h, b_mm + idx + 1)
                        ps = bank(env, idx % 4)
                        if is_xp:
                            dst = XP[ft][:, 3 + nb * NB:3 + (nb + 1) * NB]
                        else:
                            dst = XP[ft][:, 3 + nb * NB:3 + (nb + 1) * NB]
                        s_ev.inc(s.activation(dst, ps, AF.Copy), 1)
                        idx += 1
                if is_xp:
                    for ci in range(4):
                        s.wait_ge(s_cv.h, b_cv + ci + 1)
                        if ci >= 1:
                            s.wait_ge(s_xmm.h, b_xmm + ci)
                            s.wait_ge(s_sp.h, b_sp + 16 * ci)
                        s_si.inc(s.activation(XCC[:], CV[:], AF.Silu,
                                              bias=env["wcol"][:, 40 + ci:41 + ci]), 1)

            @blk.vector
            def _(v):
                if is_xp:
                    for ci in range(4):
                        v.wait_ge(s_ev.h, b_ev + (ci + 1) * 4)
                        if ci == 0:
                            v.memset(XP[0][:, 0:3], 0.0)
                            v.memset(XP[1][:, 0:3], 0.0)
                            v.memset(XP[2][:, 0:3], 0.0)
                            v.memset(XP[3][:, 0:3], 0.0)
                        if ci >= 1:
                            v.wait_ge(s_si.h, b_si + ci)
                        i1 = v.tensor_scalar(CV[:], XP[ci][:, 0:N],
                                             env["wcol"][:, 44 + 4 * ci:45 + 4 * ci],
                                             0.0, ALU.mult, ALU.add)
                        p.chain(v, "vch", i1)
                        for j in (1, 2, 3):
                            ij = v.scalar_tensor_tensor(
                                CV[:], XP[ci][:, j:j + N],
                                env["wcol"][:, 44 + 4 * ci + j:45 + 4 * ci + j],
                                CV[:], ALU.mult, ALU.add)
                            p.chain(v, "vch", ij)
                        s_cv.inc(v.nop(), 1)


def _l3a_scanprep(p, env, alog, rep_ap, sel_ap, gath_ap, DBREP, CREP, DREP,
                  A_SB, SELS, GATHS, REP16):
    nc = p.nc
    rws = env["rows"]
    drow = rws[0:1, 0:N]
    arow = rws[0:1, N:2 * N]
    erow = rws[0:1, N:2 * N]
    s_l = p.sem("l3p_l")
    s_d1 = p.sem("l3p_d1")
    s_d2 = p.sem("l3p_d2")
    s_d3 = p.sem("l3p_d3")
    s_bc = p.sem("l3p_bc")
    s_db = p.sem("l3p_db")
    s_rp = p.sem("l3p_rp")
    s_cp = p.sem("l3p_cp")
    s_ae = p.sem("l3p_ae")
    b_l, b_d1, b_d2, b_d3, b_bc, b_db, b_rp, b_cp, b_ae = (
        s_l.n, s_d1.n, s_d2.n, s_d3.n, s_bc.n, s_db.n, s_rp.n, s_cp.n, s_ae.n)
    BCF = env["scr_f32"][0:16, 0:N]
    DBF = env["scr_f32b"][0:16, 0:N].bitcast(BF16)[0:16, 0:N]
    CCF = env["scr_f32b"][0:16, N // 2:N].bitcast(BF16)[0:16, 0:N]
    REP = REP16
    ps_x = env["PSA"][1]
    s_dd = p.sem("l3p_dd")
    b_dd = s_dd.n

    with nc.Block() as blk:
        @blk.gpsimd
        def _(g):
            s_l.inc(g.dma_start(A_SB[:], alog[:, :]), 16)
            s_l.inc(g.dma_start(REP[:], rep_ap[:, :]), 16)
            s_l.inc(g.dma_start(
                SELS[:].rearrange("p (s q) -> p s q", s=16),
                sel_ap.rearrange("(s p) q -> p s q", p=128)), 16)
            s_l.inc(g.dma_start(
                GATHS[:].rearrange("p (s q) -> p s q", s=16),
                gath_ap.rearrange("(s p) q -> p s q", p=128)), 16)

        @blk.scalar
        def _(s):
            p.chain(s, "sch", s.activation(arow, ps_x[0:1, :], AF.Abs))
            p.chain(s, "sch", s.activation(erow, arow, AF.Exp, scale=-1.0))
            p.chain(s, "sch", s.activation(erow, erow, AF.Ln,
                                           bias=ccol(env, "one")[0:1, :]))
            i4 = s.activation(drow, ps_x[0:1, :], AF.Relu)
            p.link(s, s_d1, i4)
            s_d2.inc(s.activation(BCF[:], ps_x[32:48, :], AF.Copy), 1)
            s_d2.inc(s.activation(CCF[:], ps_x[64:80, :], AF.Copy), 1)
            s.wait_ge(s_l.h, b_l + 64)
            i5 = s.activation(A_SB[:], A_SB[:], AF.Exp)
            p.link(s, s_ae, i5)
            for nb in range(NNB):
                s.wait_ge(s_bc.h, b_bc + nb + 1)
                s_cp.inc(s.activation(DREP[:, nb * NB:(nb + 1) * NB],
                                      bank(env, nb % 2), AF.Copy), 1)
            for j in range(8):
                s.wait_ge(s_rp.h, b_rp + j + 1)
                dst = (DBREP if j < 4 else CREP)[:, (j % 4) * NB:(j % 4 + 1) * NB]
                s_cp.inc(s.activation(dst, bank(env, 2 + j % 2), AF.Copy), 1)

        @blk.vector
        def _(v):
            v.wait_ge(s_d1.h, b_d1 + 1)
            i1 = v.tensor_tensor(drow, drow, erow, ALU.add)
            p.link(v, s_dd, i1)
            v.wait_ge(s_d2.h, b_d2 + 1)
            i = None
            for nb in range(NNB):
                v.wait_ge(s_db.h, b_db + nb + 1)
                i = v.tensor_tensor(DBF[:, nb * NB:(nb + 1) * NB],
                                    BCF[:, nb * NB:(nb + 1) * NB],
                                    bank(env, 4 + nb % 2)[0:16, :], ALU.mult)
                s_d3.inc(i, 1)
            v.wait_ge(s_d3.h, s_d3.n)

        @blk.tensor
        def _(t):
            t.wait_ge(s_dd.h, b_dd + 1)
            for nb in range(NNB):
                if nb >= 2:
                    t.wait_ge(s_cp.h, b_cp + nb - 1)
                s_bc.inc(t.matmul(bank(env, nb % 2), env["ones_row128"][:],
                                  drow[:, nb * NB:(nb + 1) * NB],
                                  start=True, stop=True), 1)
            t.wait_ge(s_d2.h, b_d2 + 2)
            t.wait_ge(s_l.h, b_l + 64)
            for nb in range(NNB):
                if nb >= 2:
                    t.wait_ge(s_d3.h, b_d3 + nb - 1)
                s_db.inc(t.matmul(bank(env, 4 + nb % 2)[0:16, :],
                                  env["ones_row128"][:, 0:16],
                                  drow[:, nb * NB:(nb + 1) * NB],
                                  start=True, stop=True), 1)
            for j in range(8):
                srep = DBF if j < 4 else CCF
                nb = j % 4
                t.wait_ge((s_d3 if j < 4 else s_d2).h,
                          (b_d3 + nb + 1) if j < 4 else (b_d2 + 2))
                if j >= 2:
                    t.wait_ge(s_cp.h, b_cp + 4 + j - 1)
                s_rp.inc(t.matmul(bank(env, 2 + j % 2), REP[:],
                                  srep[:, nb * NB:(nb + 1) * NB],
                                  start=True, stop=True), 1)

    s_ng = p.sem("l3p_ng")
    with nc.Block() as blk:
        @blk.vector
        def _(v):
            v.tensor_scalar(A_SB[:], A_SB[:], -1.0, 0.0, ALU.mult, ALU.add)


def _l3a_scan_dt(p, env, dt, xc_sv, z_sv, y_sv, XCI, ZIN, SZ, YC, DA, G, H,
                 HCB, UEV, DBREP, CREP, DREP, A_SB, SELS, GATHS):
    nc = p.nc
    s_xc = p.sem("l3s_xc")
    s_z = p.sem("l3s_z")
    s_us = p.sem("l3s_us")
    s_ue = p.sem("l3s_ue")
    s_g = p.sem("l3s_g")
    s_da = p.sem("l3s_da")
    s_sc = p.sem("l3s_sc")
    s_hc = p.sem("l3s_hc")
    s_ym = p.sem("l3s_ym")
    s_si = p.sem("l3s_si")
    s_ye = p.sem("l3s_ye")
    s_yo = p.sem("dma_sync")
    b_xc, b_z, b_us, b_ue, b_g, b_da, b_sc, b_hc, b_ym, b_si, b_ye, b_yo = (
        s_xc.n, s_z.n, s_us.n, s_ue.n, s_g.n, s_da.n, s_sc.n, s_hc.n, s_ym.n,
        s_si.n, s_ye.n, s_yo.n)
    buf = XCI[dt % 2]
    with nc.Block() as blk:
        @blk.gpsimd
        def _(g_):
            if dt == 0:
                s_xc.inc(g_.dma_start(XCI[0][:], xc_sv[0, :, :]), 16)
            if dt < 15:
                s_xc.inc(g_.dma_start(XCI[(dt + 1) % 2][:], xc_sv[dt + 1, :, :]), 16)
            s_z.inc(g_.dma_start(ZIN[:], z_sv[dt, :, :]), 16)
            for j in range(16):
                g_.wait_ge(s_ue.h, b_ue + j + 1)
                if j >= 1:
                    g_.wait_ge(s_sc.h, b_sc + j)
                s_g.inc(g_.tensor_tensor(G[:], UEV[:], DBREP[:], ALU.mult), 1)

        @blk.tensor
        def _(t):
            need = 16 if dt == 0 else 0
            t.wait_ge(s_xc.h, b_xc + need)
            for j in range(16):
                ct = dt * 16 + j
                if j >= 1:
                    t.wait_ge(s_g.h, b_g + j)
                mm = None
                for nb in range(NNB):
                    mm = t.matmul(bank(env, nb), SELS[:, (j) * 128:(j + 1) * 128],
                                  buf[:, nb * NB:(nb + 1) * NB],
                                  start=True, stop=True)
                s_us.inc(mm, 1)
                t.wait_ge(s_hc.h, b_hc + j + 1)
                mm = None
                for nb in range(NNB):
                    mm = t.matmul(bank(env, 4 + nb)[0:128, :],
                                  GATHS[:, j * 128:(j + 1) * 128],
                                  HCB[:, nb * NB:(nb + 1) * NB],
                                  start=(j == 0), stop=(j == 15))
                s_ym.inc(mm, 1)

        @blk.scalar
        def _(s):
            for j in range(16):
                ct = dt * 16 + j
                if j >= 1:
                    s.wait_ge(s_sc.h, b_sc + j)
                s_da.inc(s.activation(DA[:], DREP[:], AF.Exp,
                                      scale=A_SB[:, ct:ct + 1]), 1)
                s.wait_ge(s_us.h, b_us + j + 1)
                if j >= 1:
                    s.wait_ge(s_g.h, b_g + j)
                s_ue.inc(s.activation(UEV[:], env["PSA"][0][:, :], AF.Copy), 1)
            s.wait_ge(s_z.h, b_z + 16)
            s_si.inc(s.activation(SZ[:], ZIN[:], AF.Silu), 1)

        @blk.vector
        def _(v):
            for j in range(16):
                v.wait_ge(s_da.h, b_da + j + 1)
                v.wait_ge(s_g.h, b_g + j + 1)
                i2 = v.tensor_tensor_scan(H[:], DA[:], G[:], 0.0,
                                          ALU.mult, ALU.add)
                p.link(v, s_sc, i2)
                if j >= 1:
                    v.wait_ge(s_ym.h, b_ym + j)
                i3 = v.tensor_tensor(HCB[:], H[:], CREP[:], ALU.mult)
                p.link(v, s_hc, i3)
            v.wait_ge(s_ym.h, b_ym + 16)
            v.wait_ge(s_si.h, b_si + 1)
            last = None
            for nb in range(NNB):
                last = v.tensor_tensor(YC[:, nb * NB:(nb + 1) * NB],
                                       bank(env, 4 + nb),
                                       SZ[:, nb * NB:(nb + 1) * NB], ALU.mult)
            p.link(v, s_ye, last)

        @blk.sync
        def _(sy):
            sy.wait_ge(s_ye.h, b_ye + 1)
            s_yo.inc(sy.dma_start(y_sv[dt, :, :], YC[:]), 16)
            sy.wait_ge(s_yo.h, b_yo + 16)


def _l3a_wout(p, env, w_out, y_sv, YIN, WSO):
    nc = p.nc
    s_w = p.sem("l3o_w")
    s_y = p.sem("l3o_y")
    s_mm = p.sem("l3o_mm")
    s_ev = p.sem("l3o_ev")
    b_w, b_y, b_mm, b_ev = s_w.n, s_y.n, s_mm.n, s_ev.n
    with nc.Block() as blk:
        @blk.gpsimd
        def _(g):
            src = w_out.rearrange("(kc p) f -> p kc f", p=128)
            s_w.inc(g.dma_start(
                WSO[:].rearrange("p (kc f) -> p kc f", f=1024), src), 16)

        @blk.sync
        def _(sy):
            for nb in range(NNB):
                if nb >= 1:
                    sy.wait_ge(s_mm.h, b_mm + nb * NCH)
                for ic in range(16):
                    s_y.inc(sy.dma_start(YIN[ic][:],
                                         y_sv[ic, :, nb * NB:(nb + 1) * NB]), 16)

        @blk.tensor
        def _(t):
            t.wait_ge(s_w.h, b_w + 16)
            for nb in range(NNB):
                t.wait_ge(s_y.h, b_y + (nb + 1) * 256)
                for ft in range(NCH):
                    idx = nb * NCH + ft
                    ps = bank(env, idx % 4)
                    if idx >= 4:
                        t.wait_ge(s_ev.h, b_ev + idx - 3)
                    mm = None
                    for ic in range(16):
                        mm = t.matmul(ps, WSO[:, ic * 1024 + ft * 128:
                                              ic * 1024 + (ft + 1) * 128],
                                      YIN[ic][:],
                                      start=(ic == 0), stop=(ic == 15))
                    s_mm.inc(mm, 1)

        @blk.vector
        def _(v):
            for nb in range(NNB):
                for ft in range(NCH):
                    idx = nb * NCH + ft
                    v.wait_ge(s_mm.h, b_mm + idx + 1)
                    ps = bank(env, idx % 4)
                    sl = slice(nb * NB, (nb + 1) * NB)
                    s_ev.inc(v.tensor_tensor(env["X"][ft][:, sl],
                                             env["X"][ft][:, sl], ps,
                                             ALU.add), 1)




def build_program():
    p = Prog()
    env = make_env(p)
    emit_init(p, env)
    emit_block_l0_attn(p, env)
    emit_block_l0_ffn(p, env)
    emit_block_l1_attn(p, env)
    emit_block_l1_ffn(p, env)
    emit_block_l2_attn(p, env)
    emit_block_l2_ffn(p, env)
    emit_block_l3_attn(p, env)
    emit_block_l3_ffn(p, env)
    out = p.dram_out("x_out", (1024, 2048))
    emit_store_x(p, env, out)
    return p


def kernel(**inputs):
    shared = prep_shared(inputs)
    p = build_program()
    needed = set(p.inputs.keys())
    base = {k: v for k, v in shared.items() if k in needed}
    miss = needed - set(base) - {"x_t"}
    assert not miss, f"missing host arrays for {sorted(miss)}"
    in_maps = []
    for b in range(8):
        m = dict(base)
        m.update(prep_core(inputs, b))
        in_maps.append(m)
    res = run_bass_kernel_spmd(p.nc, in_maps, core_ids=list(range(8)))
    p.close()
    return np.ascontiguousarray(
        np.stack([r["x_out"].T for r in res.results])).astype(np.float32)


# version 05
# speedup vs baseline: 1.1152x; 1.0285x over previous
import contextlib

import numpy as np

import concourse.bass as bass
import concourse.mybir as mybir
from concourse.bass_utils import run_bass_kernel_spmd
from concourse.alu_op_type import AluOpType as ALU

F32 = mybir.dt.float32
BF16 = mybir.dt.bfloat16
AF = mybir.ActivationFunctionType
AX = mybir.AxisListType.X

D = 1024
N = 2048
NCH = D // 128
NB = 512
NNB = N // NB
NEG = -30000.0



def _ct(a):
    return np.ascontiguousarray(a, dtype=np.float32)


def _cols(v, p=128):
    v = np.asarray(v)
    k = v.shape[-1]
    return _ct(v.reshape(k // p, p).T)


def prep_shared(inp):
    d = {}
    nn = np.arange(N, dtype=np.float64)

    d["c_identity"] = np.eye(128, dtype=np.float32)
    psw = np.zeros((128, 128), np.float32)
    for i in range(64):
        psw[2 * i, 2 * i + 1] = 1.0
        psw[2 * i + 1, 2 * i] = 1.0
    d["c_pairswap"] = psw
    p64 = np.zeros((128, 128), np.float32)
    for h in range(2):
        b = 64 * h
        for i in range(32):
            p64[b + i, b + 32 + i] = 1.0
            p64[b + 32 + i, b + i] = 1.0
    d["c_swap32"] = p64
    cm = np.zeros((4, 128, 512), np.float32)
    ki = np.arange(128)[:, None]
    qi = np.arange(512)[None, :]
    for r in range(4):
        cm[r] = np.where(r * 128 + ki > qi, NEG, 0.0)
    d["c_causalmask"] = cm.reshape(4 * 128, 512)
    hs2 = np.zeros((128, 65), np.float32)
    hs2[0:64, 0] = 1.0
    hs2[64:128, 64] = 1.0
    d["c_headsel2"] = hs2
    ohbc = np.zeros((16, 16 * 128), np.float32)
    for r in range(16):
        ohbc[r, r * 128:(r + 1) * 128] = 1.0
    d["c_ohbc"] = ohbc

    d["w_l0a_dq"] = inp["L0_attn_Wd_q_w"]
    d["b_l0a_dq"] = _cols(inp["L0_attn_Wd_q_b"])
    d["w_l0a_uq"] = inp["L0_attn_Wu_q_w"]
    d["b_l0a_uq"] = _cols(inp["L0_attn_Wu_q_b"])
    d["w_l0a_qr"] = inp["L0_attn_Wq_r_w"]
    d["b_l0a_qr"] = _cols(inp["L0_attn_Wq_r_b"])
    d["w_l0a_kr"] = inp["L0_attn_Wk_r_w"]
    d["b_l0a_kr"] = _cols(inp["L0_attn_Wk_r_b"])
    d["w_l0a_dkv"] = inp["L0_attn_Wd_kv_w"]
    d["b_l0a_dkv"] = _cols(inp["L0_attn_Wd_kv_b"])
    d["w_l0a_uk"] = inp["L0_attn_Wu_k_w"]
    d["b_l0a_uk"] = _cols(inp["L0_attn_Wu_k_b"])
    d["w_l0a_uv"] = inp["L0_attn_Wu_v_w"]
    d["b_l0a_uv"] = _cols(inp["L0_attn_Wu_v_b"])
    d["br_l0a_uv"] = _ct(np.asarray(inp["L0_attn_Wu_v_b"]).reshape(1, 1024))
    d["w_l0a_o"] = inp["L0_attn_Wo_w"]
    d["b_l0a_o"] = _cols(inp["L0_attn_Wo_b"])
    d["br_l0a_o"] = _ct(np.asarray(inp["L0_attn_Wo_b"]).reshape(1, 1024))
    ii = np.arange(512, dtype=np.float64)
    theta = (10000.0 ** (-2.0 * ii)) / 512.0
    ang = nn[None, :] * theta[:, None]
    c, s = np.cos(ang), np.sin(ang)
    C = np.zeros((1024, N), np.float32)
    S = np.zeros((1024, N), np.float32)
    C[0::2] = c
    C[1::2] = c
    S[0::2] = -s
    S[1::2] = s
    d["c_l0a_cos"] = C
    d["c_l0a_sin"] = S

    d["w_l0f_anorm"] = _cols(inp["L0_ffn_attn_norm_w"])
    d["w_l0f_mnorm"] = _cols(inp["L0_ffn_moe_norm_w"])
    d["w_l0f_q"] = _ct(np.asarray(inp["L0_ffn_wq"]).T)
    d["w_l0f_k"] = _ct(np.asarray(inp["L0_ffn_wk"]).T)
    d["w_l0f_v"] = _ct(np.asarray(inp["L0_ffn_wv"]).T)
    d["w_l0f_o"] = _ct(np.asarray(inp["L0_ffn_wo"]).T)
    d["w_l0f_gate"] = _ct(np.asarray(inp["L0_ffn_gate_w"]).T)
    d["w_l0f_w1"] = _ct(np.asarray(inp["L0_ffn_w1"]).transpose(0, 2, 1))
    d["w_l0f_w2"] = _ct(np.asarray(inp["L0_ffn_w2"]).transpose(0, 2, 1))
    d["b_l0f_b1"] = np.stack([_cols(b) for b in inp["L0_ffn_b1"]])
    d["b2row_l0f"] = np.stack([_cols(b) for b in inp["L0_ffn_b2"]])
    invf = 1.0 / (10000.0 ** (np.arange(0, 64, 2, dtype=np.float64) / 64.0))
    ang = nn[None, :] * invf[:, None]
    c32, s32 = np.cos(ang), np.sin(ang)
    c64 = np.concatenate([c32, c32], 0)
    s64 = np.concatenate([s32, s32], 0)
    sgn = np.concatenate([-np.ones((32, 1)), np.ones((32, 1))], 0)
    d["c_rh_cos"] = _ct(np.tile(c64, (2, 1)))
    d["c_rh_sin"] = _ct(np.tile(s64 * sgn, (2, 1)))

    qp = np.asarray(inp["L1_attn_query_proj"])
    kp = np.asarray(inp["L1_attn_key_proj"])
    vp = np.asarray(inp["L1_attn_value_proj"])
    op = np.asarray(inp["L1_attn_output_proj"])
    wa = np.zeros((1024, 512), np.float32)
    wb = np.zeros((1024, 512), np.float32)
    wa_core = qp.transpose(2, 1, 0)
    wb_core = kp.transpose(2, 0, 1)
    for h in range(16):
        wa[:, 32 * h:32 * h + 16] = wa_core[:, h, :]
        wb[:, 32 * h:32 * h + 16] = wb_core[:, h, :]
    d["w_l1a_a"] = wa
    d["w_l1a_b"] = wb
    d["w_l1a_v"] = _ct(vp.transpose(2, 0, 1).reshape(1024, 1024))
    d["w_l1a_o"] = _ct(op.transpose(1, 2, 0).reshape(1024, 1024))

    d["w_l1f_r"] = inp["L1_ffn_Wr"]
    d["w_l1f_g"] = inp["L1_ffn_Wg"]
    d["w_l1f_u"] = inp["L1_ffn_Wu"]
    d["w_l1f_d"] = inp["L1_ffn_Wd"]
    d["w_l1f_sg"] = inp["L1_ffn_Sg"]
    d["w_l1f_su"] = inp["L1_ffn_Su"]
    d["w_l1f_sd"] = inp["L1_ffn_Sd"]

    d["w_l2a_qkv"] = inp["L2_attn_W_qkv"]
    d["w_l2a_o"] = inp["L2_attn_W_out"]
    fr = 1.0 / (10000.0 ** (np.arange(0, 64, 2, dtype=np.float64) / 64.0))
    ang = nn[None, :] * fr[:, None]
    c2, s2 = np.cos(ang), np.sin(ang)
    C2 = np.zeros((64, N), np.float32)
    S2 = np.zeros((64, N), np.float32)
    C2[0::2] = c2
    C2[1::2] = c2
    S2[0::2] = -s2
    S2[1::2] = s2
    d["c_l2a_cos"] = _ct(np.tile(C2, (2, 1)))
    d["c_l2a_sin"] = _ct(np.tile(S2, (2, 1)))

    d["w_l2f_w1"] = inp["L2_ffn_W1"]
    d["w_l2f_w2"] = inp["L2_ffn_W2"]
    d["b_l2f_b1"] = np.stack([_cols(b) for b in inp["L2_ffn_b1"]])
    d["b2row_l2f"] = np.stack([_cols(b) for b in inp["L2_ffn_b2"]])
    d["w_l2f_r"] = inp["L2_ffn_Wr"]
    d["b_l2f_r"] = np.asarray(inp["L2_ffn_br"]).reshape(1, 8)

    d["w_l3a_in"] = _ct(np.asarray(inp["L3_attn_W_in"]).T)
    d["w_l3a_conv"] = _ct(np.asarray(inp["L3_attn_conv_w"])[:, 0, :].reshape(16, 128, 4))
    d["b_l3a_conv"] = _cols(inp["L3_attn_conv_b"])
    d["w_l3a_alog"] = _ct(np.asarray(inp["L3_attn_A_log"]).reshape(32768).reshape(256, 128).T)
    wx = np.asarray(inp["L3_attn_W_x"]).T
    wxp = np.zeros((2048, 80), np.float32)
    wxp[:, 0] = wx[:, 0]
    wxp[:, 32:48] = wx[:, 1:17]
    wxp[:, 64:80] = wx[:, 17:33]
    d["w_l3a_x"] = wxp
    d["w_l3a_o"] = _ct(np.asarray(inp["L3_attn_W_out"]).T)
    sel = np.zeros((16, 128, 128), np.float32)
    gath = np.zeros((16, 128, 128), np.float32)
    for o in range(16):
        for pp in range(128):
            sel[o, o * 8 + pp // 16, pp] = 1.0
            gath[o, pp, o * 8 + pp // 16] = 1.0
    d["c_l3a_sel"] = sel.reshape(16 * 128, 128)
    d["c_l3a_gath"] = gath.reshape(16 * 128, 128)
    rep = np.zeros((16, 128), np.float32)
    for pp in range(128):
        rep[pp % 16, pp] = 1.0
    d["c_l3a_rep16"] = rep

    d["w_l3f_qkv"] = inp["L3_ffn_wqkv"]
    d["w_l3f_o"] = inp["L3_ffn_wo"]
    w1 = np.asarray(inp["L3_ffn_w1"])
    b1 = np.asarray(inp["L3_ffn_b1"])
    w2 = np.asarray(inp["L3_ffn_w2"])
    HP = 2816
    w1p = np.zeros((2, 1024, 2 * HP), np.float32)
    w1p[:, :, 0:2730] = w1[:, :, 0:2730]
    w1p[:, :, HP:HP + 2730] = w1[:, :, 2730:5460]
    d["w_l3f_w1"] = w1p
    b1a = np.zeros((2, HP), np.float32)
    b1g = np.zeros((2, HP), np.float32)
    b1a[:, 0:2730] = b1[:, 0:2730]
    b1g[:, 0:2730] = b1[:, 2730:5460]
    d["b_l3f_b1a"] = np.stack([_cols(b) for b in b1a])
    d["b_l3f_b1g"] = np.stack([_cols(b) for b in b1g])
    w2p = np.zeros((2, HP, 1024), np.float32)
    w2p[:, 0:2730, :] = w2
    d["w_l3f_w2"] = w2p
    d["b2row_l3f"] = _ct(np.asarray(inp["L3_ffn_b2"]))
    d["w_l3f_qs"] = _ct(np.tile(np.asarray(inp["L3_ffn_q_scale"]), (1, 2)).reshape(2, 128, 1))
    d["w_l3f_ks"] = _ct(np.tile(np.asarray(inp["L3_ffn_k_scale"]), (1, 2)).reshape(2, 128, 1))
    d["w_l3f_lng"] = np.stack([_cols(g) for g in inp["L3_ffn_ln_g"]])
    d["w_l3f_lnb"] = np.stack([_cols(b) for b in inp["L3_ffn_ln_b"]])
    DH = 64
    half = np.arange(0, DH, 2, dtype=np.float64)
    invf = 1.0 / (10000.0 ** (half / DH))
    frq = nn[:, None] * invf[None, :]
    base = (half + 0.4 * DH) / (1.4 * DH)
    power = (nn - N // 2) / (512 // 2)
    sc = base[None, :] ** power[:, None]
    cosf, sinf = np.cos(frq), np.sin(frq)
    cos64 = np.concatenate([cosf, cosf], 1).T
    sin64 = np.concatenate([sinf, sinf], 1).T
    sc64 = np.concatenate([sc, sc], 1).T
    sgn64 = np.concatenate([-np.ones((32, 1)), np.ones((32, 1))], 0)
    d["c_l3f_cq"] = _ct(np.tile(cos64 * sc64, (2, 1)))
    d["c_l3f_sq"] = _ct(np.tile(sin64 * sc64 * sgn64, (2, 1)))
    d["c_l3f_ck"] = _ct(np.tile(cos64 / sc64, (2, 1)))
    d["c_l3f_sk"] = _ct(np.tile(sin64 / sc64 * sgn64, (2, 1)))
    lm = np.zeros((8, 128, 512), np.float32)
    for r in range(8):
        diff = 512 - 128 * r + qi - ki
        lm[r] = np.where((diff < 0) | (diff > 512), NEG, 0.0)
    d["c_l3f_mask"] = lm.reshape(8 * 128, 512)

    for k in list(d.keys()):
        d[k] = np.ascontiguousarray(d[k], dtype=np.float32)
    return d


def prep_core(inp, b):
    return {"x_t": _ct(np.asarray(inp["L0_attn_x"])[b].T)}




class Sem:
    __slots__ = ("h", "n")

    def __init__(self, h):
        self.h = h
        self.n = 0

    def inc(self, instr, k=1):
        instr.then_inc(self.h, k)
        self.n += k
        return self.n


class Prog:
    def __init__(self):
        self.nc = bass.Bass("TRN2", target_bir_lowering=False)
        self.stack = contextlib.ExitStack()
        self.inputs = {}
        self._sems = {}

    def dram_in(self, name, shape):
        if name not in self.inputs:
            t = self.nc.dram_tensor(name, list(shape), F32, kind="ExternalInput")
            self.inputs[name] = t.ap()
        return self.inputs[name]

    def dram_out(self, name, shape, dtype=F32):
        return self.nc.dram_tensor(name, list(shape), dtype, kind="ExternalOutput").ap()

    def sbuf(self, name, shape, dtype):
        return self.stack.enter_context(self.nc.sbuf_tensor(name, list(shape), dtype))

    def psum(self, name, shape, dtype=F32):
        return self.stack.enter_context(self.nc.psum_tensor(name, list(shape), dtype))

    def sem(self, name):
        name = name.rsplit("_", 1)[-1] if "_" in name else name
        if name not in self._sems:
            h = self.stack.enter_context(self.nc.semaphore("sm_" + name))
            self._sems[name] = Sem(h)
        return self._sems[name]

    def chain(self, eng, name, instr):
        s = self.sem(name)
        n = s.inc(instr, 1)
        eng.wait_ge(s.h, n)

    def link(self, eng, sem, instr, k=1):
        n = sem.inc(instr, k)
        eng.wait_ge(sem.h, n)
        return n

    def close(self):
        self.stack.close()


def make_env(p):
    env = {}
    env["X"] = [p.sbuf(f"X{c}", [128, N], F32) for c in range(NCH)]
    env["TN"] = [p.sbuf(f"TN{c}", [128, N], BF16) for c in range(NCH)]
    env["scr_f32"] = p.sbuf("scr_f32", [128, N], F32)
    env["scr_f32b"] = p.sbuf("scr_f32b", [128, N], F32)
    env["rows"] = p.sbuf("rows", [1, 2 * N + 64], F32)
    env["consts"] = p.sbuf("km_consts", [128, 8], F32)
    env["ones_row128"] = p.sbuf("ones_row128", [1, 128], F32)
    env["ones_bf"] = p.sbuf("ones_bf", [128, 1], BF16)
    env["ones_row512"] = p.sbuf("ones_row512", [1, NB], F32)
    env["ones_row_bf"] = p.sbuf("ones_row_bf", [1, 128], BF16)
    env["ones512_bf"] = p.sbuf("ones512_bf", [1, NB], BF16)
    env["wcol"] = p.sbuf("wcol", [128, 64], F32)
    env["PSA"] = [p.psum("psa0", [128, N]), p.psum("psa1", [128, N])]
    return env


def bank(env, i):
    return env["PSA"][i // 4][:, (i % 4) * NB:(i % 4 + 1) * NB]


CONST_COLS = {"one": 0, "eps5": 1, "eps6": 2, "neg1": 3, "epsl2": 4, "zero": 5}
CONST_VALS = {"one": 1.0, "eps5": 1e-5, "eps6": 1e-6, "neg1": -1.0,
              "epsl2": 1e-12, "zero": 0.0}


def ccol(env, name):
    return env["consts"][:, CONST_COLS[name]:CONST_COLS[name] + 1]


def emit_init(p, env):
    nc = p.nc
    x_t = p.dram_in("x_t", (1024, 2048))
    s_ld = p.sem("dma_sync")
    b0 = s_ld.n
    with nc.Block() as blk:
        @blk.sync
        def _(sy):
            for c in range(NCH):
                s_ld.inc(sy.dma_start(env["X"][c][:], x_t[c * 128:(c + 1) * 128, :]), 16)
            sy.wait_ge(s_ld.h, b0 + NCH * 16)

        @blk.vector
        def _(v):
            for nm, val in CONST_VALS.items():
                v.memset(env["consts"][:, CONST_COLS[nm]:CONST_COLS[nm] + 1], val)
            v.memset(env["ones_row128"][:], 1.0)
            v.memset(env["ones_bf"][:], 1.0)
            v.memset(env["ones_row512"][:], 1.0)
            v.memset(env["ones_row_bf"][:], 1.0)
            v.memset(env["ones512_bf"][:], 1.0)




def emit_spill_x(p, env, name="xsave"):
    nc = p.nc
    if name not in env:
        env[name] = nc.dram_tensor(name, [1024, N], F32, kind="Internal").ap()
    xs = env[name]
    s = p.sem("dma_sync")
    b0 = s.n
    with nc.Block() as blk:
        @blk.sync
        def _(sy):
            for c in range(NCH):
                s.inc(sy.dma_start(xs[c * 128:(c + 1) * 128, :], env["X"][c][:]), 16)
            sy.wait_ge(s.h, b0 + NCH * 16)


def emit_restore_add_x(p, env, name="xsave"):
    nc = p.nc
    xs = env[name]
    s = p.sem("dma_gp")
    b0 = s.n
    with nc.Block() as blk:
        @blk.gpsimd
        def _(g):
            for c in range(NCH):
                s.inc(g.dma_start(env["X"][c][:], xs[c * 128:(c + 1) * 128, :],
                                  accum_op=ALU.add), 16)
            g.wait_ge(s.h, b0 + NCH * 16)


def emit_store_x(p, env, out_ap):
    nc = p.nc
    s_st = p.sem("dma_sync")
    b0 = s_st.n
    with nc.Block() as blk:
        @blk.sync
        def _(sy):
            for c in range(NCH):
                s_st.inc(sy.dma_start(out_ap[c * 128:(c + 1) * 128, :], env["X"][c][:]), 16)
            sy.wait_ge(s_st.h, b0 + NCH * 16)




def emit_norm(p, env, kind="ln", w_name=None, src="X", out="TN", eps=1e-5):
    nc = p.nc
    X = env[src]
    TN = env[out]
    sq, sq2 = env["scr_f32"], env["scr_f32b"]
    rows = env["rows"]
    ln = kind == "ln"
    mu = rows[0:1, 0:N]
    mu2 = rows[0:1, N:2 * N]
    var = rows[0:1, N:2 * N]
    sdev = rows[0:1, N:2 * N]
    rstd = rows[0:1, N:2 * N]
    brow = rows[0:1, 0:N]
    eps_name = "eps5" if eps == 1e-5 else "eps6"
    ones_col = ccol(env, "one")
    orow = env["ones_row128"]
    wcol = env["wcol"]

    s_sq = p.sem("n_sq")
    s_mm = p.sem("n_mm")
    s_rv1 = p.sem("n_rv1")
    s_rsc = p.sem("n_rsc")
    s_rv2 = p.sem("n_rv2")
    s_bc = p.sem("n_bc")
    s_ap = p.sem("n_ap")
    s_wld = p.sem("dma_gp")
    b_sq, b_mm, b_rv1, b_rsc, b_rv2, b_bc, b_ap, b_wld = (
        s_sq.n, s_mm.n, s_rv1.n, s_rsc.n, s_rv2.n, s_bc.n, s_ap.n, s_wld.n)

    ps_sq = env["PSA"][0][0:1, :]
    ps_sx = env["PSA"][1][0:1, :]
    bcast_per_nb = 2 if ln else 1

    with nc.Block() as blk:
        if w_name is not None:
            wap = p.dram_in(w_name, (128, 8))

            @blk.gpsimd
            def _(g):
                s_wld.inc(g.dma_start(wcol[:, 0:8], wap[:, :]), 16)

        @blk.scalar
        def _(s):
            for c in range(NCH):
                buf = sq if c % 2 == 0 else sq2
                if c >= 2:
                    s.wait_ge(s_mm.h, b_mm + c - 1)
                s_sq.inc(s.activation(buf[:], X[c][:], AF.Square), 1)
            s.wait_ge(s_rv1.h, b_rv1 + 1)
            s_rsc.inc(s.activation(sdev, var, AF.Sqrt,
                                   bias=ccol(env, eps_name)[0:1, :]), 1)

        @blk.tensor
        def _(t):
            for c in range(NCH):
                buf = sq if c % 2 == 0 else sq2
                t.wait_ge(s_sq.h, b_sq + c + 1)
                mm = None
                for nb in range(NNB):
                    if ln:
                        t.matmul(ps_sx[:, nb * NB:(nb + 1) * NB], ones_col,
                                 X[c][:, nb * NB:(nb + 1) * NB],
                                 start=(c == 0), stop=(c == NCH - 1))
                    mm = t.matmul(ps_sq[:, nb * NB:(nb + 1) * NB], ones_col,
                                  buf[:, nb * NB:(nb + 1) * NB],
                                  start=(c == 0), stop=(c == NCH - 1))
                s_mm.inc(mm, 1)
            t.wait_ge(s_rv2.h, b_rv2 + 1)
            for nb in range(NNB):
                pa = bank(env, (nb % 2) * 4 + 0)
                pb = bank(env, (nb % 2) * 4 + 1)
                if nb >= 2:
                    t.wait_ge(s_ap.h, b_ap + nb - 1)
                s_bc.inc(t.matmul(pa, orow[:],
                                  rstd[:, nb * NB:(nb + 1) * NB],
                                  start=True, stop=True), 1)
                if ln:
                    s_bc.inc(t.matmul(pb, orow[:],
                                      brow[:, nb * NB:(nb + 1) * NB],
                                      start=True, stop=True), 1)

        @blk.vector
        def _(v):
            inv_d = 1.0 / D
            v.wait_ge(s_mm.h, b_mm + NCH)
            if ln:
                p.chain(v, "vch", v.tensor_scalar(mu, ps_sx, inv_d, 0.0,
                                                  ALU.mult, ALU.add))
                p.chain(v, "vch", v.tensor_tensor(mu2, mu, mu, ALU.mult))
                i3 = v.scalar_tensor_tensor(var, ps_sq, inv_d, mu2,
                                            ALU.mult, ALU.subtract)
                p.chain(v, "vch", i3)
                s_rv1.inc(v.nop(), 1)
            else:
                i3 = v.tensor_scalar(var, ps_sq, inv_d, 0.0, ALU.mult, ALU.add)
                p.chain(v, "vch", i3)
                s_rv1.inc(v.nop(), 1)
            v.wait_ge(s_rsc.h, b_rsc + 1)
            p.chain(v, "vch", v.reciprocal(rstd, sdev))
            if ln:
                p.chain(v, "vch", v.scalar_tensor_tensor(brow, mu, -1.0, rstd,
                                                         ALU.mult, ALU.mult))
            s_rv2.inc(v.nop(), 1)
            if w_name is not None:
                v.wait_ge(s_wld.h, b_wld + 16)
            for nb in range(NNB):
                pa = bank(env, (nb % 2) * 4 + 0)
                pb = bank(env, (nb % 2) * 4 + 1)
                v.wait_ge(s_bc.h, b_bc + (nb + 1) * bcast_per_nb)
                sl = slice(nb * NB, (nb + 1) * NB)
                tmps = [(sq if c < 4 else sq2)[:, (c % 4) * NB:(c % 4 + 1) * NB]
                        for c in range(NCH)]
                last = None
                for c in range(NCH):
                    last = v.tensor_tensor(tmps[c], X[c][:, sl], pa, ALU.mult)
                p.chain(v, "vch", last)
                last = None
                for c in range(NCH):
                    if ln:
                        last = v.tensor_tensor(TN[c][:, sl], tmps[c], pb, ALU.add)
                    else:
                        last = v.tensor_scalar(TN[c][:, sl], tmps[c],
                                               wcol[:, c:c + 1], 0.0,
                                               ALU.mult, ALU.add)
                p.chain(v, "vch", last)
                s_ap.inc(v.nop(), 1)




def emit_projA(p, env, tag, w_ap, Kd, Fd, in_tiles, wstage, scalar_cb=None,
               vector_cb=None, bias_ap=None, n_bias=0, w_col0=0,
               bank_base=0, nbanks=4):
    nc = p.nc
    kc = (Kd + 127) // 128
    ftn = (Fd + 127) // 128
    s_w = p.sem(tag + "_w")
    s_mm = p.sem(tag + "_mm")
    s_ev = p.sem(tag + "_ev")
    b_w, b_mm, b_ev = s_w.n, s_mm.n, s_ev.n
    need = 16 + (16 if bias_ap is not None else 0)

    with nc.Block() as blk:
        @blk.gpsimd
        def _(g):
            src = w_ap[0:Kd, w_col0:w_col0 + Fd].rearrange("(kc p) f -> p kc f", p=128)
            dst = wstage[:, 0:kc * Fd].rearrange("p (kc f) -> p kc f", f=Fd)
            s_w.inc(g.dma_start(dst, src), 16)

        if bias_ap is not None:
            @blk.sync
            def _(sy):
                s_w.inc(sy.dma_start(env["wcol"][:, 40:40 + n_bias],
                                     bias_ap[:, 0:n_bias]), 16)

        @blk.tensor
        def _(t):
            t.wait_ge(s_w.h, b_w + need)
            idx = 0
            for ft in range(ftn):
                fw = min(128, Fd - ft * 128)
                for nb in range(NNB):
                    ps = bank(env, bank_base + idx % nbanks)
                    if idx >= nbanks:
                        t.wait_ge(s_ev.h, b_ev + idx - nbanks + 1)
                    mm = None
                    for ic in range(kc):
                        kw = min(128, Kd - ic * 128)
                        mm = t.matmul(
                            ps[0:fw, :],
                            wstage[0:kw, ic * Fd + ft * 128: ic * Fd + ft * 128 + fw],
                            in_tiles[ic][0:kw, nb * NB:(nb + 1) * NB],
                            start=(ic == 0), stop=(ic == kc - 1))
                    s_mm.inc(mm, 1)
                    idx += 1

        if scalar_cb is not None:
            @blk.scalar
            def _(s):
                idx = 0
                for ft in range(ftn):
                    for nb in range(NNB):
                        s.wait_ge(s_mm.h, b_mm + idx + 1)
                        ps = bank(env, bank_base + idx % nbanks)
                        done = scalar_cb(s, ft, nb, ps)
                        if done is not None:
                            s_ev.inc(done, 1)
                        idx += 1

        if vector_cb is not None:
            @blk.vector
            def _(v):
                idx = 0
                for ft in range(ftn):
                    for nb in range(NNB):
                        v.wait_ge(s_mm.h, b_mm + idx + 1)
                        ps = bank(env, bank_base + idx % nbanks)
                        done = vector_cb(v, ft, nb, ps)
                        if done is not None:
                            s_ev.inc(done, 1)
                        idx += 1


def evac_to_tiles(env, out_tiles, func=AF.Copy, scale=1.0, has_bias=False):
    def cb(s, ft, nb, ps):
        kw = {}
        f = func
        if has_bias:
            kw["bias"] = env["wcol"][:, 40 + ft:41 + ft]
            if f == AF.Copy:
                f = AF.Identity
        return s.activation(out_tiles[ft][:, nb * NB:(nb + 1) * NB],
                            ps[:, :], f, scale=scale, **kw)
    return cb


def evac_residual(env, has_bias=False):
    def cb(v, ft, nb, ps):
        sl = slice(nb * NB, (nb + 1) * NB)
        X = env["X"][ft]
        if has_bias:
            return v.scalar_tensor_tensor(X[:, sl], ps[:, :],
                                          env["wcol"][:, 40 + ft:41 + ft],
                                          X[:, sl], ALU.add, ALU.add)
        return v.tensor_tensor(X[:, sl], X[:, sl], ps[:, :], ALU.add)
    return cb




def emit_mha(p, env, tag, w_q, w_k, w_v, qc0, kc0, vc0, cos_name, sin_name,
             perm_name, w_o, wo_bias, qk_scale=0.125, causal=True,
             in_name="TN"):
    nc = p.nc
    TN = env[in_name]
    with contextlib.ExitStack() as outer:
        osb = lambda nm, sh, dt: outer.enter_context(
            nc.sbuf_tensor(tag + nm, list(sh), dt))
        OT = [osb(f"_ot{c}", [128, N], BF16) for c in range(NCH)]
        with contextlib.ExitStack() as scope:
            sb = lambda nm, sh, dt: scope.enter_context(
                nc.sbuf_tensor(tag + nm, list(sh), dt))
            QT = sb("_qt", [128, N], BF16)
            KT = sb("_kt", [128, N], BF16)
            VT = [sb(f"_vt{i}", [128, 130], BF16) for i in range(16)]
            PB = sb("_pb", [128, 16 * NB], BF16)
            WS = PB[:, 0:3 * 1024]
            CS = sb("_cs", [128, 2 * N], BF16)
            PRM = sb("_prm", [128, 128], BF16)
            SW = PB[:, 4 * 1024:6 * 1024]
            SQA = env["scr_f32"][:].bitcast(BF16)[:, 0:N]
            SQK = env["scr_f32"][:].bitcast(BF16)[:, N:2 * N]
            HS2 = None
            MSK = env["scr_f32b"][:, 0:4 * NB]
            rws = env["rows"]
            MQ = MK = NB2 = None
            BCOL = sb("_bc", [128, 2], F32)
            RROW = rws[0:1, 0:NB]
            AVF = env["scr_f32"][:].bitcast(BF16)[0:64, 0:NB]

            s_gp = p.sem("dma_gp")
            s_sy = p.sem("dma_sync")
            b_gp, b_sy = s_gp.n, s_sy.n
            with nc.Block() as blk:
                @blk.gpsimd
                def _(g):
                    s_gp.inc(g.dma_start(CS[:, 0:N], p.dram_in(cos_name, (128, N))[:, :]), 16)
                    s_gp.inc(g.dma_start(CS[:, N:2 * N], p.dram_in(sin_name, (128, N))[:, :]), 16)
                    s_gp.inc(g.dma_start(PRM[:], p.dram_in(perm_name, (128, 128))[:, :]), 16)

                @blk.sync
                def _(sy):
                    if causal:
                        s_sy.inc(sy.dma_start(
                            MSK[:].rearrange("p (r q) -> p r q", r=4),
                            p.dram_in("c_causalmask", (512, 512))
                            .rearrange("(r p) q -> p r q", p=128)), 16)

                @blk.vector
                def _(v):
                    for i in range(16):
                        v.memset(VT[i][:, 64:65], 1.0)
                        v.memset(VT[i][:, 129:130], 1.0)
                    v.wait_ge(s_gp.h, b_gp + 48)
                    if causal:
                        v.wait_ge(s_sy.h, b_sy + 16)

            for hp in range(8):
                _mha_pair_proj(p, env, tag, hp, w_q, w_k, w_v, qc0, kc0, vc0,
                               QT, KT, VT, WS, TN)
                _mha_pair_rope(p, env, tag, QT, KT, CS, PRM, SW)
                _mha_pair_bounds(p, env, tag, QT, KT, SQA, SQK, HS2, MQ, MK,
                                 NB2, BCOL, qk_scale)
                _mha_pair_core(p, env, tag, hp, QT, KT, VT, PB, OT, MSK, RROW,
                               AVF, BCOL, causal, qk_scale)

        with contextlib.ExitStack() as scope2:
            WSO = scope2.enter_context(nc.sbuf_tensor(tag + "_wso", [128, 8192], BF16))
            emit_projA(p, env, tag + "_wo", w_o, 1024, 1024, OT, WSO,
                       vector_cb=evac_residual(env, has_bias=wo_bias is not None),
                       bias_ap=None if wo_bias is None else p.dram_in(wo_bias, (128, 8)),
                       n_bias=8)


def _mha_pair_proj(p, env, tag, hp, w_q, w_k, w_v, qc0, kc0, vc0,
                   QT, KT, VT, WS, TN):
    nc = p.nc
    s_w = p.sem(tag + "p_w")
    s_mm = p.sem(tag + "p_mm")
    s_ev = p.sem(tag + "p_ev")
    b_w, b_mm, b_ev = s_w.n, s_mm.n, s_ev.n
    with nc.Block() as blk:
        @blk.gpsimd
        def _(g):
            for i, (w, c0) in enumerate(((w_q, qc0), (w_k, kc0), (w_v, vc0))):
                src = w[:, c0 + hp * 128: c0 + (hp + 1) * 128].rearrange(
                    "(kc p) f -> p kc f", p=128)
                dst = WS[:, i * 1024:(i + 1) * 1024].rearrange(
                    "p (kc f) -> p kc f", f=128)
                s_w.inc(g.dma_start(dst, src), 16)

        @blk.tensor
        def _(t):
            t.wait_ge(s_w.h, b_w + 48)
            idx = 0
            for i in (0, 1):
                for nb in range(NNB):
                    ps = bank(env, idx % 4)
                    if idx >= 4:
                        t.wait_ge(s_ev.h, b_ev + idx - 3)
                    mm = None
                    for ic in range(8):
                        mm = t.matmul(ps, WS[:, i * 1024 + ic * 128:i * 1024 + (ic + 1) * 128],
                                      TN[ic][:, nb * NB:(nb + 1) * NB],
                                      start=(ic == 0), stop=(ic == 7))
                    s_mm.inc(mm, 1)
                    idx += 1
            for nt in range(16):
                ps = bank(env, 4 + nt % 2)
                if nt >= 2:
                    t.wait_ge(s_ev.h, b_ev + 8 + nt - 1)
                mm = None
                for ic in range(8):
                    mm = t.matmul(ps[:, 0:128],
                                  TN[ic][:, nt * 128:(nt + 1) * 128],
                                  WS[:, 2 * 1024 + ic * 128:2 * 1024 + (ic + 1) * 128],
                                  start=(ic == 0), stop=(ic == 7))
                s_mm.inc(mm, 1)

        @blk.scalar
        def _(s):
            idx = 0
            for i, dest in ((0, QT), (1, KT)):
                for nb in range(NNB):
                    s.wait_ge(s_mm.h, b_mm + idx + 1)
                    ps = bank(env, idx % 4)
                    s_ev.inc(s.activation(dest[:, nb * NB:(nb + 1) * NB], ps,
                                          AF.Copy), 1)
                    idx += 1
            for nt in range(16):
                s.wait_ge(s_mm.h, b_mm + 8 + nt + 1)
                ps = bank(env, 4 + nt % 2)
                src = ps[:, 0:128].rearrange("p (h c) -> p h c", c=64)
                dst = VT[nt][:, 0:130].rearrange("p (h c) -> p h c", c=65)[:, :, 0:64]
                s_ev.inc(s.activation(dst, src, AF.Copy), 1)


def _mha_pair_rope(p, env, tag, QT, KT, CS, PRM, SW):
    nc = p.nc
    s_rp = p.sem(tag + "r_rp")
    s_sw = p.sem(tag + "r_sw")
    s_rv = p.sem(tag + "r_rv")
    b_rp, b_sw, b_rv = s_rp.n, s_sw.n, s_rv.n
    with nc.Block() as blk:
        @blk.tensor
        def _(t):
            for idx in range(8):
                nb = idx % 4
                src = QT if idx < 4 else KT
                ps = bank(env, idx % 4)
                if idx >= 4:
                    t.wait_ge(s_sw.h, b_sw + idx - 3)
                s_rp.inc(t.matmul(ps, PRM[:], src[:, nb * NB:(nb + 1) * NB],
                                  start=True, stop=True), 1)

        @blk.scalar
        def _(s):
            for idx in range(8):
                s.wait_ge(s_rp.h, b_rp + idx + 1)
                if idx >= 4:
                    s.wait_ge(s_rv.h, b_rv + idx - 3)
                ps = bank(env, idx % 4)
                s_sw.inc(s.activation(SW[:, (idx % 4) * NB:(idx % 4 + 1) * NB],
                                      ps, AF.Copy), 1)

        @blk.vector
        def _(v):
            t1 = env["scr_f32"][:, 0:NB // 2].bitcast(BF16)
            t2 = env["scr_f32"][:, NB // 2:NB].bitcast(BF16)
            for idx in range(8):
                nb = idx % 4
                src = QT if idx < 4 else KT
                v.wait_ge(s_sw.h, b_sw + idx + 1)
                sl = slice(nb * NB, (nb + 1) * NB)
                swsl = SW[:, (idx % 4) * NB:(idx % 4 + 1) * NB]
                v.tensor_tensor(t1, src[:, sl], CS[:, sl], ALU.mult)
                i2 = v.tensor_tensor(t2, swsl, CS[:, N + nb * NB:N + (nb + 1) * NB],
                                     ALU.mult)
                p.chain(v, "vch", i2)
                i3 = v.tensor_tensor(src[:, sl], t1, t2, ALU.add)
                p.link(v, s_rv, i3)


def _mha_pair_bounds(p, env, tag, QT, KT, SQA, SQK, HS2, MQ, MK, NB2, BCOL,
                     qk_scale, bcol_off=0):
    nc = p.nc
    rws = env["rows"]
    c0 = 2 * N
    mq = rws[0:1, c0:c0 + 2]
    mk = rws[0:1, c0 + 2:c0 + 4]
    mprod = rws[0:1, c0 + 4:c0 + 6]
    msqrt = rws[0:1, c0 + 6:c0 + 8]
    negb = rws[0:1, c0 + 8:c0 + 10]
    s1 = p.sem(tag + "b_1")
    s2 = p.sem(tag + "b_2")
    s3 = p.sem(tag + "b_3")
    s4 = p.sem(tag + "b_4")
    s5 = p.sem(tag + "b_5")
    s6 = p.sem(tag + "b_6")
    s7 = p.sem(tag + "b_7")
    s8 = p.sem(tag + "b_8")
    b1, b2, b3, b4, b5, b6, b7, b8 = (s1.n, s2.n, s3.n, s4.n, s5.n, s6.n,
                                      s7.n, s8.n)
    ones = env["ones_bf"]
    with nc.Block() as blk:
        @blk.scalar
        def _(s):
            s1.inc(s.activation(SQA[:], QT[:], AF.Square), 1)
            s1.inc(s.activation(SQK[:], KT[:], AF.Square), 1)
            s.wait_ge(s5.h, b5 + 1)
            s6.inc(s.activation(msqrt, mprod, AF.Sqrt), 1)
            s.wait_ge(s8.h, b8 + 1)
            s.activation(BCOL[:, bcol_off:bcol_off + 1],
                         bank(env, 0)[:, 0:1], AF.Copy)
            s.wait_ge(s8.h, b8 + 2)
            s.activation(BCOL[:, bcol_off + 1:bcol_off + 2],
                         bank(env, 1)[:, 0:1], AF.Copy)

        @blk.tensor
        def _(t):
            t.wait_ge(s1.h, b1 + 1)
            mm = None
            for ai, base in ((0, 0), (1, 64)):
                for nb in range(NNB):
                    mm = t.matmul(env["PSA"][ai][0:1, nb * NB:(nb + 1) * NB],
                                  ones[base:base + 64, :],
                                  SQA[base:base + 64, nb * NB:(nb + 1) * NB],
                                  start=True, stop=True)
            s2.inc(mm, 1)
            t.wait_ge(s1.h, b1 + 2)
            t.wait_ge(s3.h, b3 + 1)
            for ai, base in ((0, 0), (1, 64)):
                for nb in range(NNB):
                    mm = t.matmul(env["PSA"][ai][0:1, nb * NB:(nb + 1) * NB],
                                  ones[base:base + 64, :],
                                  SQK[base:base + 64, nb * NB:(nb + 1) * NB],
                                  start=True, stop=True)
            s4.inc(mm, 1)
            t.wait_ge(s7.h, b7 + 1)
            s8.inc(t.matmul(bank(env, 0)[:, 0:1], env["ones_row128"][:],
                            negb[0:1, 0:1], start=True, stop=True), 1)
            s8.inc(t.matmul(bank(env, 1)[:, 0:1], env["ones_row128"][:],
                            negb[0:1, 1:2], start=True, stop=True), 1)

        @blk.vector
        def _(v):
            v.wait_ge(s2.h, b2 + 1)
            v.tensor_reduce(mq[0:1, 0:1], env["PSA"][0][0:1, :], axis=AX, op=ALU.max)
            i2 = v.tensor_reduce(mq[0:1, 1:2], env["PSA"][1][0:1, :], axis=AX, op=ALU.max)
            p.link(v, s3, i2)
            v.wait_ge(s4.h, b4 + 1)
            v.tensor_reduce(mk[0:1, 0:1], env["PSA"][0][0:1, :], axis=AX, op=ALU.max)
            i4 = v.tensor_reduce(mk[0:1, 1:2], env["PSA"][1][0:1, :], axis=AX, op=ALU.max)
            p.chain(v, "vch", i4)
            i5 = v.tensor_tensor(mprod, mq, mk, ALU.mult)
            p.link(v, s5, i5)
            v.wait_ge(s6.h, b6 + 1)
            s7.inc(v.tensor_scalar(negb, msqrt, -qk_scale, 0.0,
                                   ALU.mult, ALU.add), 1)


def _mha_pair_core(p, env, tag, hp, QT, KT, VT, PB, OT, MSK, RROW, AVF,
                   BCOL, causal, qk_scale, ot_tile=None):
    nc = p.nc
    s_smm = p.sem(tag + "c_smm")
    s_msk = p.sem(tag + "c_msk")
    s_exp = p.sem(tag + "c_exp")
    s_av = p.sem(tag + "c_av")
    s_r = p.sem(tag + "c_r")
    s_rb = p.sem(tag + "c_rb")
    s_avc = p.sem(tag + "c_avc")
    s_don = p.sem(tag + "c_don")
    b_smm, b_msk, b_exp, b_av, b_r, b_rb, b_avc, b_don = (
        s_smm.n, s_msk.n, s_exp.n, s_av.n, s_r.n, s_rb.n, s_avc.n, s_don.n)

    units = []
    gidx = 0
    for h in (0, 1):
        for qb in range(NNB):
            ktn = 4 * (qb + 1) if causal else 16
            groups = []
            for g0 in range(0, ktn, 2):
                kts = list(range(g0, min(g0 + 2, ktn)))
                diag = causal and (g0 // 4 == qb)
                groups.append((gidx, kts, diag))
                gidx += 1
            units.append((h, qb, groups, ktn))
    ndiag_before = {}
    nd = 0
    for h, qb, groups, ktn in units:
        for gi, kts, diag in groups:
            ndiag_before[gi] = nd
            if diag:
                nd += 1

    def s_reg(gi, kn):
        half = gi % 2
        return env["PSA"][0][:, half * 2 * NB:(half * 2 + kn) * NB]

    with nc.Block() as blk:
        @blk.tensor
        def _(t):
            ui = 0
            for h, qb, groups, ktn in units:
                hb = 64 * h
                for gi, kts, diag in groups:
                    if gi >= 2:
                        t.wait_ge(s_exp.h, b_exp + gi - 1)
                    mm = None
                    for ki, kt in enumerate(kts):
                        mm = t.matmul(
                            bank(env, (gi % 2) * 2 + ki),
                            KT[hb:hb + 64, kt * 128:(kt + 1) * 128],
                            QT[hb:hb + 64, qb * NB:(qb + 1) * NB],
                            start=True, stop=True)
                    s_smm.inc(mm, 1)
                t.wait_ge(s_exp.h, b_exp + groups[-1][0] + 1)
                if ui >= 2:
                    t.wait_ge(s_don.h, b_don + ui - 1)
                mm = None
                for j, kt in enumerate(range(ktn)):
                    mm = t.matmul(bank(env, 4 + ui % 2)[0:65, :],
                                  VT[kt][:, 65 * h:65 * h + 65],
                                  PB[:, kt * NB:(kt + 1) * NB],
                                  start=(j == 0), stop=(j == ktn - 1))
                s_av.inc(mm, 1)
                t.wait_ge(s_r.h, b_r + ui + 1)
                s_rb.inc(t.matmul(bank(env, 6)[0:64, :],
                                  env["ones_row128"][:, 0:64], RROW[:],
                                  start=True, stop=True), 1)
                ui += 1

        @blk.vector
        def _(v):
            ui = 0
            for h, qb, groups, ktn in units:
                for gi, kts, diag in groups:
                    if diag:
                        v.wait_ge(s_smm.h, b_smm + gi + 1)
                        kn = len(kts)
                        reg = s_reg(gi, kn)
                        r0 = kts[0] - 4 * qb
                        mreg = MSK[:, r0 * NB:(r0 + kn) * NB]
                        s_msk.inc(v.tensor_tensor(reg, reg, mreg, ALU.add), 1)
                v.wait_ge(s_av.h, b_av + ui + 1)
                i1 = v.reciprocal(RROW[:], bank(env, 4 + ui % 2)[64:65, :])
                p.link(v, s_r, i1)
                v.wait_ge(s_rb.h, b_rb + ui + 1)
                v.wait_ge(s_avc.h, b_avc + ui + 1)
                dest = ot_tile if ot_tile is not None else OT[hp]
                i2 = v.tensor_tensor(dest[64 * h:64 * h + 64,
                                          qb * NB:(qb + 1) * NB],
                                     AVF[:], bank(env, 6)[0:64, :], ALU.mult)
                p.link(v, s_don, i2)
                ui += 1

        @blk.scalar
        def _(s):
            ui = 0
            for h, qb, groups, ktn in units:
                for gi, kts, diag in groups:
                    if diag:
                        s.wait_ge(s_msk.h, b_msk + ndiag_before[gi] + 1)
                    else:
                        s.wait_ge(s_smm.h, b_smm + gi + 1)
                    kn = len(kts)
                    reg = s_reg(gi, kn)
                    s_exp.inc(s.activation(PB[:, kts[0] * NB:(kts[0] + kn) * NB],
                                           reg, AF.Exp, scale=qk_scale,
                                           bias=BCOL[:, h:h + 1]), 1)
                s.wait_ge(s_av.h, b_av + ui + 1)
                if ui >= 1:
                    s.wait_ge(s_don.h, b_don + ui)
                s_avc.inc(s.activation(AVF[:], bank(env, 4 + ui % 2)[0:64, :],
                                       AF.Copy), 1)
                ui += 1




def emit_block_l2_attn(p, env):
    emit_norm(p, env, kind="ln")
    emit_mha(p, env, "l2a",
             p.dram_in("w_l2a_qkv", (1024, 3072)),
             p.dram_in("w_l2a_qkv", (1024, 3072)),
             p.dram_in("w_l2a_qkv", (1024, 3072)),
             0, 1024, 2048,
             "c_l2a_cos", "c_l2a_sin", "c_pairswap",
             p.dram_in("w_l2a_o", (1024, 1024)), None,
             qk_scale=0.125, causal=True)


def emit_block_l0_ffn_attn_only(p, env):
    emit_norm(p, env, kind="rms", w_name="w_l0f_anorm", eps=1e-6)
    emit_mha(p, env, "l0fa",
             p.dram_in("w_l0f_q", (1024, 1024)),
             p.dram_in("w_l0f_k", (1024, 1024)),
             p.dram_in("w_l0f_v", (1024, 1024)),
             0, 0, 0,
             "c_rh_cos", "c_rh_sin", "c_swap32",
             p.dram_in("w_l0f_o", (1024, 1024)), None,
             qk_scale=0.125, causal=True)




def emit_moe(p, env, tag, E, w1_name, w2_name, b1_name, b2row_name,
             router, in_name="TN", H=4096):
    nc = p.nc
    TN = env[in_name]
    w1 = p.dram_in(w1_name, (E, 1024, H))
    w2 = p.dram_in(w2_name, (E, H, 1024))
    b1 = p.dram_in(b1_name, (E, 128, H // 128))
    b2r = p.dram_in(b2row_name, (E, 128, 8))
    ohbc = p.dram_in("c_ohbc", (16, 16 * 128))
    HH = H // 4
    HT = HH // 128
    with contextlib.ExitStack() as scope:
        sb = lambda nm, sh, dt: scope.enter_context(
            nc.sbuf_tensor(tag + nm, list(sh), dt))
        WH1 = sb("_wh1", [128, 8 * HH], BF16)
        WH2 = sb("_wh2", [128, HT * 1024], BF16)
        HID = sb("_hid", [128, HT * NB], BF16)
        CR = sb("_cr", [16, N], BF16)
        CBC = sb("_cbc", [128, NB], F32)
        B2C2 = sb("_b2c", [128, 16], F32)
        B2C = [B2C2[:, 0:8], B2C2[:, 8:16]]
        OHB2 = sb("_ohb", [16, 256], BF16)
        OHB = [OHB2[:, 0:128], OHB2[:, 128:256]]
        ROUT = sb("_rout", [128, 40], F32)

        with nc.Block() as blk0:
            @blk0.vector
            def _(v):
                v.memset(CR[:, :], 0.0)
        router(p, env, scope, CR, ROUT)

        steps = [(e, q) for e in range(E) for q in range(4)]
        st = {"w1": None, "w2": None, "b": None}
        s_w = p.sem(tag + "e_w")
        HH = H // 4
        with nc.Block() as blk0:
            @blk0.gpsimd
            def _(g):
                e0, q0 = steps[0]
                src1 = w1[e0, :, q0 * HH:(q0 + 1) * HH].rearrange(
                    "(kc p) f -> p kc f", p=128)
                s_w.inc(g.dma_start(
                    WH1[:].rearrange("p (kc f) -> p kc f", f=HH), src1), 16)
                src2 = w2[e0, q0 * HH:(q0 + 1) * HH, :].rearrange(
                    "(kc p) f -> p kc f", p=128)
                s_w.inc(g.dma_start(
                    WH2[:].rearrange("p (kc f) -> p kc f", f=1024), src2), 16)
                st["w1"] = s_w.n - 16
                st["w2"] = s_w.n
                s_w.inc(g.dma_start(OHB[0][:], ohbc[0:16, e0 * 128:(e0 + 1) * 128]), 16)

            @blk0.sync
            def _(sy):
                e0, q0 = steps[0]
                HT = HH // 128
                s_w.inc(sy.dma_start(env["wcol"][:, 0:HT],
                                     b1[e0, :, q0 * HT:(q0 + 1) * HT]), 16)
                s_w.inc(sy.dma_start(B2C[0], b2r[e0, :, :]), 16)
                st["b"] = s_w.n

        for i, (e, q) in enumerate(steps):
            nxt = steps[i + 1] if i + 1 < len(steps) else None
            _moe_expert_half(p, env, tag, e, q, i, nxt, E, w1, w2, b1, b2r,
                             ohbc, WH1, WH2, HID, CR, CBC, B2C, OHB, TN, H,
                             st, s_w)


def _moe_expert_half(p, env, tag, e, half, i, nxt, E, w1, w2, b1, b2r, ohbc,
                     WH1, WH2, HID, CR, CBC, B2C, OHB, TN, H, st, s_w):
    nc = p.nc
    HH = H // 4
    HT = HH // 128
    s_m1 = p.sem(tag + "e_m1")
    s_g = p.sem(tag + "e_g")
    s_cb = p.sem(tag + "e_cb")
    s_cc = p.sem(tag + "e_cc")
    s_m2 = p.sem(tag + "e_m2")
    s_tm = p.sem(tag + "e_tm")
    s_ad = p.sem(tag + "e_ad")
    b_m1, b_g, b_cb, b_cc, b_m2, b_tm, b_ad = (
        s_m1.n, s_g.n, s_cb.n, s_cc.n, s_m2.n, s_tm.n, s_ad.n)
    w1_tgt, w2_tgt, b_tgt = st["w1"], st["w2"], st["b"]
    par = i % 2
    bcol0 = 8 * par

    with nc.Block() as blk:
        @blk.gpsimd
        def _(g):
            def adds(nb):
                for dt in range(NCH):
                    g.wait_ge(s_tm.h, b_tm + nb * NCH + dt + 1)
                    tmp = (env["scr_f32"] if dt < 4 else env["scr_f32b"])[
                        :, (dt % 4) * NB:(dt % 4 + 1) * NB]
                    sl = slice(nb * NB, (nb + 1) * NB)
                    s_ad.inc(g.tensor_tensor(env["X"][dt][:, sl],
                                             env["X"][dt][:, sl], tmp,
                                             ALU.add), 1)
            adds(0)
            adds(1)
            adds(2)
            if nxt is not None:
                en, qn = nxt
                g.wait_ge(s_m1.h, b_m1 + NNB * HT)
                src1 = w1[en, :, qn * HH:(qn + 1) * HH].rearrange(
                    "(kc p) f -> p kc f", p=128)
                s_w.inc(g.dma_start(
                    WH1[:].rearrange("p (kc f) -> p kc f", f=HH), src1), 16)
                st["w1"] = s_w.n
                if qn == 0:
                    s_w.inc(g.dma_start(OHB[en % 2][:],
                                        ohbc[0:16, en * 128:(en + 1) * 128]), 16)
                st["b"] = s_w.n
            adds(3)
            if nxt is not None:
                en, qn = nxt
                g.wait_ge(s_m2.h, b_m2 + NNB * NCH)
                src2 = w2[en, qn * HH:(qn + 1) * HH, :].rearrange(
                    "(kc p) f -> p kc f", p=128)
                s_w.inc(g.dma_start(
                    WH2[:].rearrange("p (kc f) -> p kc f", f=1024), src2), 16)
                st["w2"] = s_w.n

        @blk.sync
        def _(sy):
            if nxt is not None:
                en, qn = nxt
                s_w.inc(sy.dma_start(env["wcol"][:, 8 * ((i + 1) % 2):
                                                 8 * ((i + 1) % 2) + HT],
                                     b1[en, :, qn * HT:(qn + 1) * HT]), 16)
                if qn == 0:
                    s_w.inc(sy.dma_start(B2C[en % 2], b2r[en, :, :]), 16)
                st["b"] = s_w.n

        @blk.tensor
        def _(t):
            t.wait_ge(s_w.h, max(w1_tgt, b_tgt))
            for nb in range(NNB):
                for ht in range(HT):
                    idx = nb * HT + ht
                    ps = bank(env, idx % 2)
                    if idx >= 2:
                        t.wait_ge(s_g.h, b_g + idx - 1)
                    mm = None
                    for ic in range(8):
                        mm = t.matmul(ps,
                                      WH1[:, ic * HH + ht * 128:ic * HH + (ht + 1) * 128],
                                      TN[ic][:, nb * NB:(nb + 1) * NB],
                                      start=(ic == 0), stop=(ic == 7))
                    s_m1.inc(mm, 1)
                if nb >= 1:
                    t.wait_ge(s_cc.h, b_cc + nb)
                s_cb.inc(t.matmul(bank(env, 4), OHB[e % 2][:],
                                  CR[:, nb * NB:(nb + 1) * NB],
                                  start=True, stop=True), 1)
                t.wait_ge(s_g.h, b_g + (nb + 1) * HT)
                if nb == 0:
                    t.wait_ge(s_w.h, w2_tgt)
                for dt in range(NCH):
                    idx2 = nb * NCH + dt
                    ps = bank(env, 2 + idx2 % 2)
                    if idx2 >= 2:
                        t.wait_ge(s_tm.h, b_tm + idx2 - 1)
                    for kt in range(HT):
                        mm = t.matmul(
                            ps, WH2[:, kt * 1024 + dt * 128:kt * 1024 + (dt + 1) * 128],
                            HID[:, kt * NB:(kt + 1) * NB],
                            start=(kt == 0), stop=(kt == HT - 1))
                    s_m2.inc(mm, 1)

        @blk.scalar
        def _(s):
            for nb in range(NNB):
                for ht in range(HT):
                    idx = nb * HT + ht
                    s.wait_ge(s_m1.h, b_m1 + idx + 1)
                    if nb >= 1 and ht == 0:
                        s.wait_ge(s_m2.h, b_m2 + nb * NCH)
                    ps = bank(env, idx % 2)
                    s_g.inc(s.activation(HID[:, ht * NB:(ht + 1) * NB], ps,
                                         AF.Gelu,
                                         bias=env["wcol"][:, bcol0 + ht:
                                                          bcol0 + ht + 1]), 1)
                s.wait_ge(s_cb.h, b_cb + nb + 1)
                if nb >= 1:
                    s.wait_ge(s_tm.h, b_tm + nb * NCH)
                s_cc.inc(s.activation(CBC[:], bank(env, 4), AF.Copy), 1)

        @blk.vector
        def _(v):
            for nb in range(NNB):
                v.wait_ge(s_cc.h, b_cc + nb + 1)
                for dt in range(NCH):
                    idx2 = nb * NCH + dt
                    v.wait_ge(s_m2.h, b_m2 + idx2 + 1)
                    if nb >= 1:
                        v.wait_ge(s_ad.h, b_ad + nb * NCH)
                    ps = bank(env, 2 + idx2 % 2)
                    tmp = (env["scr_f32"] if dt < 4 else env["scr_f32b"])[
                        :, (dt % 4) * NB:(dt % 4 + 1) * NB]
                    if half == 0:
                        s_tm.inc(v.scalar_tensor_tensor(
                            tmp, ps, B2C[e % 2][:, dt:dt + 1], CBC[:],
                            ALU.add, ALU.mult), 1)
                    else:
                        s_tm.inc(v.tensor_tensor(tmp, ps, CBC[:], ALU.mult), 1)


def router_l2f(p, env, scope, CR, ROUT):
    nc = p.nc
    TN = env["TN"]
    wr = p.dram_in("w_l2f_r", (1024, 8))
    br = p.dram_in("b_l2f_r", (1, 8))
    idf = p.dram_in("c_identity", (128, 128))
    sb = lambda nm, sh, dt: scope.enter_context(
        nc.sbuf_tensor("r2" + nm, list(sh), dt))
    WR = sb("_wr", [128, 64], BF16)
    BRR = sb("_br", [1, 8], BF16)
    IDF = sb("_id", [128, 128], F32)
    OBF = sb("_obf", [1, 128], BF16)
    E = 8
    m1 = ROUT[:, 8:9]
    nm1 = ROUT[:, 9:10]
    den = ROUT[:, 10:11]
    gcol = ROUT[:, 11:12]
    is1 = ROUT[:, 16:24]
    ctok = ROUT[:, 24:32]
    ex = ROUT[:, 32:40]
    s_w = p.sem("r2_w")
    s_mm = p.sem("r2_mm")
    s_v1 = p.sem("r2_v1")
    s_sc = p.sem("r2_sc")
    s_v2 = p.sem("r2_v2")
    s_tr = p.sem("r2_tr")
    s_cp = p.sem("r2_cp")
    b_w, b_mm, b_v1, b_sc, b_v2, b_tr, b_cp = (
        s_w.n, s_mm.n, s_v1.n, s_sc.n, s_v2.n, s_tr.n, s_cp.n)

    with nc.Block() as blk:
        @blk.gpsimd
        def _(g):
            s_w.inc(g.dma_start(
                WR[:].rearrange("p (kc f) -> p kc f", f=8),
                wr.rearrange("(kc p) f -> p kc f", p=128)), 16)
            s_w.inc(g.dma_start(BRR[:], br[:, :]), 16)
            s_w.inc(g.dma_start(OBF[:], env["ones_row128"][:]), 16)

        @blk.sync
        def _(sy):
            s_w.inc(sy.dma_start(IDF[:], idf[:, :]), 16)

        @blk.tensor
        def _(t):
            t.wait_ge(s_w.h, b_w + 64)
            for nt in range(16):
                ps = bank(env, nt % 2)
                if nt >= 2:
                    t.wait_ge(s_v1.h, b_v1 + nt - 1)
                for ic in range(8):
                    t.matmul(ps[:, 0:8], TN[ic][:, nt * 128:(nt + 1) * 128],
                             WR[:, ic * 8:(ic + 1) * 8],
                             start=(ic == 0), stop=False)
                mm = t.matmul(ps[:, 0:8], OBF[:], BRR[:],
                              start=False, stop=True)
                s_mm.inc(mm, 1)
                t.wait_ge(s_v2.h, b_v2 + nt + 1)
                ps2 = bank(env, 4 + nt % 2)
                s_tr.inc(t.transpose(ps2[0:8, 0:128], ctok[:, 0:8], IDF[:]), 1)

        @blk.vector
        def _(v):
            for nt in range(16):
                v.wait_ge(s_mm.h, b_mm + nt + 1)
                ps = bank(env, nt % 2)
                i1 = v.tensor_reduce(m1, ps[:, 0:8], axis=AX, op=ALU.max)
                p.chain(v, "vch", i1)
                i2 = v.tensor_scalar(is1[:, 0:8], ps[:, 0:8], m1, 1.0,
                                     ALU.is_ge, ALU.mult)
                i3 = v.tensor_scalar(nm1, m1, -1.0, 0.0, ALU.mult, ALU.add)
                p.link(v, s_v1, i3)
                v.wait_ge(s_sc.h, b_sc + nt + 1)
                i4 = v.reciprocal(gcol, den)
                p.chain(v, "vch", i4)
                v.wait_ge(s_tr.h, b_tr + nt)
                i5 = v.tensor_scalar(ctok[:, 0:8], is1[:, 0:8], gcol, 1.0,
                                     ALU.mult, ALU.mult)
                p.link(v, s_v2, i5)

        @blk.scalar
        def _(s):
            for nt in range(16):
                s.wait_ge(s_v1.h, b_v1 + nt + 1)
                ps = bank(env, nt % 2)
                s_sc.inc(s.activation(ex[:, 0:8], ps[:, 0:8], AF.Exp,
                                      bias=nm1, accum_out=den), 1)
                s.wait_ge(s_tr.h, b_tr + nt + 1)
                ps2 = bank(env, 4 + nt % 2)
                s_cp.inc(s.activation(CR[0:8, nt * 128:(nt + 1) * 128],
                                      ps2[0:8, 0:128], AF.Copy), 1)



def router_l0f(p, env, scope, CR, ROUT):
    nc = p.nc
    TN = env["TN"]
    wr = p.dram_in("w_l0f_gate", (1024, 8))
    idf = p.dram_in("c_identity", (128, 128))
    sb = lambda nm, sh, dt: scope.enter_context(
        nc.sbuf_tensor("r0" + nm, list(sh), dt))
    WR = sb("_wr", [128, 64], BF16)
    IDF = sb("_id", [128, 128], F32)
    m1 = ROUT[:, 8:9]
    m2 = ROUT[:, 9:10]
    dcol = ROUT[:, 10:11]
    rw1 = ROUT[:, 11:12]
    rw2 = ROUT[:, 12:13]
    is1 = ROUT[:, 16:24]
    is2 = ROUT[:, 24:32]
    msk = ROUT[:, 32:40]
    ctok = ROUT[:, 0:8]
    s_w = p.sem("r0_w")
    s_mm = p.sem("r0_mm")
    s_v1 = p.sem("r0_v1")
    s_sc = p.sem("r0_sc")
    s_v2 = p.sem("r0_v2")
    s_tr = p.sem("r0_tr")
    s_cp = p.sem("r0_cp")
    b_w, b_mm, b_v1, b_sc, b_v2, b_tr, b_cp = (
        s_w.n, s_mm.n, s_v1.n, s_sc.n, s_v2.n, s_tr.n, s_cp.n)

    with nc.Block() as blk:
        @blk.gpsimd
        def _(g):
            s_w.inc(g.dma_start(
                WR[:].rearrange("p (kc f) -> p kc f", f=8),
                wr.rearrange("(kc p) f -> p kc f", p=128)), 16)

        @blk.sync
        def _(sy):
            s_w.inc(sy.dma_start(IDF[:], idf[:, :]), 16)

        @blk.tensor
        def _(t):
            t.wait_ge(s_w.h, b_w + 32)
            for nt in range(16):
                ps = bank(env, nt % 2)
                if nt >= 2:
                    t.wait_ge(s_v1.h, b_v1 + nt - 1)
                mm = None
                for ic in range(8):
                    mm = t.matmul(ps[:, 0:8], TN[ic][:, nt * 128:(nt + 1) * 128],
                                  WR[:, ic * 8:(ic + 1) * 8],
                                  start=(ic == 0), stop=(ic == 7))
                s_mm.inc(mm, 1)
                t.wait_ge(s_v2.h, b_v2 + nt + 1)
                ps2 = bank(env, 4 + nt % 2)
                s_tr.inc(t.transpose(ps2[0:8, 0:128], ctok[:, 0:8], IDF[:]), 1)

        @blk.vector
        def _(v):
            for nt in range(16):
                v.wait_ge(s_mm.h, b_mm + nt + 1)
                ps = bank(env, nt % 2)
                i1 = v.tensor_reduce(m1, ps[:, 0:8], axis=AX, op=ALU.max)
                p.chain(v, "vch", i1)
                i2 = v.tensor_scalar(is1[:, 0:8], ps[:, 0:8], m1, 1.0,
                                     ALU.is_ge, ALU.mult)
                p.chain(v, "vch", i2)
                i3 = v.scalar_tensor_tensor(msk[:, 0:8], is1[:, 0:8], NEG,
                                            ps[:, 0:8], ALU.mult, ALU.add)
                p.chain(v, "vch", i3)
                i4 = v.tensor_reduce(m2, msk[:, 0:8], axis=AX, op=ALU.max)
                p.chain(v, "vch", i4)
                i5 = v.tensor_scalar(is2[:, 0:8], msk[:, 0:8], m2, 1.0,
                                     ALU.is_ge, ALU.mult)
                i6 = v.tensor_tensor(dcol, m1, m2, ALU.subtract)
                p.link(v, s_v1, i6)
                v.wait_ge(s_sc.h, b_sc + nt + 1)
                i7 = v.tensor_scalar(rw2, rw1, -1.0, 1.0, ALU.mult, ALU.add)
                p.chain(v, "vch", i7)
                v.wait_ge(s_tr.h, b_tr + nt)
                i8 = v.tensor_scalar(ctok[:, 0:8], is1[:, 0:8], rw1, 1.0,
                                     ALU.mult, ALU.mult)
                p.chain(v, "vch", i8)
                i9 = v.scalar_tensor_tensor(ctok[:, 0:8], is2[:, 0:8], rw2,
                                            ctok[:, 0:8], ALU.mult, ALU.add)
                p.link(v, s_v2, i9)

        @blk.scalar
        def _(s):
            for nt in range(16):
                s.wait_ge(s_v1.h, b_v1 + nt + 1)
                s_sc.inc(s.activation(rw1, dcol, AF.Sigmoid), 1)
                s.wait_ge(s_tr.h, b_tr + nt + 1)
                ps2 = bank(env, 4 + nt % 2)
                s_cp.inc(s.activation(CR[0:8, nt * 128:(nt + 1) * 128],
                                      ps2[0:8, 0:128], AF.Copy), 1)



def emit_block_l2_ffn(p, env):
    emit_norm(p, env, kind="ln")
    emit_moe(p, env, "l2f", 8, "w_l2f_w1", "w_l2f_w2", "b_l2f_b1",
             "b2row_l2f", router_l2f)


def emit_block_l0_ffn(p, env):
    emit_spill_x(p, env)
    emit_norm(p, env, kind="ln", out="X")
    emit_norm(p, env, kind="rms", w_name="w_l0f_anorm", eps=1e-6)
    emit_mha(p, env, "l0fa",
             p.dram_in("w_l0f_q", (1024, 1024)),
             p.dram_in("w_l0f_k", (1024, 1024)),
             p.dram_in("w_l0f_v", (1024, 1024)),
             0, 0, 0,
             "c_rh_cos", "c_rh_sin", "c_swap32",
             p.dram_in("w_l0f_o", (1024, 1024)), None,
             qk_scale=0.125, causal=True)
    emit_norm(p, env, kind="rms", w_name="w_l0f_mnorm", eps=1e-6)
    emit_moe(p, env, "l0f", 8, "w_l0f_w1", "w_l0f_w2", "b_l0f_b1",
             "b2row_l0f", router_l0f)
    emit_restore_add_x(p, env)




def emit_block_l0_attn(p, env):
    nc = p.nc
    emit_norm(p, env, kind="ln")
    TN = env["TN"]
    qk_scale = 1.0 / float(np.sqrt(1040.0))
    w_dq = p.dram_in("w_l0a_dq", (1024, 512))
    w_dkv = p.dram_in("w_l0a_dkv", (1024, 512))
    w_uq = p.dram_in("w_l0a_uq", (512, 1024))
    w_qr = p.dram_in("w_l0a_qr", (512, 1024))
    w_uk = p.dram_in("w_l0a_uk", (512, 1024))
    w_kr = p.dram_in("w_l0a_kr", (1024, 1024))
    w_uv = p.dram_in("w_l0a_uv", (512, 1024))
    w_o = p.dram_in("w_l0a_o", (1024, 1024))
    cosap = p.dram_in("c_l0a_cos", (1024, N))
    sinap = p.dram_in("c_l0a_sin", (1024, N))

    with contextlib.ExitStack() as outer:
        with contextlib.ExitStack() as scope:
            sb = lambda nm, sh, dt: scope.enter_context(
                nc.sbuf_tensor("l0a" + nm, list(sh), dt))
            OTP = sb("_otp", [128, N], BF16)
            WOS = sb("_wos", [128, 1024], BF16)
            BOR = env["rows"][:].bitcast(BF16)[0:1, 2048:3072]
            CTQ = [sb(f"_ctq{i}", [128, N], BF16) for i in range(4)]
            CTK = [sb(f"_ctk{i}", [128, N], BF16) for i in range(4)]
            QT = sb("_qt", [128, N], BF16)
            KT = sb("_kt", [128, N], BF16)
            VT = [sb(f"_vt{i}", [128, 130], BF16) for i in range(16)]
            PB = sb("_pb", [128, 16 * NB], BF16)
            WS = PB[:, 0:24 * 128]
            CS = PB[:, 3072:3072 + 2 * N]
            SWs = PB[:, 7168:8192]
            PRM = sb("_prm", [128, 128], BF16)
            WSD = PB[:, 0:4096]
            emit_projA(p, env, "l0a_dq", w_dq, 1024, 512, TN, WSD,
                       scalar_cb=evac_to_tiles(env, CTQ, has_bias=True),
                       bias_ap=p.dram_in("b_l0a_dq", (128, 4)), n_bias=4)
            emit_projA(p, env, "l0a_dkv", w_dkv, 1024, 512, TN, WSD,
                       scalar_cb=evac_to_tiles(env, CTK, has_bias=True),
                       bias_ap=p.dram_in("b_l0a_dkv", (128, 4)), n_bias=4)
            SQA = env["scr_f32"][:].bitcast(BF16)[:, 0:N]
            SQK = env["scr_f32"][:].bitcast(BF16)[:, N:2 * N]
            rws = env["rows"]
            BCOL = sb("_bc", [128, 2], F32)
            RROW = rws[0:1, 0:NB]
            AVF = env["scr_f32"][:].bitcast(BF16)[0:64, 0:NB]
            BVR = env["rows"][:].bitcast(BF16)[0:1, 4096:5120]

            s_c = p.sem("l0a_c")
            with nc.Block() as blk:
                @blk.gpsimd
                def _(g):
                    s_c.inc(g.dma_start(PRM[:], p.dram_in("c_pairswap", (128, 128))[:, :]), 16)
                    s_c.inc(g.dma_start(BVR[:], p.dram_in("br_l0a_uv", (1, 1024))[:, :]), 16)
                    s_c.inc(g.dma_start(BOR[:], p.dram_in("br_l0a_o", (1, 1024))[:, :]), 16)

                @blk.sync
                def _(sy):
                    for j, bn in enumerate(("b_l0a_uq", "b_l0a_qr",
                                            "b_l0a_uk", "b_l0a_kr")):
                        s_c.inc(sy.dma_start(env["wcol"][:, 8 + 8 * j:16 + 8 * j],
                                             p.dram_in(bn, (128, 8))[:, :]), 16)

                @blk.vector
                def _(v):
                    for i in range(16):
                        v.memset(VT[i][:, 64:65], 1.0)
                        v.memset(VT[i][:, 129:130], 1.0)
                    v.wait_ge(s_c.h, s_c.n)

            for hp in range(8):
                _l0a_pair_proj(p, env, hp, w_uq, w_qr, w_uk, w_kr, w_uv,
                               cosap, sinap, CTQ, CTK, TN, QT, KT, VT, WS,
                               CS, PRM, SWs, BVR)
                _mha_pair_bounds(p, env, "l0a", QT, KT, SQA, SQK, None,
                                 None, None, None, BCOL, qk_scale)
                _mha_pair_core(p, env, "l0a", hp, QT, KT, VT, PB, None,
                               None, RROW, AVF, BCOL, False, qk_scale,
                               ot_tile=OTP)
                _pair_wo(p, env, "l0a", hp, w_o, OTP, WOS,
                         bias_row=BOR if hp == 0 else None)


def _pair_wo(p, env, tag, hp, w_o, OTP, WOS, bias_row=None):
    nc = p.nc
    s_w = p.sem(tag + "w_w")
    s_mm = p.sem(tag + "w_mm")
    s_ev = p.sem(tag + "w_ev")
    b_w, b_mm, b_ev = s_w.n, s_mm.n, s_ev.n
    with nc.Block() as blk:
        @blk.gpsimd
        def _(g):
            src = w_o[hp * 128:(hp + 1) * 128, :].rearrange("p f -> p f")
            s_w.inc(g.dma_start(WOS[:], src), 16)

        @blk.tensor
        def _(t):
            t.wait_ge(s_w.h, b_w + 16)
            idx = 0
            for ft in range(NCH):
                for nb in range(NNB):
                    ps = bank(env, idx % 4)
                    if idx >= 4:
                        t.wait_ge(s_ev.h, b_ev + idx - 3)
                    mm = t.matmul(ps, WOS[:, ft * 128:(ft + 1) * 128],
                                  OTP[:, nb * NB:(nb + 1) * NB],
                                  start=True, stop=(bias_row is None))
                    if bias_row is not None:
                        mm = t.matmul(ps, bias_row[0:1, ft * 128:(ft + 1) * 128],
                                      env["ones512_bf"][:],
                                      start=False, stop=True)
                    s_mm.inc(mm, 1)
                    idx += 1

        @blk.vector
        def _(v):
            idx = 0
            for ft in range(NCH):
                for nb in range(NNB):
                    v.wait_ge(s_mm.h, b_mm + idx + 1)
                    ps = bank(env, idx % 4)
                    sl = slice(nb * NB, (nb + 1) * NB)
                    s_ev.inc(v.tensor_tensor(env["X"][ft][:, sl],
                                             env["X"][ft][:, sl], ps, ALU.add), 1)
                    idx += 1


def _l0a_pair_proj(p, env, hp, w_uq, w_qr, w_uk, w_kr, w_uv, cosap, sinap,
                   CTQ, CTK, TN, QT, KT, VT, WS, CS, PRM, SWs, BVR):
    nc = p.nc
    s_w = p.sem("l0ap_w")
    s_mm = p.sem("l0ap_mm")
    s_ev = p.sem("l0ap_ev")
    s_pr = p.sem("l0ap_pr")
    s_cb = p.sem("l0ap_cmb")
    b_w, b_mm, b_ev, b_pr, b_cb = s_w.n, s_mm.n, s_ev.n, s_pr.n, s_cb.n
    with nc.Block() as blk:
        @blk.gpsimd
        def _(g):
            for j, (w, kc) in enumerate(((w_uq, 4), (w_qr, 4), (w_uk, 4),
                                         (w_kr, 8), (w_uv, 4))):
                off = (0, 4, 8, 12, 20)[j]
                src = w[:, hp * 128:(hp + 1) * 128].rearrange(
                    "(kc p) f -> p kc f", p=128)
                dst = WS[:, off * 128:(off + kc) * 128].rearrange(
                    "p (kc f) -> p kc f", f=128)
                s_w.inc(g.dma_start(dst, src), 16)
            s_w.inc(g.dma_start(CS[:, 0:N], cosap[hp * 128:(hp + 1) * 128, :]), 16)
            s_w.inc(g.dma_start(CS[:, N:2 * N], sinap[hp * 128:(hp + 1) * 128, :]), 16)

        @blk.tensor
        def _(t):
            t.wait_ge(s_w.h, b_w + 7 * 16)
            for i in (0, 1):
                for nb in range(NNB):
                    u = i * NNB + nb
                    if u >= 1:
                        t.wait_ge(s_cb.h, b_cb + u)
                    sl = slice(nb * NB, (nb + 1) * NB)
                    if i == 0:
                        cw, ckc, cin = 0, 4, CTQ
                        rw_, rkc, rin = 4, 4, CTQ
                    else:
                        cw, ckc, cin = 8, 4, CTK
                        rw_, rkc, rin = 12, 8, TN
                    for ic in range(ckc):
                        t.matmul(bank(env, 0),
                                 WS[:, (cw + ic) * 128:(cw + ic + 1) * 128],
                                 cin[ic][:, sl], start=(ic == 0), stop=(ic == ckc - 1))
                    mm = None
                    for ic in range(rkc):
                        mm = t.matmul(bank(env, 1),
                                      WS[:, (rw_ + ic) * 128:(rw_ + ic + 1) * 128],
                                      rin[ic][:, sl], start=(ic == 0), stop=(ic == rkc - 1))
                    s_mm.inc(mm, 1)
                    t.wait_ge(s_ev.h, b_ev + 2 * u + 1)
                    s_pr.inc(t.matmul(bank(env, 2), PRM[:],
                                      SWs[:, 0:NB], start=True, stop=True), 1)
            for nt in range(16):
                ps = bank(env, 4 + nt % 2)
                if nt >= 2:
                    t.wait_ge(s_ev.h, b_ev + 16 + nt - 1)
                for ic in range(4):
                    t.matmul(ps[:, 0:128],
                             CTK[ic][:, nt * 128:(nt + 1) * 128],
                             WS[:, (20 + ic) * 128:(20 + ic + 1) * 128],
                             start=(ic == 0), stop=False)
                mm = t.matmul(ps[:, 0:128], env["ones_row_bf"][:],
                              BVR[0:1, hp * 128:(hp + 1) * 128],
                              start=False, stop=True)
                s_mm.inc(mm, 1)

        @blk.scalar
        def _(s):
            for u in range(8):
                s.wait_ge(s_mm.h, b_mm + u + 1)
                if u >= 1:
                    s.wait_ge(s_cb.h, b_cb + u)
                bcol = 16 + 16 * (u // 4) + hp
                s_ev.inc(s.activation(SWs[:, 0:NB], bank(env, 1), AF.Identity,
                                      bias=env["wcol"][:, bcol:bcol + 1]), 1)
                s.wait_ge(s_pr.h, b_pr + u + 1)
                s_ev.inc(s.activation(SWs[:, NB:2 * NB], bank(env, 2), AF.Copy), 1)
            for nt in range(16):
                s.wait_ge(s_mm.h, b_mm + 8 + nt + 1)
                ps = bank(env, 4 + nt % 2)
                src = ps[:, 0:128].rearrange("p (h c) -> p h c", c=64)
                dst = VT[nt][:, 0:130].rearrange("p (h c) -> p h c", c=65)[:, :, 0:64]
                s_ev.inc(s.activation(dst, src, AF.Copy), 1)

        @blk.vector
        def _(v):
            t1 = env["scr_f32b"][:, 0:NB // 2].bitcast(BF16)
            t2 = env["scr_f32b"][:, NB // 2:NB].bitcast(BF16)
            for u in range(8):
                i, nb = u // 4, u % 4
                dest = QT if i == 0 else KT
                sl = slice(nb * NB, (nb + 1) * NB)
                v.wait_ge(s_ev.h, b_ev + 2 * u + 2)
                i1 = v.tensor_tensor(t1, SWs[:, 0:NB], CS[:, sl], ALU.mult)
                i2 = v.tensor_tensor(t2, SWs[:, NB:2 * NB],
                                     CS[:, N + nb * NB:N + (nb + 1) * NB], ALU.mult)
                p.chain(v, "vch", i2)
                i3 = v.tensor_tensor(t1, t1, t2, ALU.add)
                p.chain(v, "vch", i3)
                ccol_i = 8 + 16 * (u // 4) + hp
                i4 = v.scalar_tensor_tensor(t2, bank(env, 0),
                                            env["wcol"][:, ccol_i:ccol_i + 1],
                                            t1, ALU.add, ALU.add)
                p.chain(v, "vch", i4)
                i5 = v.tensor_copy(dest[:, sl], t2)
                p.link(v, s_cb, i5)




def emit_block_l1_attn(p, env):
    nc = p.nc
    emit_norm(p, env, kind="ln")
    TN = env["TN"]
    wa = p.dram_in("w_l1a_a", (1024, 512))
    wb = p.dram_in("w_l1a_b", (1024, 512))
    wv = p.dram_in("w_l1a_v", (1024, 1024))
    wo = p.dram_in("w_l1a_o", (1024, 1024))
    with contextlib.ExitStack() as outer:
        osb = lambda nm, sh, dt: outer.enter_context(
            nc.sbuf_tensor("l1a" + nm, list(sh), dt))
        OT = [osb(f"_ot{c}", [128, N], BF16) for c in range(NCH)]
        with contextlib.ExitStack() as scope:
            sb = lambda nm, sh, dt: scope.enter_context(
                nc.sbuf_tensor("l1a" + nm, list(sh), dt))
            AT = [sb(f"_at{i}", [128, N], BF16) for i in range(4)]
            BT = [sb(f"_bt{i}", [128, N], BF16) for i in range(4)]
            VH = sb("_vh", [128, 16 * 64], BF16)
            WVH = sb("_wvh", [128, 8 * 64], BF16)
            EB = env["scr_f32"][:].bitcast(BF16)[:, 0:N]
            WSAB = env["scr_f32b"][:].bitcast(BF16)[:, 0:4096]
            rws = env["rows"]
            dcol = sb("_dcol", [128, 1], F32)
            rcol = sb("_rcol", [128, 1], F32)
            emit_projA(p, env, "l1a_a", wa, 1024, 512, TN, WSAB,
                       scalar_cb=evac_to_tiles(env, AT))
            emit_projA(p, env, "l1a_b", wb, 1024, 512, TN, WSAB,
                       scalar_cb=evac_to_tiles(env, BT))
            for h in range(16):
                _l1a_head(p, env, h, wv, TN, AT, BT, VH, WVH, EB, OT,
                          dcol, rcol)
        with contextlib.ExitStack() as scope2:
            WSO = scope2.enter_context(
                nc.sbuf_tensor("l1a_wso", [128, 8192], BF16))
            emit_projA(p, env, "l1a_wo", wo, 1024, 1024, OT, WSO,
                       vector_cb=evac_residual(env))


def _l1a_head(p, env, h, wv, TN, AT, BT, VH, WVH, EB, OT, dcol, rcol):
    nc = p.nc
    ch, base = h // 4, 32 * (h % 4)
    s_w = p.sem("l1h_w")
    s_vm = p.sem("l1h_vm")
    s_ve = p.sem("l1h_ve")
    s_tm = p.sem("l1h_tm")
    s_ex = p.sem("l1h_ex")
    s_ef = p.sem("l1h_ef")
    s_av = p.sem("l1h_av")
    s_oe = p.sem("l1h_oe")
    b_w, b_vm, b_ve, b_tm, b_ex, b_ef, b_av, b_oe = (
        s_w.n, s_vm.n, s_ve.n, s_tm.n, s_ex.n, s_ef.n, s_av.n, s_oe.n)

    with nc.Block() as blk:
        @blk.gpsimd
        def _(g):
            src = wv[:, h * 64:(h + 1) * 64].rearrange("(kc p) f -> p kc f", p=128)
            s_w.inc(g.dma_start(
                WVH[:].rearrange("p (kc f) -> p kc f", f=64), src), 16)

        @blk.tensor
        def _(t):
            t.wait_ge(s_w.h, b_w + 16)
            for nt in range(16):
                ps = bank(env, nt % 2)
                if nt >= 2:
                    t.wait_ge(s_ve.h, b_ve + nt - 1)
                mm = None
                for ic in range(8):
                    mm = t.matmul(ps[:, 0:64],
                                  TN[ic][:, nt * 128:(nt + 1) * 128],
                                  WVH[:, ic * 64:(ic + 1) * 64],
                                  start=(ic == 0), stop=(ic == 7))
                s_vm.inc(mm, 1)
            for mt in range(16):
                t.wait_ge(s_ve.h, b_ve + 16)
                if mt >= 1:
                    t.wait_ge(s_ex.h, b_ex + mt)
                mm = None
                for nb in range(NNB):
                    mm = t.matmul(bank(env, nb),
                                  BT[ch][base:base + 32,
                                         mt * 128:(mt + 1) * 128],
                                  AT[ch][base:base + 32,
                                         nb * NB:(nb + 1) * NB],
                                  start=True, stop=True,
                                  tile_position=(base, 0))
                s_tm.inc(mm, 1)
                t.wait_ge(s_ef.h, b_ef + mt + 1)
                mm = None
                for nb in range(NNB):
                    mm = t.matmul(bank(env, 4 + nb)[0:64, :],
                                  VH[:, mt * 64:(mt + 1) * 64],
                                  EB[:, nb * NB:(nb + 1) * NB],
                                  start=(mt == 0), stop=(mt == 15))
                s_av.inc(mm, 1)

        @blk.scalar
        def _(s):
            for nt in range(16):
                s.wait_ge(s_vm.h, b_vm + nt + 1)
                ps = bank(env, nt % 2)
                s_ve.inc(s.activation(VH[:, nt * 64:(nt + 1) * 64],
                                      ps[:, 0:64], AF.Copy), 1)
            for mt in range(16):
                s.wait_ge(s_tm.h, b_tm + mt + 1)
                if mt >= 1:
                    s.wait_ge(s_av.h, b_av + mt)
                s_ex.inc(s.activation(EB[:], env["PSA"][0][:, :], AF.Exp,
                                      scale=0.25, accum_out=dcol[:]), 1)
            s.wait_ge(s_av.h, b_av + 16)
            oc = OT[h // 2]
            off = 64 * (h % 2)
            for nb in range(NNB):
                s_oe.inc(s.activation(oc[off:off + 64, nb * NB:(nb + 1) * NB],
                                      bank(env, 4 + nb)[0:64, :], AF.Copy), 1)

        @blk.vector
        def _(v):
            for mt in range(16):
                v.wait_ge(s_ex.h, b_ex + mt + 1)
                i1 = v.reciprocal(rcol[:], dcol[:])
                p.chain(v, "vch", i1)
                i2 = v.tensor_scalar(EB[:], EB[:], rcol[:], 1.0,
                                     ALU.mult, ALU.mult)
                p.link(v, s_ef, i2)




def emit_block_l1_ffn(p, env):
    nc = p.nc
    emit_norm(p, env, kind="ln")
    TN = env["TN"]
    sg = p.dram_in("w_l1f_sg", (1024, 1024))
    su = p.dram_in("w_l1f_su", (1024, 1024))
    sd = p.dram_in("w_l1f_sd", (1024, 1024))
    wr = p.dram_in("w_l1f_r", (1024, 16))
    wg = p.dram_in("w_l1f_g", (16, 1024, 512))
    wu = p.dram_in("w_l1f_u", (16, 1024, 512))
    wd = p.dram_in("w_l1f_d", (16, 512, 1024))
    ohbc = p.dram_in("c_ohbc", (16, 16 * 128))
    with contextlib.ExitStack() as scope:
        sb = lambda nm, sh, dt: scope.enter_context(
            nc.sbuf_tensor("l1f" + nm, list(sh), dt))
        with contextlib.ExitStack() as sc1:
            HUV = [sc1.enter_context(nc.sbuf_tensor(f"l1f_huv{i}", [128, N], BF16))
                   for i in range(NCH)]
            WSH = sc1.enter_context(nc.sbuf_tensor("l1f_wsh", [128, 16384], BF16))
            _l1f_shared(p, env, sg, su, TN, HUV, WSH)
            emit_projA(p, env, "l1f_sd", sd, 1024, 1024, HUV, WSH,
                       vector_cb=evac_residual(env))
        CR = sb("_cr", [16, N], BF16)
        ROUT = sb("_rout", [128, 64], F32)
        TMPG = sb("_tmpg", [128, 1024], BF16)
        with nc.Block() as blk0:
            @blk0.vector
            def _(v):
                v.memset(CR[:, :], 0.0)
        _l1f_router(p, env, scope, wr, TN, CR, ROUT)
        WSE = sb("_wse", [128, 12 * 1024], BF16)
        HGE = sb("_hge", [128, 4 * NB], BF16)
        CBC = sb("_cbc", [128, NB], F32)
        OHB = sb("_ohb", [16, 128], BF16)
        for e in range(16):
            _l1f_expert(p, env, e, wg, wu, wd, ohbc, TN, CR, WSE, HGE,
                        CBC, OHB, TMPG)


def _l1f_shared(p, env, sg, su, TN, HUV, WSH):
    nc = p.nc
    s_w = p.sem("l1s_w")
    s_mm = p.sem("l1s_mm")
    s_si = p.sem("l1s_si")
    s_hv = p.sem("l1s_hv")
    b_w, b_mm, b_si, b_hv = s_w.n, s_mm.n, s_si.n, s_hv.n
    with nc.Block() as blk:
        @blk.gpsimd
        def _(g):
            for j, w in enumerate((sg, su)):
                src = w.rearrange("(kc p) f -> p kc f", p=128)
                dst = WSH[:, j * 8192:(j + 1) * 8192].rearrange(
                    "p (kc f) -> p kc f", f=1024)
                s_w.inc(g.dma_start(dst, src), 16)

        @blk.tensor
        def _(t):
            t.wait_ge(s_w.h, b_w + 32)
            idx = 0
            for ft in range(NCH):
                for nb in range(NNB):
                    if idx >= 2:
                        t.wait_ge(s_hv.h, b_hv + idx - 1)
                    for j in range(2):
                        ps = bank(env, 2 * (idx % 2) + j)
                        mm = None
                        for ic in range(8):
                            mm = t.matmul(
                                ps,
                                WSH[:, j * 8192 + ic * 1024 + ft * 128:
                                    j * 8192 + ic * 1024 + (ft + 1) * 128],
                                TN[ic][:, nb * NB:(nb + 1) * NB],
                                start=(ic == 0), stop=(ic == 7))
                        s_mm.inc(mm, 1)
                    idx += 1

        @blk.scalar
        def _(s):
            idx = 0
            for ft in range(NCH):
                for nb in range(NNB):
                    s.wait_ge(s_mm.h, b_mm + 2 * idx + 1)
                    psg = bank(env, 2 * (idx % 2))
                    tmp = env["scr_f32"][:, 0:NB].bitcast(BF16)[:, (idx % 2) * NB:
                                                                 (idx % 2) * NB + NB]
                    s_si.inc(s.activation(tmp, psg, AF.Silu), 1)
                    idx += 1

        @blk.vector
        def _(v):
            idx = 0
            for ft in range(NCH):
                for nb in range(NNB):
                    v.wait_ge(s_si.h, b_si + idx + 1)
                    v.wait_ge(s_mm.h, b_mm + 2 * idx + 2)
                    psu = bank(env, 2 * (idx % 2) + 1)
                    tmp = env["scr_f32"][:, 0:NB].bitcast(BF16)[:, (idx % 2) * NB:
                                                                 (idx % 2) * NB + NB]
                    s_hv.inc(v.tensor_tensor(HUV[ft][:, nb * NB:(nb + 1) * NB],
                                             tmp, psu, ALU.mult), 1)
                    idx += 1


def _l1f_router(p, env, scope, wr, TN, CR, ROUT):
    nc = p.nc
    idf = p.dram_in("c_identity", (128, 128))
    WR = scope.enter_context(nc.sbuf_tensor("l1f_wr", [128, 128], BF16))
    IDF = scope.enter_context(nc.sbuf_tensor("l1f_id", [128, 128], F32))
    m1 = ROUT[:, 16:17]
    nm1 = ROUT[:, 17:18]
    den = ROUT[:, 18:19]
    gcol = ROUT[:, 19:20]
    m2 = ROUT[:, 20:21]
    prb = ROUT[:, 32:48]
    is12 = ROUT[:, 0:16]
    ctok = ROUT[:, 0:16]
    s_w = p.sem("l1r_w")
    s_mm = p.sem("l1r_mm")
    s_v1 = p.sem("l1r_v1")
    s_sc = p.sem("l1r_sc")
    s_v2 = p.sem("l1r_v2")
    s_tr = p.sem("l1r_tr")
    s_cp = p.sem("l1r_cp")
    b_w, b_mm, b_v1, b_sc, b_v2, b_tr, b_cp = (
        s_w.n, s_mm.n, s_v1.n, s_sc.n, s_v2.n, s_tr.n, s_cp.n)

    with nc.Block() as blk:
        @blk.gpsimd
        def _(g):
            s_w.inc(g.dma_start(
                WR[:].rearrange("p (kc f) -> p kc f", f=16),
                wr.rearrange("(kc p) f -> p kc f", p=128)), 16)

        @blk.sync
        def _(sy):
            s_w.inc(sy.dma_start(IDF[:], idf[:, :]), 16)

        @blk.tensor
        def _(t):
            t.wait_ge(s_w.h, b_w + 32)
            for nt in range(16):
                ps = bank(env, nt % 2)
                if nt >= 2:
                    t.wait_ge(s_v1.h, b_v1 + nt - 1)
                mm = None
                for ic in range(8):
                    mm = t.matmul(ps[:, 0:16], TN[ic][:, nt * 128:(nt + 1) * 128],
                                  WR[:, ic * 16:(ic + 1) * 16],
                                  start=(ic == 0), stop=(ic == 7))
                s_mm.inc(mm, 1)
                t.wait_ge(s_v2.h, b_v2 + nt + 1)
                ps2 = bank(env, 4 + nt % 2)
                s_tr.inc(t.transpose(ps2[0:16, 0:128], ctok[:, 0:16], IDF[:]), 1)

        @blk.vector
        def _(v):
            for nt in range(16):
                v.wait_ge(s_mm.h, b_mm + nt + 1)
                ps = bank(env, nt % 2)
                i1 = v.tensor_reduce(m1, ps[:, 0:16], axis=AX, op=ALU.max)
                p.chain(v, "vch", i1)
                i3 = v.tensor_scalar(nm1, m1, -1.0, 0.0, ALU.mult, ALU.add)
                p.link(v, s_v1, i3)
                v.wait_ge(s_sc.h, b_sc + nt + 1)
                i4 = v.reciprocal(gcol, den)
                p.chain(v, "vch", i4)
                i5 = v.tensor_scalar(prb, prb, gcol, 1.0, ALU.mult, ALU.mult)
                p.chain(v, "vch", i5)
                i6 = v.tensor_reduce(m1, prb, axis=AX, op=ALU.max)
                p.chain(v, "vch", i6)
                i7 = v.tensor_scalar(is12, prb, m1, 1.0, ALU.is_ge, ALU.mult)
                p.chain(v, "vch", i7)
                msk16 = ROUT[:, 48:64]
                i8 = v.scalar_tensor_tensor(msk16, is12, -10.0,
                                            prb, ALU.mult, ALU.add)
                p.chain(v, "vch", i8)
                i9 = v.tensor_reduce(m2, msk16, axis=AX, op=ALU.max)
                p.chain(v, "vch", i9)
                i10 = v.tensor_scalar(msk16, msk16, m2, 1.0,
                                      ALU.is_ge, ALU.mult)
                p.chain(v, "vch", i10)
                i11 = v.tensor_tensor(is12, is12, msk16, ALU.add)
                p.chain(v, "vch", i11)
                v.wait_ge(s_tr.h, b_tr + nt)
                i12 = v.tensor_tensor(ctok[:, 0:16], is12, prb, ALU.mult)
                p.link(v, s_v2, i12)

        @blk.scalar
        def _(s):
            for nt in range(16):
                s.wait_ge(s_v1.h, b_v1 + nt + 1)
                ps = bank(env, nt % 2)
                s_sc.inc(s.activation(prb, ps[:, 0:16], AF.Exp,
                                      bias=nm1, accum_out=den), 1)
                s.wait_ge(s_tr.h, b_tr + nt + 1)
                ps2 = bank(env, 4 + nt % 2)
                s_cp.inc(s.activation(CR[0:16, nt * 128:(nt + 1) * 128],
                                      ps2[0:16, 0:128], AF.Copy), 1)


def _l1f_expert(p, env, e, wg, wu, wd, ohbc, TN, CR, WSE, HGE, CBC, OHB,
                TMPG):
    nc = p.nc
    s_w = p.sem("l1e_w")
    s_mm = p.sem("l1e_mm")
    s_si = p.sem("l1e_si")
    s_hv = p.sem("l1e_hv")
    s_cb = p.sem("l1e_cb")
    s_cc = p.sem("l1e_cc")
    s_m2 = p.sem("l1e_m2")
    s_tm = p.sem("l1e_tm")
    s_ad = p.sem("l1e_ad")
    b_w, b_mm, b_si, b_hv, b_cb, b_cc, b_m2, b_tm, b_ad = (
        s_w.n, s_mm.n, s_si.n, s_hv.n, s_cb.n, s_cc.n, s_m2.n, s_tm.n, s_ad.n)
    with nc.Block() as blk:
        @blk.gpsimd
        def _(g):
            for j, w in enumerate((wg, wu)):
                src = w[e].rearrange("(kc p) f -> p kc f", p=128)
                dst = WSE[:, j * 4096:(j + 1) * 4096].rearrange(
                    "p (kc f) -> p kc f", f=512)
                s_w.inc(g.dma_start(dst, src), 16)
            src = wd[e].rearrange("(kc p) f -> p kc f", p=128)
            dst = WSE[:, 8192:12288].rearrange("p (kc f) -> p kc f", f=1024)
            s_w.inc(g.dma_start(dst, src), 16)
            s_w.inc(g.dma_start(OHB[:], ohbc[0:16, e * 128:(e + 1) * 128]), 16)
            for nb in range(NNB):
                for dt in range(NCH):
                    g.wait_ge(s_tm.h, b_tm + nb * NCH + dt + 1)
                    tmp = (env["scr_f32"] if dt < 4 else env["scr_f32b"])[
                        :, (dt % 4) * NB:(dt % 4 + 1) * NB]
                    sl = slice(nb * NB, (nb + 1) * NB)
                    s_ad.inc(g.tensor_tensor(env["X"][dt][:, sl],
                                             env["X"][dt][:, sl], tmp,
                                             ALU.add), 1)

        @blk.tensor
        def _(t):
            t.wait_ge(s_w.h, b_w + 64)
            for nb in range(NNB):
                for ft in range(4):
                    u = nb * 4 + ft
                    if u >= 2:
                        t.wait_ge(s_hv.h, b_hv + u - 1)
                    for j in range(2):
                        ps = bank(env, 2 * (u % 2) + j)
                        mm = None
                        for ic in range(8):
                            mm = t.matmul(
                                ps,
                                WSE[:, j * 4096 + ic * 512 + ft * 128:
                                    j * 4096 + ic * 512 + (ft + 1) * 128],
                                TN[ic][:, nb * NB:(nb + 1) * NB],
                                start=(ic == 0), stop=(ic == 7))
                        s_mm.inc(mm, 1)
                if nb >= 1:
                    t.wait_ge(s_cc.h, b_cc + nb)
                s_cb.inc(t.matmul(bank(env, 4), OHB[:],
                                  CR[:, nb * NB:(nb + 1) * NB],
                                  start=True, stop=True), 1)
                t.wait_ge(s_hv.h, b_hv + (nb + 1) * 4)
                for dt in range(NCH):
                    idx2 = nb * NCH + dt
                    ps = bank(env, 5 + idx2 % 2)
                    if idx2 >= 2:
                        t.wait_ge(s_tm.h, b_tm + idx2 - 1)
                    mm = None
                    for kt in range(4):
                        mm = t.matmul(
                            ps, WSE[:, 8192 + kt * 1024 + dt * 128:
                                    8192 + kt * 1024 + (dt + 1) * 128],
                            HGE[:, kt * NB:(kt + 1) * NB],
                            start=(kt == 0), stop=(kt == 3))
                    s_m2.inc(mm, 1)

        @blk.scalar
        def _(s):
            for nb in range(NNB):
                for ft in range(4):
                    u = nb * 4 + ft
                    s.wait_ge(s_mm.h, b_mm + 2 * u + 1)
                    if nb >= 1 and ft == 0:
                        s.wait_ge(s_m2.h, b_m2 + nb * NCH)
                    psg = bank(env, 2 * (u % 2))
                    tmp = TMPG[:, (u % 2) * NB:(u % 2) * NB + NB]
                    s_si.inc(s.activation(tmp, psg, AF.Silu), 1)
                s.wait_ge(s_cb.h, b_cb + nb + 1)
                if nb >= 1:
                    s.wait_ge(s_tm.h, b_tm + nb * NCH)
                s_cc.inc(s.activation(CBC[:], bank(env, 4), AF.Copy), 1)

        @blk.vector
        def _(v):
            for nb in range(NNB):
                for ft in range(4):
                    u = nb * 4 + ft
                    v.wait_ge(s_si.h, b_si + u + 1)
                    v.wait_ge(s_mm.h, b_mm + 2 * u + 2)
                    if nb >= 1 and ft == 0:
                        v.wait_ge(s_m2.h, b_m2 + nb * NCH)
                    psu = bank(env, 2 * (u % 2) + 1)
                    tmp = TMPG[:, (u % 2) * NB:(u % 2) * NB + NB]
                    s_hv.inc(v.tensor_tensor(HGE[:, ft * NB:(ft + 1) * NB],
                                             tmp, psu, ALU.mult), 1)
                v.wait_ge(s_cc.h, b_cc + nb + 1)
                for dt in range(NCH):
                    idx2 = nb * NCH + dt
                    v.wait_ge(s_m2.h, b_m2 + idx2 + 1)
                    if nb >= 1:
                        v.wait_ge(s_ad.h, b_ad + nb * NCH)
                    ps = bank(env, 5 + idx2 % 2)
                    tmp2 = (env["scr_f32"] if dt < 4 else env["scr_f32b"])[
                        :, (dt % 4) * NB:(dt % 4 + 1) * NB]
                    s_tm.inc(v.tensor_tensor(tmp2, ps, CBC[:], ALU.mult), 1)




def emit_shift_to_tn(p, env):
    nc = p.nc
    s_v = p.sem("sh_v")
    with nc.Block() as blk:
        @blk.vector
        def _(v):
            for c in range(NCH):
                if c < 4:
                    v.tensor_copy(env["TN"][c][:], env["X"][c][:])
                else:
                    v.memset(env["TN"][c][:, 0:1], 0.0)
                    v.tensor_copy(env["TN"][c][:, 1:N], env["X"][c][:, 0:N - 1])


def emit_norm_tn_affine(p, env, g_ap, b_ap):
    nc = p.nc
    TN = env["TN"]
    sq, sq2 = env["scr_f32"], env["scr_f32b"]
    rows = env["rows"]
    mu = rows[0:1, 0:N]
    mu2 = rows[0:1, N:2 * N]
    var = rows[0:1, N:2 * N]
    sdev = rows[0:1, N:2 * N]
    rstd = rows[0:1, N:2 * N]
    brow = rows[0:1, 0:N]
    ones_col = ccol(env, "one")
    ones_bf = env["ones_bf"]
    orow = env["ones_row128"]
    s_w = p.sem("na_w")
    s_sq = p.sem("na_sq")
    s_mm = p.sem("na_mm")
    s_rv1 = p.sem("na_rv1")
    s_rsc = p.sem("na_rsc")
    s_rv2 = p.sem("na_rv2")
    s_bc = p.sem("na_bc")
    s_ap = p.sem("na_ap")
    b_w, b_sq, b_mm, b_rv1, b_rsc, b_rv2, b_bc, b_apl = (
        s_w.n, s_sq.n, s_mm.n, s_rv1.n, s_rsc.n, s_rv2.n, s_bc.n, s_ap.n)
    ps_sq = env["PSA"][0][0:1, :]
    ps_sx = env["PSA"][1][0:1, :]

    with nc.Block() as blk:
        @blk.sync
        def _(sy):
            s_w.inc(sy.dma_start(env["wcol"][:, 0:8], g_ap), 16)
            s_w.inc(sy.dma_start(env["wcol"][:, 8:16], b_ap), 16)

        @blk.scalar
        def _(s):
            for c in range(NCH):
                buf = sq if c % 2 == 0 else sq2
                if c >= 2:
                    s.wait_ge(s_mm.h, b_mm + c - 1)
                s_sq.inc(s.activation(buf[:], TN[c][:], AF.Square), 1)
            s.wait_ge(s_rv1.h, b_rv1 + 1)
            s_rsc.inc(s.activation(sdev, var, AF.Sqrt,
                                   bias=ccol(env, "eps5")[0:1, :]), 1)

        @blk.tensor
        def _(t):
            for c in range(NCH):
                buf = sq if c % 2 == 0 else sq2
                t.wait_ge(s_sq.h, b_sq + c + 1)
                mm = None
                for nb in range(NNB):
                    t.matmul(ps_sx[:, nb * NB:(nb + 1) * NB], ones_bf[:],
                             TN[c][:, nb * NB:(nb + 1) * NB],
                             start=(c == 0), stop=(c == NCH - 1))
                    mm = t.matmul(ps_sq[:, nb * NB:(nb + 1) * NB], ones_col,
                                  buf[:, nb * NB:(nb + 1) * NB],
                                  start=(c == 0), stop=(c == NCH - 1))
                s_mm.inc(mm, 1)
            t.wait_ge(s_rv2.h, b_rv2 + 1)
            for nb in range(NNB):
                pa = bank(env, (nb % 2) * 4 + 0)
                pb = bank(env, (nb % 2) * 4 + 1)
                if nb >= 2:
                    t.wait_ge(s_ap.h, b_apl + nb - 1)
                s_bc.inc(t.matmul(pa, orow[:], rstd[:, nb * NB:(nb + 1) * NB],
                                  start=True, stop=True), 1)
                s_bc.inc(t.matmul(pb, orow[:], brow[:, nb * NB:(nb + 1) * NB],
                                  start=True, stop=True), 1)

        @blk.vector
        def _(v):
            inv_d = 1.0 / D
            v.wait_ge(s_w.h, b_w + 32)
            v.wait_ge(s_mm.h, b_mm + NCH)
            p.chain(v, "vch", v.tensor_scalar(mu, ps_sx, inv_d, 0.0,
                                              ALU.mult, ALU.add))
            p.chain(v, "vch", v.tensor_tensor(mu2, mu, mu, ALU.mult))
            i3 = v.scalar_tensor_tensor(var, ps_sq, inv_d, mu2,
                                        ALU.mult, ALU.subtract)
            p.chain(v, "vch", i3)
            s_rv1.inc(v.nop(), 1)
            v.wait_ge(s_rsc.h, b_rsc + 1)
            p.chain(v, "vch", v.reciprocal(rstd, sdev))
            p.chain(v, "vch", v.scalar_tensor_tensor(brow, mu, -1.0, rstd,
                                                     ALU.mult, ALU.mult))
            s_rv2.inc(v.nop(), 1)
            for nb in range(NNB):
                pa = bank(env, (nb % 2) * 4 + 0)
                pb = bank(env, (nb % 2) * 4 + 1)
                v.wait_ge(s_bc.h, b_bc + (nb + 1) * 2)
                sl = slice(nb * NB, (nb + 1) * NB)
                tmps = [(sq if c < 4 else sq2)[:, (c % 4) * NB:(c % 4 + 1) * NB]
                        for c in range(NCH)]
                last = None
                for c in range(NCH):
                    last = v.tensor_tensor(tmps[c], TN[c][:, sl], pa, ALU.mult)
                p.chain(v, "vch", last)
                last = None
                for c in range(NCH):
                    last = v.tensor_tensor(TN[c][:, sl], tmps[c], pb, ALU.add)
                p.chain(v, "vch", last)
                last = None
                for c in range(NCH):
                    last = v.tensor_scalar(TN[c][:, sl], TN[c][:, sl],
                                           env["wcol"][:, c:c + 1],
                                           env["wcol"][:, 8 + c:9 + c],
                                           ALU.mult, ALU.add)
                p.chain(v, "vch", last)
                s_ap.inc(v.nop(), 1)


def emit_block_l3_ffn(p, env):
    nc = p.nc
    emit_spill_x(p, env, name="xsave3")
    emit_norm(p, env, kind="ln", out="X")
    for l in range(2):
        _l3f_layer(p, env, l)
    emit_restore_add_x(p, env, name="xsave3")


def _l3f_layer(p, env, l):
    nc = p.nc
    wqkv = p.dram_in("w_l3f_qkv", (2, 1024, 1536))
    wo = p.dram_in("w_l3f_o", (2, 512, 1024))
    w1 = p.dram_in("w_l3f_w1", (2, 1024, 5632))
    w2 = p.dram_in("w_l3f_w2", (2, 2816, 1024))
    b1a = p.dram_in("b_l3f_b1a", (2, 128, 22))
    b1g = p.dram_in("b_l3f_b1g", (2, 128, 22))
    b2r = p.dram_in("b2row_l3f", (2, 1024))
    qsap = p.dram_in("w_l3f_qs", (2, 128, 1))
    ksap = p.dram_in("w_l3f_ks", (2, 128, 1))
    lng = p.dram_in("w_l3f_lng", (2, 128, 8))
    lnb = p.dram_in("w_l3f_lnb", (2, 128, 8))
    TN = env["TN"]

    emit_shift_to_tn(p, env)

    with contextlib.ExitStack() as scope:
        sb = lambda nm, sh, dt: scope.enter_context(
            nc.sbuf_tensor(f"l3f{l}" + nm, list(sh), dt))
        QTp = [sb(f"_qt{i}", [128, N], BF16) for i in range(4)]
        KTp = [sb(f"_kt{i}", [128, N], BF16) for i in range(4)]
        VT = [sb(f"_vt{i}", [128, 520], BF16) for i in range(16)]
        PB = sb("_pb", [128, 8 * NB + NB], BF16)
        OTL = env["TN"][0:4]
        TBL = env["TN"][4:8]
        BCOL = sb("_bc", [128, 32], F32)
        PRM = sb("_prm", [128, 128], BF16)
        rws = env["rows"]
        RROW = rws[0:1, 0:NB]
        AVF = PB[0:64, 8 * NB:9 * NB]

        with contextlib.ExitStack() as sc1:
            STG = sc1.enter_context(nc.sbuf_tensor(f"l3f{l}_stg", [128, 4096], BF16))
            s_c = p.sem("l3f_c")
            b_c = s_c.n
            with nc.Block() as blk:
                @blk.gpsimd
                def _(g):
                    s_c.inc(g.dma_start(PRM[:], p.dram_in("c_swap32", (128, 128))[:, :]), 16)

                @blk.sync
                def _(sy):
                    s_c.inc(sy.dma_start(env["wcol"][:, 30:31], qsap[l, :, :]), 16)
                    s_c.inc(sy.dma_start(env["wcol"][:, 31:32], ksap[l, :, :]), 16)

                @blk.vector
                def _(v):
                    for i in range(16):
                        for hh in range(8):
                            v.memset(VT[i][:, 65 * hh + 64:65 * hh + 65], 1.0)
                    v.wait_ge(s_c.h, b_c + 3 * 16)
            emit_projA(p, env, "l3f_q", wqkv[l], 1024, 512, TN, STG,
                       scalar_cb=evac_to_tiles(env, QTp))
            emit_projA(p, env, "l3f_k", wqkv[l], 1024, 512, TN, STG,
                       scalar_cb=evac_to_tiles(env, KTp), w_col0=512)
            _l3f_vproj(p, env, l, wqkv, TN, VT, STG)
            s_c2 = p.sem("l3f_c2")
            b_c2 = s_c2.n
            with nc.Block() as blk:
                @blk.gpsimd
                def _(g):
                    for j, nm in enumerate(("c_l3f_cq", "c_l3f_sq",
                                            "c_l3f_ck", "c_l3f_sk")):
                        s_c2.inc(g.dma_start(TBL[j][:], p.dram_in(nm, (128, N))[:, :]), 16)
                    g.wait_ge(s_c2.h, b_c2 + 64)

        SQA = env["scr_f32"][:].bitcast(BF16)[:, 0:N]
        SQK = env["scr_f32"][:].bitcast(BF16)[:, N:2 * N]
        for c in range(4):
            _l3f_qk_l2n(p, env, c, QTp[c], env["wcol"][:, 30:31])
            _l3f_qk_l2n(p, env, c, KTp[c], env["wcol"][:, 31:32])
            _l3f_bounds(p, env, c, QTp[c], KTp[c], SQA, SQK, BCOL)
            _l3f_qk_xpos(p, env, c, QTp[c], TBL[0], TBL[1], PRM)
            _l3f_qk_xpos(p, env, c, KTp[c], TBL[2], TBL[3], PRM)

        s_m = p.sem("l3f_m")
        b_m = s_m.n
        mask_ap = p.dram_in("c_l3f_mask", (1024, 512))
        with nc.Block() as blk:
            @blk.sync
            def _(sy):
                s_m.inc(sy.dma_start(
                    env["scr_f32"][:].rearrange("p (r q) -> p r q", r=4),
                    mask_ap[0:512, :].rearrange("(r p) q -> p r q", p=128)), 16)
                s_m.inc(sy.dma_start(
                    env["scr_f32b"][:].rearrange("p (r q) -> p r q", r=4),
                    mask_ap[512:1024, :].rearrange("(r p) q -> p r q", p=128)), 16)
                sy.wait_ge(s_m.h, b_m + 32)

        for h in range(8):
            _l3f_head(p, env, h, QTp, KTp, VT, PB, OTL, BCOL, RROW, AVF)

        with contextlib.ExitStack() as sc2:
            WSO = sc2.enter_context(nc.sbuf_tensor(f"l3f{l}_wso", [128, 4096], BF16))
            emit_projA(p, env, "l3f_wo", wo[l], 512, 1024, OTL, WSO,
                       vector_cb=evac_residual(env))

    emit_shift_to_tn(p, env)
    emit_norm_tn_affine(p, env, lng[l, :, :], lnb[l, :, :])
    _l3f_mlp(p, env, l, w1, w2, b1a, b1g, b2r)


def _l3f_vproj(p, env, l, wqkv, TN, VT, STG):
    nc = p.nc
    s_w = p.sem("l3v_w")
    s_mm = p.sem("l3v_mm")
    s_ev = p.sem("l3v_ev")
    b_w, b_mm, b_ev = s_w.n, s_mm.n, s_ev.n
    with nc.Block() as blk:
        @blk.gpsimd
        def _(g):
            src = wqkv[l][:, 1024:1536].rearrange("(kc p) f -> p kc f", p=128)
            dst = STG[:, 0:4096].rearrange("p (kc f) -> p kc f", f=512)
            s_w.inc(g.dma_start(dst, src), 16)

        @blk.tensor
        def _(t):
            t.wait_ge(s_w.h, b_w + 16)
            for nt in range(16):
                ps = bank(env, nt % 2)
                if nt >= 2:
                    t.wait_ge(s_ev.h, b_ev + nt - 1)
                mm = None
                for ic in range(8):
                    mm = t.matmul(ps, TN[ic][:, nt * 128:(nt + 1) * 128],
                                  STG[:, ic * 512:(ic + 1) * 512],
                                  start=(ic == 0), stop=(ic == 7))
                s_mm.inc(mm, 1)

        @blk.scalar
        def _(s):
            for nt in range(16):
                s.wait_ge(s_mm.h, b_mm + nt + 1)
                ps = bank(env, nt % 2)
                src = ps[:, :].rearrange("p (h c) -> p h c", c=64)
                dst = VT[nt][:, 0:520].rearrange("p (h c) -> p h c", c=65)[:, :, 0:64]
                s_ev.inc(s.activation(dst, src, AF.Copy), 1)


def _l3f_qk_l2n(p, env, c, QC, scol):
    nc = p.nc
    rws = env["rows"]
    s_sq = p.sem("l3n_sq")
    s_nm = p.sem("l3n_nm")
    s_sc = p.sem("l3n_sc")
    s_rv = p.sem("l3n_rv")
    s_bc = p.sem("l3n_bc")
    s_l2 = p.sem("l3n_l2")
    b_sq, b_nm, b_sc, b_rv, b_bc, b_l2 = (s_sq.n, s_nm.n, s_sc.n, s_rv.n,
                                          s_bc.n, s_l2.n)
    SQB = env["scr_f32"][:].bitcast(BF16)[:, 0:N]
    n0 = rws[0:1, 0:N]
    n1 = rws[0:1, N:2 * N]
    ones_bf = env["ones_bf"]
    orow = env["ones_row128"]
    with nc.Block() as blk:
        @blk.scalar
        def _(s):
            s_sq.inc(s.activation(SQB[:], QC[:], AF.Square), 1)
            s.wait_ge(s_nm.h, b_nm + 1)
            s_sc.inc(s.activation(n0, env["PSA"][0][0:1, :], AF.Sqrt), 1)
            s_sc.inc(s.activation(n1, env["PSA"][1][0:1, :], AF.Sqrt), 1)

        @blk.tensor
        def _(t):
            t.wait_ge(s_sq.h, b_sq + 1)
            mm = None
            for ai, base in ((0, 0), (1, 64)):
                for nb in range(NNB):
                    mm = t.matmul(env["PSA"][ai][0:1, nb * NB:(nb + 1) * NB],
                                  ones_bf[base:base + 64, :],
                                  SQB[base:base + 64, nb * NB:(nb + 1) * NB],
                                  start=True, stop=True,
                                  tile_position=(base, 0))
            s_nm.inc(mm, 1)
            t.wait_ge(s_rv.h, b_rv + 1)
            for nb in range(NNB):
                if nb >= 2:
                    t.wait_ge(s_l2.h, b_l2 + 2 * (nb - 1))
                s_bc.inc(t.matmul(bank(env, 2 + nb % 2)[0:64, :],
                                  orow[:, 0:64], n0[:, nb * NB:(nb + 1) * NB],
                                  start=True, stop=True), 1)
                s_bc.inc(t.matmul(bank(env, 4 + nb % 2)[0:64, :],
                                  orow[:, 0:64], n1[:, nb * NB:(nb + 1) * NB],
                                  start=True, stop=True), 1)

        @blk.vector
        def _(v):
            v.wait_ge(s_sc.h, b_sc + 2)
            i1 = v.tensor_scalar(n0, n0, 1e-12, 1.0, ALU.max, ALU.mult)
            i2 = v.tensor_scalar(n1, n1, 1e-12, 1.0, ALU.max, ALU.mult)
            p.chain(v, "vch", i2)
            i3 = v.reciprocal(n0, n0)
            i4 = v.reciprocal(n1, n1)
            p.link(v, s_rv, i4)
            for nb in range(NNB):
                v.wait_ge(s_bc.h, b_bc + 2 * (nb + 1))
                sl = slice(nb * NB, (nb + 1) * NB)
                ia = v.tensor_tensor(QC[0:64, sl], QC[0:64, sl],
                                     bank(env, 2 + nb % 2)[0:64, :], ALU.mult)
                ib = v.tensor_tensor(QC[64:128, sl], QC[64:128, sl],
                                     bank(env, 4 + nb % 2)[0:64, :], ALU.mult)
                p.chain(v, "vch", ib)
                ig = v.tensor_scalar(QC[:, sl], QC[:, sl], scol, 1.0,
                                     ALU.mult, ALU.mult)
                s_l2.inc(ia, 1)
                p.link(v, s_l2, ig)


def _l3f_qk_xpos(p, env, c, QC, CTB, STB, PRM):
    nc = p.nc
    s_pr = p.sem("l3x_pr")
    s_sw = p.sem("l3x_sw")
    s_cb = p.sem("l3x_cb")
    b_pr, b_sw, b_cb = s_pr.n, s_sw.n, s_cb.n
    with nc.Block() as blk:
        @blk.tensor
        def _(t):
            for nb in range(NNB):
                if nb >= 2:
                    t.wait_ge(s_sw.h, b_sw + nb - 1)
                s_pr.inc(t.matmul(bank(env, 6 + nb % 2), PRM[:],
                                  QC[:, nb * NB:(nb + 1) * NB],
                                  start=True, stop=True), 1)

        @blk.scalar
        def _(s):
            for nb in range(NNB):
                s.wait_ge(s_pr.h, b_pr + nb + 1)
                if nb >= 2:
                    s.wait_ge(s_cb.h, b_cb + nb - 1)
                s_sw.inc(s.activation(env["scr_f32b"][:, 0:NB].bitcast(BF16)
                                      [:, (nb % 2) * NB:(nb % 2) * NB + NB],
                                      bank(env, 6 + nb % 2), AF.Copy), 1)

        @blk.vector
        def _(v):
            t1 = env["scr_f32"][:, 0:NB // 2].bitcast(BF16)
            for nb in range(NNB):
                v.wait_ge(s_sw.h, b_sw + nb + 1)
                sl = slice(nb * NB, (nb + 1) * NB)
                sw = env["scr_f32b"][:, 0:NB].bitcast(BF16)[
                    :, (nb % 2) * NB:(nb % 2) * NB + NB]
                ia = v.tensor_tensor(t1, QC[:, sl], CTB[:, sl], ALU.mult)
                ib = v.tensor_tensor(QC[:, sl], sw, STB[:, sl], ALU.mult)
                p.chain(v, "vch", ib)
                ic_ = v.tensor_tensor(QC[:, sl], QC[:, sl], t1, ALU.add)
                p.link(v, s_cb, ic_)


def _l3f_bounds(p, env, c, QC, KC, SQA, SQK, BCOL):
    nc = p.nc
    rws = env["rows"]
    c0 = 2 * N
    mq = rws[0:1, c0:c0 + 8]
    mk = rws[0:1, c0 + 8:c0 + 16]
    mkw = rws[0:1, c0 + 16:c0 + 24]
    prod = rws[0:1, c0 + 24:c0 + 32]
    sq = rws[0:1, c0 + 32:c0 + 40]
    negb = rws[0:1, c0 + 40:c0 + 48]
    ones_bf = env["ones_bf"]
    s1 = p.sem("l3b_1")
    s2 = p.sem("l3b_2")
    s3 = p.sem("l3b_3")
    s4 = p.sem("l3b_4")
    s5 = p.sem("l3b_5")
    s6 = p.sem("l3b_6")
    s7 = p.sem("l3b_7")
    s8 = p.sem("l3b_8")
    b1, b2, b3, b4, b5, b6, b7, b8 = (s1.n, s2.n, s3.n, s4.n, s5.n, s6.n,
                                      s7.n, s8.n)
    with nc.Block() as blk:
        @blk.scalar
        def _(s):
            s1.inc(s.activation(SQA[:], QC[:], AF.Square), 1)
            s1.inc(s.activation(SQK[:], KC[:], AF.Square), 1)
            s.wait_ge(s5.h, b5 + 1)
            s6.inc(s.activation(sq, prod, AF.Sqrt), 1)
            for j in range(8):
                s.wait_ge(s8.h, b8 + j + 1)
                h2, w = j // 4, j % 4
                s.activation(BCOL[:, 4 * (2 * c + h2) + w:
                                  4 * (2 * c + h2) + w + 1],
                             bank(env, j % 2)[:, 0:1], AF.Copy)

        @blk.tensor
        def _(t):
            t.wait_ge(s1.h, b1 + 1)
            mm = None
            for h2, base in ((0, 0), (1, 64)):
                for nb in range(NNB):
                    mm = t.matmul(env["PSA"][h2][0:1, nb * NB:(nb + 1) * NB],
                                  ones_bf[base:base + 64, :],
                                  SQA[base:base + 64, nb * NB:(nb + 1) * NB],
                                  start=True, stop=True,
                                  tile_position=(base, 0))
            s2.inc(mm, 1)
            t.wait_ge(s1.h, b1 + 2)
            t.wait_ge(s3.h, b3 + 1)
            for h2, base in ((0, 0), (1, 64)):
                for nb in range(NNB):
                    mm = t.matmul(env["PSA"][h2][0:1, nb * NB:(nb + 1) * NB],
                                  ones_bf[base:base + 64, :],
                                  SQK[base:base + 64, nb * NB:(nb + 1) * NB],
                                  start=True, stop=True,
                                  tile_position=(base, 0))
            s4.inc(mm, 1)
            t.wait_ge(s7.h, b7 + 1)
            for j in range(8):
                if j >= 2:
                    t.wait_ge(s8.h, b8 + j - 1)
                s8.inc(t.matmul(bank(env, j % 2)[:, 0:1], env["ones_row128"][:],
                                negb[0:1, j:j + 1], start=True, stop=True), 1)

        @blk.vector
        def _(v):
            v.wait_ge(s2.h, b2 + 1)
            last = None
            for h2 in (0, 1):
                for w in range(NNB):
                    last = v.tensor_reduce(mq[0:1, 4 * h2 + w:4 * h2 + w + 1],
                                           env["PSA"][h2][0:1, w * NB:(w + 1) * NB],
                                           axis=AX, op=ALU.max)
            p.link(v, s3, last)
            v.wait_ge(s4.h, b4 + 1)
            for h2 in (0, 1):
                for w in range(NNB):
                    last = v.tensor_reduce(mk[0:1, 4 * h2 + w:4 * h2 + w + 1],
                                           env["PSA"][h2][0:1, w * NB:(w + 1) * NB],
                                           axis=AX, op=ALU.max)
            p.chain(v, "vch", last)
            for h2 in (0, 1):
                v.tensor_copy(mkw[0:1, 4 * h2:4 * h2 + 1],
                              mk[0:1, 4 * h2:4 * h2 + 1])
                last = v.tensor_tensor(mkw[0:1, 4 * h2 + 1:4 * h2 + 4],
                                       mk[0:1, 4 * h2:4 * h2 + 3],
                                       mk[0:1, 4 * h2 + 1:4 * h2 + 4], ALU.max)
            p.chain(v, "vch", last)
            i5 = v.tensor_tensor(prod, mq, mkw, ALU.mult)
            p.link(v, s5, i5)
            v.wait_ge(s6.h, b6 + 1)
            s7.inc(v.tensor_scalar(negb, sq, -8.0, 0.0, ALU.mult, ALU.add), 1)



def _l3f_head(p, env, h, QTp, KTp, VT, PB, OTL, BCOL, RROW, AVF):
    nc = p.nc
    ch, base = h // 2, 64 * (h % 2)
    s_smm = p.sem("l3h_smm")
    s_msk = p.sem("l3h_msk")
    s_exp = p.sem("l3h_exp")
    s_av = p.sem("l3h_av")
    s_r = p.sem("l3h_r")
    s_rb = p.sem("l3h_rb")
    s_avc = p.sem("l3h_avc")
    s_don = p.sem("l3h_don")
    b_smm, b_msk, b_exp, b_av, b_r, b_rb, b_avc, b_don = (
        s_smm.n, s_msk.n, s_exp.n, s_av.n, s_r.n, s_rb.n, s_avc.n, s_don.n)
    units = []
    gidx = 0
    for w in range(NNB):
        groups = []
        for g0 in (0, 2, 4, 6):
            rs = [r for r in (g0, g0 + 1) if 4 * w - 4 + r >= 0]
            if rs:
                groups.append((gidx, rs))
                gidx += 1
        units.append((w, groups))

    def s_reg(gi, kn):
        half = gi % 2
        return env["PSA"][0][:, half * 2 * NB:(half * 2 + kn) * NB]

    with nc.Block() as blk:
        @blk.tensor
        def _(t):
            ui = 0
            for w, groups in units:
                for gi, rs in groups:
                    if gi >= 2:
                        t.wait_ge(s_exp.h, b_exp + gi - 1)
                    mm = None
                    for ki, r in enumerate(rs):
                        kt = 4 * w - 4 + r
                        mm = t.matmul(
                            bank(env, (gi % 2) * 2 + ki),
                            KTp[ch][base:base + 64, kt * 128:(kt + 1) * 128],
                            QTp[ch][base:base + 64, w * NB:(w + 1) * NB],
                            start=True, stop=True)
                    s_smm.inc(mm, 1)
                t.wait_ge(s_exp.h, b_exp + groups[-1][0] + 1)
                if ui >= 2:
                    t.wait_ge(s_don.h, b_don + ui - 1)
                mm = None
                kts = [4 * w - 4 + r for r in range(8) if 4 * w - 4 + r >= 0]
                for j, kt in enumerate(kts):
                    r = kt - (4 * w - 4)
                    mm = t.matmul(bank(env, 4 + ui % 2)[0:65, :],
                                  VT[kt][:, 65 * h:65 * h + 65],
                                  PB[:, r * NB:(r + 1) * NB],
                                  start=(j == 0), stop=(j == len(kts) - 1))
                s_av.inc(mm, 1)
                t.wait_ge(s_r.h, b_r + ui + 1)
                s_rb.inc(t.matmul(bank(env, 6)[0:64, :],
                                  env["ones_row128"][:, 0:64], RROW[:],
                                  start=True, stop=True), 1)
                ui += 1

        @blk.vector
        def _(v):
            ui = 0
            for w, groups in units:
                for gi, rs in groups:
                    v.wait_ge(s_smm.h, b_smm + gi + 1)
                    kn = len(rs)
                    reg = s_reg(gi, kn)
                    msrc = (env["scr_f32"] if rs[0] < 4 else env["scr_f32b"])[
                        :, (rs[0] % 4) * NB:((rs[0] % 4) + kn) * NB]
                    s_msk.inc(v.tensor_tensor(reg, reg, msrc, ALU.add), 1)
                v.wait_ge(s_av.h, b_av + ui + 1)
                i1 = v.reciprocal(RROW[:], bank(env, 4 + ui % 2)[64:65, :])
                p.link(v, s_r, i1)
                v.wait_ge(s_rb.h, b_rb + ui + 1)
                v.wait_ge(s_avc.h, b_avc + ui + 1)
                i2 = v.tensor_tensor(OTL[ch][base:base + 64, w * NB:(w + 1) * NB],
                                     AVF[:], bank(env, 6)[0:64, :], ALU.mult)
                p.link(v, s_don, i2)
                ui += 1

        @blk.scalar
        def _(s):
            ui = 0
            for w, groups in units:
                for gi, rs in groups:
                    s.wait_ge(s_msk.h, b_msk + gi + 1)
                    kn = len(rs)
                    reg = s_reg(gi, kn)
                    s_exp.inc(s.activation(PB[:, rs[0] * NB:(rs[0] + kn) * NB],
                                           reg, AF.Exp, scale=8.0,
                                           bias=BCOL[:, 4 * h + w:4 * h + w + 1]), 1)
                s.wait_ge(s_av.h, b_av + ui + 1)
                if ui >= 1:
                    s.wait_ge(s_don.h, b_don + ui)
                s_avc.inc(s.activation(AVF[:], bank(env, 4 + ui % 2)[0:64, :],
                                       AF.Copy), 1)
                ui += 1


def _l3f_mlp(p, env, l, w1, w2, b1a, b1g, b2r):
    nc = p.nc
    HP = 2816
    NT = HP // 128
    with contextlib.ExitStack() as scope:
        sb = lambda nm, sh, dt: scope.enter_context(
            nc.sbuf_tensor(f"l3m{l}" + nm, list(sh), dt))
        HIDn = sb("_hid", [128, NT * NB], BF16)
        STG = sb("_stg", [128, 2 * 2 * 8 * 256], BF16)
        STG2 = sb("_stg2", [128, 11 * 1024], BF16)
        B2R = env["rows"][0:1, N:N + 1024]
        s_b = p.sem("l3m_b")
        b_b = s_b.n
        with nc.Block() as blk:
            @blk.sync
            def _(sy):
                s_b.inc(sy.dma_start(env["wcol"][:, 0:22], b1a[l, :, :]), 16)
                s_b.inc(sy.dma_start(env["wcol"][:, 22:44], b1g[l, :, :]), 16)
                s_b.inc(sy.dma_start(B2R[:], b2r[l:l + 1, :]), 16)
                sy.wait_ge(s_b.h, b_b + 48)
        for nb in range(NNB):
            _l3f_mlp_w1(p, env, l, nb, w1, HIDn, STG)
            for sect in range(2):
                _l3f_mlp_w2(p, env, l, nb, sect, w2, HIDn, STG2,
                            B2R if sect == 0 else None)


def _l3f_mlp_w1(p, env, l, nb, w1, HIDn, STG):
    nc = p.nc
    NT = 22
    s_w = p.sem("l3w1_w")
    s_mm = p.sem("l3w1_mm")
    s_si = p.sem("l3w1_si")
    s_hv = p.sem("l3w1_hv")
    b_w, b_mm, b_si, b_hv = s_w.n, s_mm.n, s_si.n, s_hv.n
    TN = env["TN"]
    GS = 2 * 8 * 256
    with nc.Block() as blk:
        @blk.gpsimd
        def _(g):
            for grp in range(11):
                if grp >= 2:
                    g.wait_ge(s_mm.h, b_mm + 4 * (grp - 1))
                par = (grp % 2) * GS
                for j in range(2):
                    for tp in range(2):
                        ti = grp * 2 + tp
                        src = w1[l][:, j * 2816 + ti * 128:
                                    j * 2816 + (ti + 1) * 128].rearrange(
                            "(kc p) f -> p kc f", p=128)
                        dst = STG[:, par + (j * 2 + tp) * 1024:
                                  par + (j * 2 + tp + 1) * 1024
                                  ].rearrange("p (kc f) -> p kc f", f=128)
                        s_w.inc(g.dma_start(dst, src), 16)

        @blk.tensor
        def _(t):
            for grp in range(11):
                t.wait_ge(s_w.h, b_w + (grp + 1) * 64)
                par = (grp % 2) * GS
                for tp in range(2):
                    ti = grp * 2 + tp
                    u = ti
                    if u >= 2:
                        t.wait_ge(s_hv.h, b_hv + u - 1)
                    for j in range(2):
                        ps = bank(env, 2 * (u % 2) + j)
                        mm = None
                        for ic in range(8):
                            mm = t.matmul(
                                ps, STG[:, par + (j * 2 + tp) * 1024 + ic * 128:
                                        par + (j * 2 + tp) * 1024 + (ic + 1) * 128],
                                TN[ic][:, nb * NB:(nb + 1) * NB],
                                start=(ic == 0), stop=(ic == 7))
                        s_mm.inc(mm, 1)

        @blk.scalar
        def _(s):
            for u in range(NT):
                s.wait_ge(s_mm.h, b_mm + 2 * u + 2)
                if u >= 2:
                    s.wait_ge(s_hv.h, b_hv + u - 1)
                psg = bank(env, 2 * (u % 2) + 1)
                tmp = env["scr_f32"][:, 0:NB].bitcast(BF16)[
                    :, (u % 2) * NB:(u % 2) * NB + NB]
                s_si.inc(s.activation(tmp, psg, AF.Gelu,
                                      bias=env["wcol"][:, 22 + u:23 + u]), 1)

        @blk.vector
        def _(v):
            for u in range(NT):
                v.wait_ge(s_si.h, b_si + u + 1)
                psa = bank(env, 2 * (u % 2))
                tmp = env["scr_f32"][:, 0:NB].bitcast(BF16)[
                    :, (u % 2) * NB:(u % 2) * NB + NB]
                s_hv.inc(v.scalar_tensor_tensor(HIDn[:, u * NB:(u + 1) * NB],
                                                psa, env["wcol"][:, u:u + 1],
                                                tmp, ALU.add, ALU.mult), 1)


def _l3f_mlp_w2(p, env, l, nb, sect, w2, HIDn, STG2, B2R):
    nc = p.nc
    s_w = p.sem("l3w2_w")
    s_mm = p.sem("l3w2_mm")
    s_tm = p.sem("l3w2_tm")
    b_w, b_mm, b_tm = s_w.n, s_mm.n, s_tm.n
    kts = list(range(sect * 11, sect * 11 + 11))
    with nc.Block() as blk:
        @blk.gpsimd
        def _(g):
            src = w2[l][sect * 11 * 128:(sect + 1) * 11 * 128, :].rearrange(
                "(kc p) f -> p kc f", p=128)
            dst = STG2[:].rearrange("p (kc f) -> p kc f", f=1024)
            s_w.inc(g.dma_start(dst, src), 16)

        @blk.tensor
        def _(t):
            t.wait_ge(s_w.h, b_w + 16)
            for dt in range(NCH):
                ps = bank(env, 4 + dt % 2)
                if dt >= 2:
                    t.wait_ge(s_tm.h, b_tm + dt - 1)
                mm = None
                for j, kt in enumerate(kts):
                    mm = t.matmul(ps, STG2[:, j * 1024 + dt * 128:
                                           j * 1024 + (dt + 1) * 128],
                                  HIDn[:, kt * NB:(kt + 1) * NB],
                                  start=(j == 0),
                                  stop=(j == 10 and B2R is None))
                if B2R is not None:
                    mm = t.matmul(ps, B2R[:, dt * 128:(dt + 1) * 128],
                                  env["ones_row512"][:],
                                  start=False, stop=True)
                s_mm.inc(mm, 1)

        @blk.vector
        def _(v):
            for dt in range(NCH):
                v.wait_ge(s_mm.h, b_mm + dt + 1)
                ps = bank(env, 4 + dt % 2)
                sl = slice(nb * NB, (nb + 1) * NB)
                s_tm.inc(v.tensor_tensor(env["X"][dt][:, sl],
                                         env["X"][dt][:, sl], ps, ALU.add), 1)




def emit_block_l3_attn(p, env):
    nc = p.nc
    emit_norm(p, env, kind="ln")
    TN = env["TN"]
    w_in = p.dram_in("w_l3a_in", (1024, 4096))
    w_x = p.dram_in("w_l3a_x", (2048, 80))
    w_out = p.dram_in("w_l3a_o", (2048, 1024))
    conv_w = p.dram_in("w_l3a_conv", (16, 128, 4))
    conv_b = p.dram_in("b_l3a_conv", (128, 16))
    alog = p.dram_in("w_l3a_alog", (128, 256))
    sel_ap = p.dram_in("c_l3a_sel", (16 * 128, 128))
    gath_ap = p.dram_in("c_l3a_gath", (16 * 128, 128))
    rep_ap = p.dram_in("c_l3a_rep16", (16, 128))
    xc_sv = nc.dram_tensor("xc_save", [16, 128, N], BF16, kind="Internal").ap()
    z_sv = nc.dram_tensor("z_save", [16, 128, N], BF16, kind="Internal").ap()
    y_sv = nc.dram_tensor("y_save", [16, 128, N], BF16, kind="Internal").ap()

    with contextlib.ExitStack() as scope:
        sb = lambda nm, sh, dt: scope.enter_context(
            nc.sbuf_tensor("l3a" + nm, list(sh), dt))
        with contextlib.ExitStack() as sc1:
            s1 = lambda nm, sh, dt: sc1.enter_context(
                nc.sbuf_tensor("l3ab" + nm, list(sh), dt))
            XP = [s1(f"_xp{i}", [128, N + 3], BF16) for i in range(4)]
            CV = s1("_cv", [128, N], F32)
            XCC = s1("_xcc", [128, N], BF16)
            WSP = s1("_wsp", [128, 4096], BF16)
            WXS = s1("_wxs", [128, 16 * 80], BF16)
            _l3a_xz(p, env, w_in, w_x, conv_w, conv_b, TN, XP, CV, XCC,
                    WSP, WXS, xc_sv, z_sv)
        with contextlib.ExitStack() as sc2:
            s2 = lambda nm, sh, dt: sc2.enter_context(
                nc.sbuf_tensor("l3ae" + nm, list(sh), dt))
            DBREP = s2("_dbrep", [128, N], BF16)
            CREP = s2("_crep", [128, N], BF16)
            DREP = s2("_drep", [128, N], F32)
            A_SB = s2("_a", [128, 256], F32)
            SELS = s2("_sels", [128, 16 * 128], BF16)
            GATHS = s2("_gaths", [128, 16 * 128], BF16)
            REP16 = s2("_rep", [16, 128], BF16)
            _l3a_scanprep(p, env, alog, rep_ap, sel_ap, gath_ap, DBREP, CREP,
                          DREP, A_SB, SELS, GATHS, REP16)
            XCI = [s2(f"_xci{i}", [128, N], BF16) for i in range(2)]
            UEV2 = [env["TN"][0], env["TN"][1]]
            ZIN = s2("_zin", [128, N], BF16)
            SZ = s2("_sz", [128, N], BF16)
            YC = s2("_yc", [128, N], BF16)
            DA2 = [env["scr_f32"][:], env["scr_f32b"][:]]
            G2 = [s2(f"_g{i}", [128, N], F32) for i in range(2)]
            H = s2("_h", [128, N], F32)
            HCB2 = [env["TN"][2], env["TN"][3]]
            for dt in range(16):
                _l3a_scan_dt(p, env, dt, xc_sv, z_sv, y_sv, XCI, ZIN, SZ, YC,
                             DA2, G2, H, HCB2, UEV2, DBREP, CREP, DREP, A_SB,
                             SELS, GATHS)
        with contextlib.ExitStack() as sc3:
            s3 = lambda nm, sh, dt: sc3.enter_context(
                nc.sbuf_tensor("l3af" + nm, list(sh), dt))
            YIN = [s3(f"_yin{i}", [128, NB], BF16) for i in range(16)]
            WSO = s3("_wso", [128, 16384], BF16)
            _l3a_wout(p, env, w_out, y_sv, YIN, WSO)


def _l3a_xz(p, env, w_in, w_x, conv_w, conv_b, TN, XP, CV, XCC, WSP, WXS,
            xc_sv, z_sv):
    nc = p.nc
    s_wx = p.sem("l3x_wx")
    b_wx = s_wx.n
    with nc.Block() as blk:
        @blk.gpsimd
        def _(g):
            src = w_x.rearrange("(kc p) f -> p kc f", p=128)
            s_wx.inc(g.dma_start(
                WXS[:].rearrange("p (kc f) -> p kc f", f=80), src), 16)
            g.wait_ge(s_wx.h, b_wx + 16)

    for piece in range(8):
        is_xp = piece < 4
        s_w = p.sem("l3x_w")
        s_mm = p.sem("l3x_mm")
        s_ev = p.sem("l3x_ev")
        s_cv = p.sem("l3x_cv")
        s_si = p.sem("l3x_si")
        s_sp = p.sem("dma_gp")
        s_xmm = p.sem("l3x_xmm")
        b_w, b_mm, b_ev, b_cv, b_si, b_sp, b_xmm = (
            s_w.n, s_mm.n, s_ev.n, s_cv.n, s_si.n, s_sp.n, s_xmm.n)
        with nc.Block() as blk:
            @blk.gpsimd
            def _(g):
                src = w_in[:, piece * 512:(piece + 1) * 512].rearrange(
                    "(kc p) f -> p kc f", p=128)
                s_w.inc(g.dma_start(
                    WSP[:].rearrange("p (kc f) -> p kc f", f=512), src), 16)
                for ci in range(4):
                    ch = piece * 4 + ci
                    if is_xp:
                        g.wait_ge(s_si.h, b_si + ci + 1)
                        s_sp.inc(g.dma_start(xc_sv[ch, :, :], XCC[:]), 16)
                        g.wait_ge(s_sp.h, s_sp.n)
                    else:
                        g.wait_ge(s_ev.h, b_ev + (ci + 1) * 4)
                        s_sp.inc(g.dma_start(z_sv[ch - 16, :, :],
                                             XP[ci][:, 3:N + 3]), 16)
                if not is_xp:
                    g.wait_ge(s_sp.h, s_sp.n)

            @blk.sync
            def _(sy):
                if is_xp:
                    for ci in range(4):
                        s_w.inc(sy.dma_start(env["wcol"][:, 40 + ci:41 + ci],
                                             conv_b[:, piece * 4 + ci:
                                                    piece * 4 + ci + 1],
                                             allow_slow_non_contiguous=True), 16)
                        s_w.inc(sy.dma_start(
                            env["wcol"][:, 44 + 4 * ci:48 + 4 * ci],
                            conv_w[piece * 4 + ci, :, :]), 16)

            @blk.tensor
            def _(t):
                t.wait_ge(s_w.h, b_w + (16 * 9 if is_xp else 16))
                idx = 0
                for ft in range(4):
                    for nb in range(NNB):
                        ps = bank(env, idx % 4)
                        if idx >= 4:
                            t.wait_ge(s_ev.h, b_ev + idx - 3)
                        mm = None
                        for ic in range(8):
                            mm = t.matmul(ps, WSP[:, ic * 512 + ft * 128:
                                                  ic * 512 + (ft + 1) * 128],
                                          TN[ic][:, nb * NB:(nb + 1) * NB],
                                          start=(ic == 0), stop=(ic == 7))
                        s_mm.inc(mm, 1)
                        idx += 1
                if is_xp:
                    for ci in range(4):
                        ch = piece * 4 + ci
                        t.wait_ge(s_si.h, b_si + ci + 1)
                        mm = None
                        for nb in range(NNB):
                            mm = t.matmul(env["PSA"][1][0:80, nb * NB:(nb + 1) * NB],
                                          WXS[:, ch * 80:(ch + 1) * 80],
                                          XCC[:, nb * NB:(nb + 1) * NB],
                                          start=(ch == 0), stop=(ch == 15))
                        s_xmm.inc(mm, 1)

            @blk.scalar
            def _(s):
                idx = 0
                for ft in range(4):
                    for nb in range(NNB):
                        s.wait_ge(s_mm.h, b_mm + idx + 1)
                        ps = bank(env, idx % 4)
                        if is_xp:
                            dst = XP[ft][:, 3 + nb * NB:3 + (nb + 1) * NB]
                        else:
                            dst = XP[ft][:, 3 + nb * NB:3 + (nb + 1) * NB]
                        s_ev.inc(s.activation(dst, ps, AF.Copy), 1)
                        idx += 1
                if is_xp:
                    for ci in range(4):
                        s.wait_ge(s_cv.h, b_cv + ci + 1)
                        if ci >= 1:
                            s.wait_ge(s_xmm.h, b_xmm + ci)
                            s.wait_ge(s_sp.h, b_sp + 16 * ci)
                        s_si.inc(s.activation(XCC[:], CV[:], AF.Silu,
                                              bias=env["wcol"][:, 40 + ci:41 + ci]), 1)

            @blk.vector
            def _(v):
                if is_xp:
                    for ci in range(4):
                        v.wait_ge(s_ev.h, b_ev + (ci + 1) * 4)
                        if ci == 0:
                            v.memset(XP[0][:, 0:3], 0.0)
                            v.memset(XP[1][:, 0:3], 0.0)
                            v.memset(XP[2][:, 0:3], 0.0)
                            v.memset(XP[3][:, 0:3], 0.0)
                        if ci >= 1:
                            v.wait_ge(s_si.h, b_si + ci)
                        i1 = v.tensor_scalar(CV[:], XP[ci][:, 0:N],
                                             env["wcol"][:, 44 + 4 * ci:45 + 4 * ci],
                                             0.0, ALU.mult, ALU.add)
                        p.chain(v, "vch", i1)
                        for j in (1, 2, 3):
                            ij = v.scalar_tensor_tensor(
                                CV[:], XP[ci][:, j:j + N],
                                env["wcol"][:, 44 + 4 * ci + j:45 + 4 * ci + j],
                                CV[:], ALU.mult, ALU.add)
                            p.chain(v, "vch", ij)
                        s_cv.inc(v.nop(), 1)


def _l3a_scanprep(p, env, alog, rep_ap, sel_ap, gath_ap, DBREP, CREP, DREP,
                  A_SB, SELS, GATHS, REP16):
    nc = p.nc
    rws = env["rows"]
    drow = rws[0:1, 0:N]
    arow = rws[0:1, N:2 * N]
    erow = rws[0:1, N:2 * N]
    s_l = p.sem("l3p_l")
    s_d1 = p.sem("l3p_d1")
    s_d2 = p.sem("l3p_d2")
    s_d3 = p.sem("l3p_d3")
    s_bc = p.sem("l3p_bc")
    s_db = p.sem("l3p_db")
    s_rp = p.sem("l3p_rp")
    s_cp = p.sem("l3p_cp")
    s_ae = p.sem("l3p_ae")
    b_l, b_d1, b_d2, b_d3, b_bc, b_db, b_rp, b_cp, b_ae = (
        s_l.n, s_d1.n, s_d2.n, s_d3.n, s_bc.n, s_db.n, s_rp.n, s_cp.n, s_ae.n)
    BCF = env["scr_f32"][0:16, 0:N]
    DBF = env["scr_f32b"][0:16, 0:N].bitcast(BF16)[0:16, 0:N]
    CCF = env["scr_f32b"][0:16, N // 2:N].bitcast(BF16)[0:16, 0:N]
    REP = REP16
    ps_x = env["PSA"][1]
    s_dd = p.sem("l3p_dd")
    b_dd = s_dd.n

    with nc.Block() as blk:
        @blk.gpsimd
        def _(g):
            s_l.inc(g.dma_start(A_SB[:], alog[:, :]), 16)
            s_l.inc(g.dma_start(REP[:], rep_ap[:, :]), 16)
            s_l.inc(g.dma_start(
                SELS[:].rearrange("p (s q) -> p s q", s=16),
                sel_ap.rearrange("(s p) q -> p s q", p=128)), 16)
            s_l.inc(g.dma_start(
                GATHS[:].rearrange("p (s q) -> p s q", s=16),
                gath_ap.rearrange("(s p) q -> p s q", p=128)), 16)

        @blk.scalar
        def _(s):
            p.chain(s, "sch", s.activation(arow, ps_x[0:1, :], AF.Abs))
            p.chain(s, "sch", s.activation(erow, arow, AF.Exp, scale=-1.0))
            p.chain(s, "sch", s.activation(erow, erow, AF.Ln,
                                           bias=ccol(env, "one")[0:1, :]))
            i4 = s.activation(drow, ps_x[0:1, :], AF.Relu)
            p.link(s, s_d1, i4)
            s_d2.inc(s.activation(BCF[:], ps_x[32:48, :], AF.Copy), 1)
            s_d2.inc(s.activation(CCF[:], ps_x[64:80, :], AF.Copy), 1)
            s.wait_ge(s_l.h, b_l + 64)
            i5 = s.activation(A_SB[:], A_SB[:], AF.Exp)
            p.link(s, s_ae, i5)
            for nb in range(NNB):
                s.wait_ge(s_bc.h, b_bc + nb + 1)
                s_cp.inc(s.activation(DREP[:, nb * NB:(nb + 1) * NB],
                                      bank(env, nb % 2), AF.Copy), 1)
            for j in range(8):
                s.wait_ge(s_rp.h, b_rp + j + 1)
                dst = (DBREP if j < 4 else CREP)[:, (j % 4) * NB:(j % 4 + 1) * NB]
                s_cp.inc(s.activation(dst, bank(env, 2 + j % 2), AF.Copy), 1)

        @blk.vector
        def _(v):
            v.wait_ge(s_d1.h, b_d1 + 1)
            i1 = v.tensor_tensor(drow, drow, erow, ALU.add)
            p.link(v, s_dd, i1)
            v.wait_ge(s_d2.h, b_d2 + 1)
            i = None
            for nb in range(NNB):
                v.wait_ge(s_db.h, b_db + nb + 1)
                i = v.tensor_tensor(DBF[:, nb * NB:(nb + 1) * NB],
                                    BCF[:, nb * NB:(nb + 1) * NB],
                                    bank(env, 4 + nb % 2)[0:16, :], ALU.mult)
                s_d3.inc(i, 1)
            v.wait_ge(s_d3.h, s_d3.n)

        @blk.tensor
        def _(t):
            t.wait_ge(s_dd.h, b_dd + 1)
            for nb in range(NNB):
                if nb >= 2:
                    t.wait_ge(s_cp.h, b_cp + nb - 1)
                s_bc.inc(t.matmul(bank(env, nb % 2), env["ones_row128"][:],
                                  drow[:, nb * NB:(nb + 1) * NB],
                                  start=True, stop=True), 1)
            t.wait_ge(s_d2.h, b_d2 + 2)
            t.wait_ge(s_l.h, b_l + 64)
            for nb in range(NNB):
                if nb >= 2:
                    t.wait_ge(s_d3.h, b_d3 + nb - 1)
                s_db.inc(t.matmul(bank(env, 4 + nb % 2)[0:16, :],
                                  env["ones_row128"][:, 0:16],
                                  drow[:, nb * NB:(nb + 1) * NB],
                                  start=True, stop=True), 1)
            for j in range(8):
                srep = DBF if j < 4 else CCF
                nb = j % 4
                t.wait_ge((s_d3 if j < 4 else s_d2).h,
                          (b_d3 + nb + 1) if j < 4 else (b_d2 + 2))
                if j >= 2:
                    t.wait_ge(s_cp.h, b_cp + 4 + j - 1)
                s_rp.inc(t.matmul(bank(env, 2 + j % 2), REP[:],
                                  srep[:, nb * NB:(nb + 1) * NB],
                                  start=True, stop=True), 1)

    s_ng = p.sem("l3p_ng")
    with nc.Block() as blk:
        @blk.vector
        def _(v):
            v.tensor_scalar(A_SB[:], A_SB[:], -1.0, 0.0, ALU.mult, ALU.add)


def _l3a_scan_dt(p, env, dt, xc_sv, z_sv, y_sv, XCI, ZIN, SZ, YC, DA2, G2,
                 H, HCB2, UEV2, DBREP, CREP, DREP, A_SB, SELS, GATHS):
    nc = p.nc
    s_xc = p.sem("l3s_xc")
    s_z = p.sem("l3s_z")
    s_us = p.sem("l3s_us")
    s_ue = p.sem("l3s_ue")
    s_g = p.sem("l3s_g")
    s_da = p.sem("l3s_da")
    s_sc = p.sem("l3s_sc")
    s_hc = p.sem("l3s_hc")
    s_ym = p.sem("l3s_ym")
    s_si = p.sem("l3s_si")
    s_ye = p.sem("l3s_ye")
    s_yo = p.sem("dma_sync")
    b_xc, b_z, b_si, b_ye, b_yo = s_xc.n, s_z.n, s_si.n, s_ye.n, s_yo.n
    z_us = s_us.n - dt * 16
    z_ue = s_ue.n - dt * 16
    z_g = s_g.n - dt * 16
    z_da = s_da.n - dt * 16
    z_sc = s_sc.n - dt * 16
    z_hc = s_hc.n - dt * 16
    z_ym = s_ym.n - dt * 16
    buf = XCI[dt % 2]
    with nc.Block() as blk:
        @blk.gpsimd
        def _(g_):
            if dt == 0:
                s_xc.inc(g_.dma_start(XCI[0][:], xc_sv[0, :, :]), 16)
            if dt < 15:
                if dt >= 1:
                    g_.wait_ge(s_us.h, z_us + dt * 16)
                s_xc.inc(g_.dma_start(XCI[(dt + 1) % 2][:], xc_sv[dt + 1, :, :]), 16)
            if dt >= 1:
                g_.wait_ge(s_si.h, b_si)
            s_z.inc(g_.dma_start(ZIN[:], z_sv[dt, :, :]), 16)
            for j in range(16):
                u = dt * 16 + j
                g_.wait_ge(s_ue.h, z_ue + u + 1)
                if u >= 2:
                    g_.wait_ge(s_sc.h, z_sc + u - 1)
                s_g.inc(g_.tensor_tensor(G2[u % 2][:], UEV2[u % 2][:],
                                         DBREP[:], ALU.mult), 1)

        @blk.tensor
        def _(t):
            for j in range(16):
                u = dt * 16 + j
                if j == 0:
                    t.wait_ge(s_xc.h, b_xc + 16 if dt == 0 else b_xc)
                if u >= 1:
                    t.wait_ge(s_ue.h, z_ue + u)
                mm = None
                for nb in range(NNB):
                    mm = t.matmul(bank(env, nb), SELS[:, j * 128:(j + 1) * 128],
                                  buf[:, nb * NB:(nb + 1) * NB],
                                  start=True, stop=True)
                s_us.inc(mm, 1)
                t.wait_ge(s_hc.h, z_hc + u + 1)
                mm = None
                for nb in range(NNB):
                    mm = t.matmul(bank(env, 4 + nb)[0:128, :],
                                  GATHS[:, j * 128:(j + 1) * 128],
                                  HCB2[u % 2][:, nb * NB:(nb + 1) * NB],
                                  start=(j == 0), stop=(j == 15))
                s_ym.inc(mm, 1)

        @blk.scalar
        def _(s):
            for j in range(16):
                u = dt * 16 + j
                if u >= 2:
                    s.wait_ge(s_sc.h, z_sc + u - 1)
                s_da.inc(s.activation(DA2[u % 2], DREP[:], AF.Exp,
                                      scale=A_SB[:, u:u + 1]), 1)
                s.wait_ge(s_us.h, z_us + u + 1)
                if u >= 2:
                    s.wait_ge(s_g.h, z_g + u - 1)
                s_ue.inc(s.activation(UEV2[u % 2][:], env["PSA"][0][:, :],
                                      AF.Copy), 1)
            if dt >= 1:
                s.wait_ge(s_ye.h, b_ye)
            s.wait_ge(s_z.h, b_z + 16)
            s_si.inc(s.activation(SZ[:], ZIN[:], AF.Silu), 1)

        @blk.vector
        def _(v):
            for j in range(16):
                u = dt * 16 + j
                v.wait_ge(s_da.h, z_da + u + 1)
                v.wait_ge(s_g.h, z_g + u + 1)
                i2 = v.tensor_tensor_scan(H[:], DA2[u % 2], G2[u % 2][:], 0.0,
                                          ALU.mult, ALU.add)
                p.link(v, s_sc, i2)
                if u >= 2:
                    v.wait_ge(s_ym.h, z_ym + u - 1)
                i3 = v.tensor_tensor(HCB2[u % 2][:], H[:], CREP[:], ALU.mult)
                p.link(v, s_hc, i3)
            v.wait_ge(s_ym.h, z_ym + dt * 16 + 16)
            v.wait_ge(s_si.h, b_si + 1)
            v.wait_ge(s_yo.h, b_yo)
            last = None
            for nb in range(NNB):
                last = v.tensor_tensor(YC[:, nb * NB:(nb + 1) * NB],
                                       bank(env, 4 + nb),
                                       SZ[:, nb * NB:(nb + 1) * NB], ALU.mult)
            p.link(v, s_ye, last)

        @blk.sync
        def _(sy):
            sy.wait_ge(s_ye.h, b_ye + 1)
            s_yo.inc(sy.dma_start(y_sv[dt, :, :], YC[:]), 16)
            sy.wait_ge(s_yo.h, b_yo + 16)


def _l3a_wout(p, env, w_out, y_sv, YIN, WSO):
    nc = p.nc
    s_w = p.sem("l3o_w")
    s_y = p.sem("l3o_y")
    s_mm = p.sem("l3o_mm")
    s_ev = p.sem("l3o_ev")
    b_w, b_y, b_mm, b_ev = s_w.n, s_y.n, s_mm.n, s_ev.n
    with nc.Block() as blk:
        @blk.gpsimd
        def _(g):
            src = w_out.rearrange("(kc p) f -> p kc f", p=128)
            s_w.inc(g.dma_start(
                WSO[:].rearrange("p (kc f) -> p kc f", f=1024), src), 16)

        @blk.sync
        def _(sy):
            for nb in range(NNB):
                if nb >= 1:
                    sy.wait_ge(s_mm.h, b_mm + nb * NCH)
                for ic in range(16):
                    s_y.inc(sy.dma_start(YIN[ic][:],
                                         y_sv[ic, :, nb * NB:(nb + 1) * NB]), 16)

        @blk.tensor
        def _(t):
            t.wait_ge(s_w.h, b_w + 16)
            for nb in range(NNB):
                t.wait_ge(s_y.h, b_y + (nb + 1) * 256)
                for ft in range(NCH):
                    idx = nb * NCH + ft
                    ps = bank(env, idx % 4)
                    if idx >= 4:
                        t.wait_ge(s_ev.h, b_ev + idx - 3)
                    mm = None
                    for ic in range(16):
                        mm = t.matmul(ps, WSO[:, ic * 1024 + ft * 128:
                                              ic * 1024 + (ft + 1) * 128],
                                      YIN[ic][:],
                                      start=(ic == 0), stop=(ic == 15))
                    s_mm.inc(mm, 1)

        @blk.vector
        def _(v):
            for nb in range(NNB):
                for ft in range(NCH):
                    idx = nb * NCH + ft
                    v.wait_ge(s_mm.h, b_mm + idx + 1)
                    ps = bank(env, idx % 4)
                    sl = slice(nb * NB, (nb + 1) * NB)
                    s_ev.inc(v.tensor_tensor(env["X"][ft][:, sl],
                                             env["X"][ft][:, sl], ps,
                                             ALU.add), 1)




def build_program():
    p = Prog()
    env = make_env(p)
    emit_init(p, env)
    emit_block_l0_attn(p, env)
    emit_block_l0_ffn(p, env)
    emit_block_l1_attn(p, env)
    emit_block_l1_ffn(p, env)
    emit_block_l2_attn(p, env)
    emit_block_l2_ffn(p, env)
    emit_block_l3_attn(p, env)
    emit_block_l3_ffn(p, env)
    out = p.dram_out("x_out", (1024, 2048))
    emit_store_x(p, env, out)
    return p


def kernel(**inputs):
    shared = prep_shared(inputs)
    p = build_program()
    needed = set(p.inputs.keys())
    base = {k: v for k, v in shared.items() if k in needed}
    miss = needed - set(base) - {"x_t"}
    assert not miss, f"missing host arrays for {sorted(miss)}"
    in_maps = []
    for b in range(8):
        m = dict(base)
        m.update(prep_core(inputs, b))
        in_maps.append(m)
    res = run_bass_kernel_spmd(p.nc, in_maps, core_ids=list(range(8)))
    p.close()
    return np.ascontiguousarray(
        np.stack([r["x_out"].T for r in res.results])).astype(np.float32)
